# version 15
# baseline (speedup 1.0000x reference)
import numpy as np
import concourse.bass as bass
import concourse.mybir as mybir
from concourse.bass_utils import run_bass_kernel_spmd
from contextlib import ExitStack

F32 = mybir.dt.float32
BF16 = mybir.dt.bfloat16
I8 = mybir.dt.int8
AF = mybir.ActivationFunctionType
ALU = mybir.AluOpType
AX = mybir.AxisListType

L = 16384
D = 2048
T = 512
NCH = L // T
G = 4
DS = 512
NJ = 4
NDT = 16
NST = 16
NP = 8
EPS = 1e-5
TQ = 4096
FTC = TQ // T
NSP = 32
NIT = 64
INTER = 8192

BDT_, ALOG_, DPAR_, SG_, SU_, SD_, MASK_ = 0, 4, 68, 72, 136, 200, 216
EPS_, ONE_, NEG1_ = 220, 221, 222
CPW = 224

ENGS = ("sync", "gpsimd", "scalar", "vector", "tensor")
USE_ACCUM_LN = True
PROD_ON_GPSIMD = True
PIECED_GATHER = False


class Step:
    __slots__ = ("eng", "fn", "inc", "dma", "r", "w", "cc_wait", "cc_inc",
                 "barrier")

    def __init__(self, eng, fn, inc, dma, r, w, cc_wait, cc_inc, barrier):
        self.eng, self.fn, self.inc, self.dma = eng, fn, inc, dma
        self.r, self.w = r, w
        self.cc_wait, self.cc_inc, self.barrier = cc_wait, cc_inc, barrier


def build_program():
    nc = bass.Bass("TRN2", target_bir_lowering=False, debug=False)

    xb = nc.dram_tensor("xb", [L, D], F32, kind="ExternalInput").ap()
    xbs = nc.dram_tensor("xbs", [L, DS], F32, kind="ExternalInput").ap()
    xq = nc.dram_tensor("xq", [TQ, D], F32, kind="ExternalInput").ap()
    wxpT = nc.dram_tensor("wxpT", [D, 96], F32, kind="ExternalInput").ap()
    wdtT = nc.dram_tensor("wdtT", [64, DS], F32, kind="ExternalInput").ap()
    cpack = nc.dram_tensor("cpack", [128, CPW], F32, kind="ExternalInput").ap()
    idn = nc.dram_tensor("idn", [128, 128], F32, kind="ExternalInput").ap()
    wgp = nc.dram_tensor("wgp", [NSP, 128, NDT * 256], I8,
                         kind="ExternalInput").ap()
    wup = nc.dram_tensor("wup", [NSP, 128, NDT * 256], I8,
                         kind="ExternalInput").ap()
    wdp = nc.dram_tensor("wdp", [16, 128, INTER], I8, kind="ExternalInput").ap()
    outq = nc.dram_tensor("outq", [TQ, D], F32, kind="ExternalOutput").ap()

    bcd = nc.dram_tensor("bcd", [2, 32, T], F32, kind="Internal")
    s_d = nc.dram_tensor("s_d", [L, DS], F32, kind="Internal")
    s_g = nc.dram_tensor("s_g", [8, 4 * 2048, DS], F32, kind="Internal")
    fuse_d = nc.dram_tensor("fuse_d", [INTER, TQ], BF16, kind="Internal")

    steps = []

    def step(eng, r, w, n_dma=0, cc_wait=0, cc_inc=False):
        def deco(fn):
            inc = 0 if cc_inc else (16 * n_dma if n_dma else 1)
            steps.append(Step(eng, fn, inc, n_dma > 0, tuple(r), tuple(w),
                              cc_wait, cc_inc, False))
        return deco

    def barrier():
        for eng in ENGS:
            steps.append(Step(eng, None, 1, False, (), (), False, False, True))

    with ExitStack() as top:
        sbuf = lambda n, s, d=F32: top.enter_context(nc.sbuf_tensor(n, s, d))
        c_wxpT = sbuf("c_wxpT", [128, NDT * 96], BF16)
        c_wdtT = sbuf("c_wdtT", [64, DS], BF16)
        c_pack = sbuf("c_pack", [128, CPW])
        c_acol = sbuf("c_acol", [128, 64])
        c_idn = sbuf("c_idn", [128, 128])

        semE = {e: top.enter_context(nc.semaphore(f"sem_{e}")) for e in ENGS}
        semC = top.enter_context(nc.semaphore("semC"))

        with ExitStack() as pre:
            c_wxpT_f = pre.enter_context(
                nc.sbuf_tensor("c_wxpT_f", [128, NDT * 96], F32))
            c_wdtT_f = pre.enter_context(
                nc.sbuf_tensor("c_wdtT_f", [64, DS], F32))

            @step("sync", r=(), w=("c_wxpT_f", "c_wdtT_f", "c_pack", "c_idn"),
                  n_dma=4)
            def _(e):
                return [
                    e.dma_start(out=c_wxpT_f[:, :]
                                .rearrange("p (j e) -> p j e", j=NDT),
                                in_=wxpT.rearrange("(j p) e -> p j e", p=128)),
                    e.dma_start(out=c_wdtT_f[:, :], in_=wdtT),
                    e.dma_start(out=c_pack[:, :], in_=cpack),
                    e.dma_start(out=c_idn[:, :], in_=idn),
                ]

            @step("scalar", r=("c_wxpT_f", "c_wdtT_f", "c_pack"),
                  w=("c_wxpT", "c_wdtT", "c_acol"))
            def _(e):
                e.activation(out=c_wxpT[:, :], in_=c_wxpT_f[:, :], func=AF.Copy)
                e.activation(out=c_wdtT[:, :], in_=c_wdtT_f[:, :], func=AF.Copy)
                return e.activation(out=c_acol[:, :],
                                    in_=c_pack[:, ALOG_:ALOG_ + 64],
                                    func=AF.Exp)

            barrier()

        with ExitStack() as ms:
            msb = lambda n, s, d=F32: ms.enter_context(nc.sbuf_tensor(n, s, d))
            psT = ms.enter_context(nc.psum_tensor("psT", [128, 512], F32))
            psE = ms.enter_context(nc.psum_tensor("psE", [96, 512], F32))
            psBm = ms.enter_context(nc.psum_tensor("psBm", [128, 512], F32))
            psG = [ms.enter_context(nc.psum_tensor(f"psG{i}", [128, 512], F32))
                   for i in range(2)]
            psU = [ms.enter_context(nc.psum_tensor(f"psU{i}", [128, 512], F32))
                   for i in range(2)]
            xchH = [msb(f"xchH{i}", [128, 2 * D]) for i in range(2)]
            xbschH = [msb("xbschH", [128, 2 * DS])] * 2
            stH = [msb(f"stH{i}", [128, 16]) for i in range(2)]
            xnbf = msb("xnbf", [128, 2 * D], BF16)
            xnTbf = msb("xnTbf", [128, NDT * 256], BF16)
            xnsT = msb("xnsT", [128, G * DS])
            xnfT = msb("xnfT", [128, NJ * T])
            dtrawbf = msb("dtrawbf", [64, T], BF16)
            bcsb = msb("bcsb", [32, T])
            dt4 = msb("dt4", [128, NJ * T])
            dtx4 = msb("dtx4", [128, NJ * T])
            brep = [msb(f"brep{i}", [128, 2 * T]) for i in range(2)]
            crep = msb("crep", [128, 2 * T])
            aq = [msb(f"aq{i}", [128, 2 * T]) for i in range(2)]
            uq = msb("uq", [128, 2 * T])
            hq = [msb(f"hq{i}", [128, 2 * T]) for i in range(2)]
            hps = [msb(f"hps{i}", [128, NJ * T]) for i in range(2)]
            yb = msb("yb", [128, NJ * T])
            hst = msb("hst", [128, 64])
            zb = aq[0][:, 0:T]
            ebuf = aq[0][:, T:2 * T]
            xnT = msb("xnT", [128, NDT * T], BF16)
            scr = None if USE_ACCUM_LN else msb("scr", [128, D])
            wgb = [msb(f"wgb{i}", [128, NDT * 256], BF16) for i in range(2)]
            wub = [msb(f"wub{i}", [128, NDT * 256], BF16) for i in range(2)]
            gbx = msb("gbx", [128, 512])
            fusebf = msb("fusebf", [128, 512], BF16)

            xnsT3 = xnsT[:, :].rearrange("p (g d) -> p g d", g=G)
            xnT4m = xnTbf[:, :].rearrange("p (j t) -> p j t", j=NDT)
            uq3 = uq[:, :].rearrange("p (k t) -> p k t", k=2)
            fxnT4 = xnT[:, :].rearrange("p (j gg t) -> p j gg t", j=NDT, gg=G)

            @step("vector", r=("c_acol",), w=("c_acol", "hst"))
            def _(e):
                e.tensor_scalar_mul(c_acol[:, :], c_acol[:, :], -1.0)
                return e.memset(hst[:, :], 0.0)

            def emit_ln(XT, STT_, XNB, xch3, st_, xnb3, src_rows, extra=None):
                @step("sync", r=(), w=(XT,) + ((extra[0],) if extra else ()),
                      n_dma=2 if extra else 1)
                def _(e):
                    out = [e.dma_start(out=xch3,
                                       in_=src_rows
                                       .rearrange("(g p) d -> p g d", p=128))]
                    if extra:
                        out.append(e.dma_start(
                            out=extra[2],
                            in_=extra[1].rearrange("(g p) d -> p g d", p=128)))
                    return out

                for gg in range(2):
                    if USE_ACCUM_LN:
                        @step("scalar", r=(XT,), w=(XNB, STT_))
                        def _(e, gg=gg):
                            return e.activation(out=xnb3[:, gg, :],
                                                in_=xch3[:, gg, :],
                                                func=AF.Square,
                                                accum_out=st_[:, 2 + gg:3 + gg])

                        @step("vector", r=(XT,), w=(STT_,))
                        def _(e, gg=gg):
                            return e.reduce_sum(st_[:, gg:gg + 1],
                                                xch3[:, gg, :], axis=AX.X)
                    else:
                        @step("scalar", r=(XT,), w=("scr",))
                        def _(e, gg=gg):
                            return e.activation(out=scr[:, :],
                                                in_=xch3[:, gg, :],
                                                func=AF.Square)

                        @step("vector", r=(XT, "scr"), w=(STT_,))
                        def _(e, gg=gg):
                            e.reduce_sum(st_[:, gg:gg + 1], xch3[:, gg, :],
                                         axis=AX.X)
                            return e.reduce_sum(st_[:, 2 + gg:3 + gg],
                                                scr[:, :], axis=AX.X)

                @step("vector", r=(STT_,), w=(STT_,))
                def _(e):
                    e.tensor_scalar_mul(st_[:, 4:6], st_[:, 0:2], 1.0 / D)
                    return e.tensor_scalar_mul(st_[:, 6:8], st_[:, 2:4],
                                               1.0 / D)

                @step("vector", r=(STT_,), w=(STT_,))
                def _(e):
                    return e.tensor_tensor(out=st_[:, 8:10], in0=st_[:, 4:6],
                                           in1=st_[:, 4:6], op=ALU.mult)

                @step("vector", r=(STT_,), w=(STT_,))
                def _(e):
                    return e.tensor_tensor(out=st_[:, 6:8], in0=st_[:, 6:8],
                                           in1=st_[:, 8:10], op=ALU.subtract)

                @step("scalar", r=(STT_, "c_pack"), w=(STT_,))
                def _(e):
                    return e.activation(out=st_[:, 8:10], in_=st_[:, 6:8],
                                        func=AF.Sqrt,
                                        bias=c_pack[:, EPS_:EPS_ + 1])

                @step("vector", r=(STT_,), w=(STT_,))
                def _(e):
                    return e.reciprocal(st_[:, 10:12], st_[:, 8:10])

                @step("vector", r=(STT_,), w=(STT_,))
                def _(e):
                    return e.tensor_tensor(out=st_[:, 12:14], in0=st_[:, 4:6],
                                           in1=st_[:, 10:12], op=ALU.mult)

                @step("vector", r=(STT_,), w=(STT_,))
                def _(e):
                    return e.tensor_scalar_mul(st_[:, 12:14], st_[:, 12:14],
                                               -1.0)

            def emit_fa_ln(tc):
                for hh in range(2):
                    lo = tc * T + hh * 256
                    XT, STT_ = f"xchH{hh}", f"stH{hh}"
                    fxch3 = xchH[hh][:, :].rearrange("p (g d) -> p g d", g=2)
                    fxnbf3 = xnbf[:, :].rearrange("p (g d) -> p g d", g=2)
                    emit_ln(XT, STT_, "xnbf", fxch3, stH[hh],
                            fxnbf3, xq[lo:lo + 256, :])

                    @step("scalar", r=(XT, STT_), w=("xnbf",))
                    def _(e, hh=hh, fxch3=fxch3, fxnbf3=fxnbf3):
                        r = None
                        for gg in range(2):
                            r = e.activation(out=fxnbf3[:, gg, :],
                                             in_=fxch3[:, gg, :],
                                             func=AF.Identity,
                                             scale=stH[hh][:, 10 + gg:11 + gg],
                                             bias=stH[hh][:, 12 + gg:13 + gg])
                        return r

                    @step("sync", r=("xnbf",), w=("xnT",), n_dma=2)
                    def _(e, hh=hh, fxnbf3=fxnbf3):
                        return [
                            e.dma_start(out=fxnT4[:, :, 2 * hh + gg, :],
                                        in_=fxnbf3[:, gg, :]
                                        .rearrange("t (j p) -> t j p", j=NDT),
                                        transpose=True)
                            for gg in range(2)
                        ]

            def emit_fa_span(tc, sp):
                pw = sp % 2
                WGB, WUB = f"wgb{pw}", f"wub{pw}"
                wgb3 = wgb[pw][:, :].rearrange("p (j i) -> p j i", j=NDT)
                wub3 = wub[pw][:, :].rearrange("p (j i) -> p j i", j=NDT)

                @step("gpsimd", r=(), w=(WGB, WUB), n_dma=2)
                def _(e, sp=sp, pw=pw):
                    return [
                        e.dma_start(out=wgb[pw][:, :], in_=wgp[sp, :, :]),
                        e.dma_start(out=wub[pw][:, :], in_=wup[sp, :, :]),
                    ]

                for it in range(2):
                    pg = it % 2
                    PG, PU = f"psG{pg}", f"psU{pg}"

                    @step("tensor", r=(WGB, WUB, "xnT"), w=(PG, PU))
                    def _(e, it=it, pg=pg, wgb3=wgb3, wub3=wub3):
                        r = None
                        for j in range(NDT):
                            e.matmul(psG[pg][:, :],
                                     wgb3[:, j, it * 128:(it + 1) * 128],
                                     xnT[:, j * T:(j + 1) * T],
                                     start=(j == 0), stop=(j == NDT - 1))
                        for j in range(NDT):
                            r = e.matmul(
                                psU[pg][:, :],
                                wub3[:, j, it * 128:(it + 1) * 128],
                                xnT[:, j * T:(j + 1) * T],
                                start=(j == 0), stop=(j == NDT - 1))
                        return r

                    @step("scalar", r=(PG, "c_pack"), w=("gbx",))
                    def _(e, sp=sp, it=it, pg=pg):
                        col = SG_ + sp * 2 + it
                        return e.activation(out=gbx[:, :],
                                            in_=psG[pg][:, :],
                                            func=AF.Silu,
                                            scale=c_pack[:, col:col + 1])

                    @step("vector", r=(PU, "gbx", "c_pack"), w=("fusebf",))
                    def _(e, sp=sp, it=it, pg=pg):
                        col = SU_ + sp * 2 + it
                        return e.scalar_tensor_tensor(
                            out=fusebf[:, :], in0=psU[pg][:, :],
                            scalar=c_pack[:, col:col + 1],
                            in1=gbx[:, :], op0=ALU.mult, op1=ALU.mult)

                    @step("gpsimd", r=("fusebf",), w=("fuse_d",), n_dma=1)
                    def _(e, sp=sp, it=it, tc=tc):
                        row = (sp * 2 + it) * 128
                        return e.dma_start(
                            out=fuse_d.ap()[row:row + 128,
                                            tc * T:(tc + 1) * T],
                            in_=fusebf[:, :])

            emit_fa_ln(0)
            for c in range(NCH):
                p = c % 2
                BCD = f"bcd{p}"
                tc_cur = c // 4

                for hh in range(2):
                    XT, XBT, STT_ = f"xchH{hh}", "xbschH", f"stH{hh}"
                    xch3 = xchH[hh][:, :].rearrange("p (g d) -> p g d", g=2)
                    xbs3 = xbschH[hh][:, :].rearrange("p (g d) -> p g d", g=2)
                    xnb3 = xnbf[:, :].rearrange("p (g d) -> p g d", g=2)
                    lo = c * T + hh * 256

                    emit_ln(XT, STT_, "xnbf", xch3, stH[hh], xnb3,
                            xb[lo:lo + 256, :],
                            extra=(XBT, xbs[lo:lo + 256, :], xbs3))

                    @step("scalar", r=(XT, XBT, STT_), w=("xnbf", "xnsT"))
                    def _(e, hh=hh, xch3=xch3, xbs3=xbs3, xnb3=xnb3):
                        r = None
                        for gg in range(2):
                            g = 2 * hh + gg
                            e.activation(out=xnb3[:, gg, :], in_=xch3[:, gg, :],
                                         func=AF.Identity,
                                         scale=stH[hh][:, 10 + gg:11 + gg],
                                         bias=stH[hh][:, 12 + gg:13 + gg])
                            r = e.activation(out=xnsT3[:, g, :],
                                             in_=xbs3[:, gg, :],
                                             func=AF.Identity,
                                             scale=stH[hh][:, 10 + gg:11 + gg],
                                             bias=stH[hh][:, 12 + gg:13 + gg])
                        return r

                    @step("sync", r=("xnbf",), w=("xnTbf",), n_dma=2)
                    def _(e, xnb3=xnb3):
                        return [
                            e.dma_start(out=xnT4m[:, :, gg * 128:(gg + 1) * 128],
                                        in_=xnb3[:, gg, :]
                                        .rearrange("t (j p) -> t j p", j=NDT),
                                        transpose=True)
                            for gg in range(2)
                        ]

                    @step("tensor", r=("xnTbf", "c_wxpT"), w=("psE",))
                    def _(e):
                        r = None
                        for j2 in range(NDT):
                            r = e.matmul(psE[:, 0:256],
                                         c_wxpT[:, j2 * 96:(j2 + 1) * 96],
                                         xnT4m[:, j2, :],
                                         start=(j2 == 0), stop=(j2 == NDT - 1))
                        return r

                    @step("scalar", r=("psE",), w=("dtrawbf",))
                    def _(e, hh=hh):
                        return e.activation(
                            out=dtrawbf[:, hh * 256:(hh + 1) * 256],
                            in_=psE[0:64, 0:256], func=AF.Copy)

                    @step("vector", r=("psE",), w=("bcsb",))
                    def _(e, hh=hh):
                        return e.tensor_copy(
                            out=bcsb[:, hh * 256:(hh + 1) * 256],
                            in_=psE[64:96, 0:256])

                @step("gpsimd", r=("bcsb",), w=(BCD,), n_dma=1)
                def _(e, p=p):
                    return e.dma_start(out=bcd.ap()[p], in_=bcsb[:, :])

                for j in range(NJ):
                    @step("tensor", r=("xnsT", "c_idn"), w=("psT",))
                    def _(e, j=j):
                        r = None
                        for g in range(G):
                            r = e.transpose(psT[:, g * 128:(g + 1) * 128],
                                            xnsT3[:, g, j * 128:(j + 1) * 128],
                                            c_idn[:, :])
                        return r

                    @step("vector", r=("psT",), w=("xnfT",))
                    def _(e, j=j):
                        return e.tensor_copy(out=xnfT[:, j * T:(j + 1) * T],
                                             in_=psT[:, :])

                for j in range(NJ):
                    @step("tensor", r=("dtrawbf", "c_wdtT"), w=("psBm",))
                    def _(e, j=j):
                        return e.matmul(psBm[:, :],
                                        c_wdtT[:, j * 128:(j + 1) * 128],
                                        dtrawbf[:, :], start=True, stop=True)

                    @step("scalar", r=("psBm", "c_pack"), w=("aq0",))
                    def _(e, j=j):
                        return e.activation(
                            out=zb, in_=psBm[:, :], func=AF.Identity,
                            bias=c_pack[:, BDT_ + j:BDT_ + j + 1])

                    @step("scalar", r=("aq0", "c_pack"), w=("aq0",))
                    def _(e):
                        e.activation(out=ebuf, in_=zb, func=AF.Abs)
                        e.activation(out=ebuf, in_=ebuf, func=AF.Exp,
                                     scale=c_pack[:, NEG1_:NEG1_ + 1])
                        return e.activation(out=ebuf, in_=ebuf, func=AF.Ln,
                                            bias=c_pack[:, ONE_:ONE_ + 1])

                    @step("vector", r=("aq0", "xnfT"), w=("dt4", "dtx4"))
                    def _(e, j=j):
                        e.scalar_tensor_tensor(
                            out=dt4[:, j * T:(j + 1) * T], in0=zb,
                            scalar=0.0, in1=ebuf, op0=ALU.max, op1=ALU.add)
                        return e.tensor_tensor(
                            out=dtx4[:, j * T:(j + 1) * T],
                            in0=dt4[:, j * T:(j + 1) * T],
                            in1=xnfT[:, j * T:(j + 1) * T], op=ALU.mult)

                @step("vector", r=(), w=("yb",))
                def _(e):
                    return e.memset(yb[:, :], 0.0)

                for np_ in range(NP):
                    pq = np_ % 2
                    BR, CR = f"brep{pq}", "crep"

                    if np_ % 2 == 1:
                        sp0 = (c % 4) * 8 + (np_ // 2) * 2
                        emit_fa_span(tc_cur, sp0)
                        emit_fa_span(tc_cur, sp0 + 1)

                    @step("gpsimd", r=(BCD,), w=(BR, CR), n_dma=2)
                    def _(e, np_=np_, p=p, pq=pq):
                        return [
                            e.dma_start(
                                out=brep[pq][:, :]
                                .rearrange("p (k t) -> p k t", k=2),
                                in_=bcd.ap()[p, np_ * 2:(np_ + 1) * 2, :]
                                .unsqueeze(0).broadcast_to([128, 2, T])),
                            e.dma_start(
                                out=crep[:, :]
                                .rearrange("p (k t) -> p k t", k=2),
                                in_=bcd.ap()[p, 16 + np_ * 2:18 + np_ * 2, :]
                                .unsqueeze(0).broadcast_to([128, 2, T])),
                        ]

                    for j in range(NJ):
                        pa = (np_ * NJ + j) % 2
                        AQ, HQ = f"aq{pa}", f"hq{pa}"
                        HPS = f"hps{np_ % 2}"
                        aq3 = aq[pa][:, :].rearrange("p (k t) -> p k t", k=2)
                        hq3 = hq[pa][:, :].rearrange("p (k t) -> p k t", k=2)

                        @step("scalar", r=("dt4", "c_acol"), w=(AQ,))
                        def _(e, np_=np_, j=j, aq3=aq3):
                            r = None
                            for k in range(2):
                                idx = j * NST + np_ * 2 + k
                                r = e.activation(
                                    out=aq3[:, k, :],
                                    in_=dt4[:, j * T:(j + 1) * T],
                                    func=AF.Exp,
                                    scale=c_acol[:, idx:idx + 1])
                            return r

                        @step("vector", r=("dtx4", BR, AQ, "hst"),
                              w=("uq", HQ))
                        def _(e, np_=np_, j=j, pq=pq, aq3=aq3, hq3=hq3):
                            e.tensor_tensor(
                                out=uq3,
                                in0=dtx4[:, j * T:(j + 1) * T].unsqueeze(1)
                                .broadcast_to([128, 2, T]),
                                in1=brep[pq][:, :]
                                .rearrange("p (k t) -> p k t", k=2),
                                op=ALU.mult)
                            r = None
                            for k in range(2):
                                idx = j * NST + np_ * 2 + k
                                r = e.tensor_tensor_scan(
                                    out=hq3[:, k, :], data0=aq3[:, k, :],
                                    data1=uq3[:, k, :],
                                    initial=hst[:, idx:idx + 1],
                                    op0=ALU.mult, op1=ALU.add)
                            return r

                        if np_ > 0:
                            @step("vector", r=(f"hps{1 - np_ % 2}", "yb"),
                                  w=("yb",))
                            def _(e, np_=np_, j=j):
                                return e.tensor_tensor(
                                    out=yb[:, j * T:(j + 1) * T],
                                    in0=yb[:, j * T:(j + 1) * T],
                                    in1=hps[1 - np_ % 2]
                                    [:, j * T:(j + 1) * T],
                                    op=ALU.add)

                        @step("gpsimd", r=(HQ, CR), w=(HQ, HPS, "hst"))
                        def _(e, np_=np_, j=j, pa=pa, hq3=hq3):
                            base = j * NST + np_ * 2
                            e.tensor_copy(out=hst[:, base:base + 2],
                                          in_=hq3[:, :, T - 1:T])
                            e.tensor_tensor(out=hq[pa][:, :],
                                            in0=hq[pa][:, :],
                                            in1=crep[:, :],
                                            op=ALU.mult)
                            return e.tensor_tensor(
                                out=hps[np_ % 2][:, j * T:(j + 1) * T],
                                in0=hq[pa][:, 0:T], in1=hq[pa][:, T:2 * T],
                                op=ALU.add)

                @step("vector", r=("hps1", "yb"), w=("yb",))
                def _(e):
                    r = None
                    for j in range(NJ):
                        r = e.tensor_tensor(
                            out=yb[:, j * T:(j + 1) * T],
                            in0=yb[:, j * T:(j + 1) * T],
                            in1=hps[1][:, j * T:(j + 1) * T], op=ALU.add)
                    return r

                @step("vector", r=("xnfT", "c_pack", "yb"), w=("yb",))
                def _(e):
                    r = None
                    for j in range(NJ):
                        r = e.scalar_tensor_tensor(
                            out=yb[:, j * T:(j + 1) * T],
                            in0=xnfT[:, j * T:(j + 1) * T],
                            scalar=c_pack[:, DPAR_ + j:DPAR_ + j + 1],
                            in1=yb[:, j * T:(j + 1) * T],
                            op0=ALU.mult, op1=ALU.add)
                    return r

                for g in range(G):
                    @step("tensor", r=("yb", "c_idn"), w=("psT",))
                    def _(e, g=g):
                        r = None
                        for jj in range(NJ):
                            r = e.transpose(
                                psT[:, jj * 128:(jj + 1) * 128],
                                yb[:, jj * T + g * 128:jj * T + (g + 1) * 128],
                                c_idn[:, :])
                        return r

                    @step("vector", r=("psT",), w=("xnsT",))
                    def _(e, g=g):
                        return e.tensor_copy(
                            out=xnsT[:, g * DS:(g + 1) * DS], in_=psT[:, :])

                @step("gpsimd", r=("xnsT",), w=("s_d",), n_dma=1)
                def _(e, c=c):
                    return e.dma_start(
                        out=s_d.ap()[c * T:(c + 1) * T, :]
                        .rearrange("(g p) d -> p g d", p=128),
                        in_=xnsT[:, :].rearrange("p (g d) -> p g d", g=G))

                if c % 4 == 3:
                    if PIECED_GATHER:
                        @step("gpsimd", r=("s_d",), w=(), cc_inc=True)
                        def _(e, j=c // 4):
                            return e.collective_compute(
                                "AllGather", ALU.bypass,
                                replica_groups=[[0, 1, 2, 3], [4, 5, 6, 7]],
                                ins=[s_d.ap()[j * 2048:(j + 1) * 2048, :]
                                     .opt()],
                                outs=[s_g.ap()[j].opt()])

                    if c + 1 < NCH:
                        emit_fa_ln(c // 4 + 1)

            if not PIECED_GATHER:
                @step("gpsimd", r=("s_d",), w=(), cc_inc=True)
                def _(e):
                    return e.collective_compute(
                        "AllGather", ALU.bypass,
                        replica_groups=[[0, 1, 2, 3], [4, 5, 6, 7]],
                        ins=[s_d.ap().opt()],
                        outs=[s_g.ap().rearrange("w t d -> (w t) d").opt()])

            barrier()

        with ExitStack() as fb:
            fsb2 = lambda n, s, d=F32: fb.enter_context(
                nc.sbuf_tensor(n, s, d))
            fusesb = fsb2("fusesb", [128, NIT * T], BF16)
            wdb = [fsb2(f"wdb{i}", [128, NIT * 128], BF16) for i in range(2)]
            gbuf = fsb2("gbuf", [128, NDT * T])
            gTt = fsb2("gTt", [128, D])
            ssel = fsb2("ssel", [128, 16 * DS])
            sacc = fsb2("sacc", [128, D])
            xt = fsb2("xt", [128, D])
            psC = fb.enter_context(nc.psum_tensor("psC", [128, 512], F32))
            psT2 = fb.enter_context(nc.psum_tensor("psT2", [128, 512], F32))

            fusesb3 = fusesb[:, :].rearrange("p (it t) -> p it t", it=NIT)
            gbuf3 = gbuf[:, :].rearrange("p (j t) -> p j t", j=NDT)
            ssel4 = ssel[:, :].rearrange("p (r q d) -> p r q d", r=4, q=4)
            if PIECED_GATHER:
                sgv = s_g.ap().rearrange(
                    "(q tw) (r c gg p) d -> p r q tw c gg d",
                    q=4, tw=2, r=4, c=4, gg=G, p=128)
            else:
                sgv = s_g.ap().rearrange("w t d -> (w t) d").rearrange(
                    "(r q c gg p) d -> p r q c gg d",
                    r=4, q=4, c=FTC, gg=G, p=128)

            for tc in range(FTC):
                @step("sync", r=("fuse_d",), w=("fusesb",), n_dma=1)
                def _(e, tc=tc):
                    return e.dma_start(
                        out=fusesb3,
                        in_=fuse_d.ap()[:, tc * T:(tc + 1) * T]
                        .rearrange("(it p) t -> p it t", p=128))

                for dt in range(NDT):
                    pw = dt % 2
                    WDB = f"wdb{pw}"
                    wdb3 = wdb[pw][:, :].rearrange("p (it d) -> p it d",
                                                   it=NIT)

                    @step("gpsimd", r=(), w=(WDB,), n_dma=1)
                    def _(e, dt=dt, pw=pw):
                        return e.dma_start(out=wdb[pw][:, :],
                                           in_=wdp[dt, :, :])

                    @step("tensor", r=(WDB, "fusesb"), w=("psC",))
                    def _(e, dt=dt, wdb3=wdb3):
                        r = None
                        for it in range(NIT):
                            r = e.matmul(psC[:, :], wdb3[:, it, :],
                                         fusesb3[:, it, :],
                                         start=(it == 0),
                                         stop=(it == NIT - 1))
                        return r

                    @step("scalar", r=("psC", "c_pack"), w=("gbuf",))
                    def _(e, dt=dt):
                        col = SD_ + dt
                        return e.activation(out=gbuf3[:, dt, :],
                                            in_=psC[:, :], func=AF.Silu,
                                            scale=c_pack[:, col:col + 1])

                for g in range(G):
                    if PIECED_GATHER:
                        @step("sync", r=(), w=("ssel", "xt"), n_dma=5,
                              cc_wait=7 + tc // 4)
                        def _(e, tc=tc, g=g):
                            out = [
                                e.dma_start(out=ssel4[:, :, q, :],
                                            in_=sgv[:, :, q, tc // 4,
                                                    tc % 4, g, :])
                                for q in range(4)
                            ]
                            out.append(
                                e.dma_start(out=xt[:, :],
                                            in_=xq[tc * T + g * 128:
                                                   tc * T + (g + 1) * 128,
                                                   :]))
                            return out
                    else:
                        @step("sync", r=(), w=("ssel", "xt"), n_dma=2,
                              cc_wait=1)
                        def _(e, tc=tc, g=g):
                            return [
                                e.dma_start(out=ssel4,
                                            in_=sgv[:, :, :, tc, g, :]),
                                e.dma_start(out=xt[:, :],
                                            in_=xq[tc * T + g * 128:
                                                   tc * T + (g + 1) * 128,
                                                   :]),
                            ]

                    for rr in range(4):
                        @step("tensor", r=("gbuf", "c_idn"), w=("psT2",))
                        def _(e, g=g, rr=rr):
                            r = None
                            for u in range(4):
                                dt = rr * 4 + u
                                r = e.transpose(
                                    psT2[:, u * 128:(u + 1) * 128],
                                    gbuf[:, dt * T + g * 128:
                                         dt * T + (g + 1) * 128],
                                    c_idn[:, :])
                            return r

                        @step("vector", r=("psT2",), w=("gTt",))
                        def _(e, rr=rr):
                            return e.tensor_copy(
                                out=gTt[:, rr * 512:(rr + 1) * 512],
                                in_=psT2[:, :])

                    @step("vector", r=("ssel", "c_pack"), w=("sacc",))
                    def _(e):
                        r = None
                        for r4 in range(4):
                            e.tensor_single_scalar(
                                sacc[:, r4 * DS:(r4 + 1) * DS],
                                ssel4[:, r4, 0, :],
                                c_pack[:, MASK_:MASK_ + 1], ALU.mult)
                            for Q in range(1, 4):
                                r = e.scalar_tensor_tensor(
                                    out=sacc[:, r4 * DS:(r4 + 1) * DS],
                                    in0=ssel4[:, r4, Q, :],
                                    scalar=c_pack[:, MASK_ + Q:MASK_ + Q + 1],
                                    in1=sacc[:, r4 * DS:(r4 + 1) * DS],
                                    op0=ALU.mult, op1=ALU.add)
                        return r

                    @step("vector", r=("gTt", "xt", "sacc"), w=("sacc",))
                    def _(e):
                        e.tensor_tensor(out=sacc[:, :], in0=sacc[:, :],
                                        in1=gTt[:, :], op=ALU.mult)
                        return e.tensor_tensor(out=sacc[:, :], in0=sacc[:, :],
                                               in1=xt[:, :], op=ALU.add)

                    @step("gpsimd", r=("sacc",), w=(), n_dma=1)
                    def _(e, tc=tc, g=g):
                        return e.dma_start(
                            out=outq[tc * T + g * 128:tc * T + (g + 1) * 128,
                                     :],
                            in_=sacc[:, :])

            barrier()

        RELAXED = set() if False else {
            "aq0", "aq1", "hq0", "hq1", "uq", "yb", "hps0", "hps1", "hst",
            "dt4", "dtx4", "xnfT", "xnsT", "xnbf", "xnTbf", "xnT", "gbx",
            "fusebf", "fusesb", "gbuf", "gTt", "sacc", "ssel", "xt",
            "brep0", "brep1", "crep",
            "wgb0", "wgb1", "wub0", "wub1", "wdb0", "wdb1",
        }
        engcnt = {e: 0 for e in ENGS}
        last_w = {}
        readers = {}
        stepidx = {e: 0 for e in ENGS}
        plans = []

        for s in steps:
            waits = {}

            def add(eng2, c2):
                if waits.get(eng2, 0) < c2:
                    waits[eng2] = c2

            if s.barrier:
                for e2 in ENGS:
                    if e2 != s.eng and engcnt[e2] > 0:
                        add(e2, engcnt[e2])
            else:
                for t in s.r:
                    if t in last_w:
                        e2, c2, si2, wdma = last_w[t]
                        if e2 != s.eng:
                            add(e2, c2)
                        elif wdma or (stepidx[s.eng] - si2 <= 2
                                      and t not in RELAXED):
                            add(e2, c2)
                    if t.startswith("ps"):
                        for e2, c2 in readers.get(t, {}).items():
                            if e2 != s.eng:
                                add(e2, c2)
                for t in s.w:
                    if t in last_w:
                        e2, c2, si2, wdma = last_w[t]
                        if e2 != s.eng:
                            add(e2, c2)
                        elif wdma:
                            add(e2, c2)
                    for e2, c2 in readers.get(t, {}).items():
                        if e2 != s.eng:
                            add(e2, c2)
                        elif s.dma:
                            add(e2, c2)

            plans.append(waits)
            engcnt[s.eng] += s.inc
            si = stepidx[s.eng]
            stepidx[s.eng] += 1
            if not s.barrier:
                for t in s.w:
                    last_w[t] = (s.eng, engcnt[s.eng], si, s.dma)
                    readers[t] = {}
                for t in s.r:
                    readers.setdefault(t, {})[s.eng] = engcnt[s.eng]
                    if t.startswith("ps") and t not in s.w:
                        pass

        totals = dict(engcnt)

        with nc.Block() as block:
            def make_body(engname):
                def body(eng):
                    watermark = {e2: 0 for e2 in ENGS}
                    for s, waits in zip(steps, plans):
                        if s.eng != engname:
                            continue
                        for e2 in ENGS:
                            c2 = waits.get(e2, 0)
                            if c2 > watermark[e2]:
                                eng.wait_ge(semE[e2], c2)
                                watermark[e2] = c2
                        if s.cc_wait:
                            eng.wait_ge(semC, s.cc_wait)
                        if s.barrier:
                            eng.nop().then_inc(semE[engname], 1)
                            continue
                        ret = s.fn(eng)
                        if s.cc_inc:
                            ret.then_inc(semC, 1)
                        elif s.dma:
                            instrs = ret if isinstance(ret, list) else [ret]
                            assert 16 * len(instrs) == s.inc
                            for ins_ in instrs:
                                ins_.then_inc(semE[engname], 16)
                        else:
                            ret.then_inc(semE[engname], 1)
                    for e2 in ENGS:
                        if totals[e2] > watermark[e2]:
                            eng.wait_ge(semE[e2], totals[e2])
                return body

            block.sync(make_body("sync"))
            block.gpsimd(make_body("gpsimd"))
            block.scalar(make_body("scalar"))
            block.vector(make_body("vector"))
            block.tensor(make_body("tensor"))

    return nc


def kernel(L0_attn_x, L0_attn_W_xproj, L0_attn_W_dt, L0_attn_b_dt,
           L0_attn_A_log, L0_attn_D_param, L0_ffn_w_gate, L0_ffn_w_up,
           L0_ffn_w_down, L0_ffn_s_gate, L0_ffn_s_up, L0_ffn_s_down):
    x = np.asarray(L0_attn_x, dtype=np.float32)
    W_xproj = np.asarray(L0_attn_W_xproj, dtype=np.float32)
    W_dt = np.asarray(L0_attn_W_dt, dtype=np.float32)
    b_dt = np.asarray(L0_attn_b_dt, dtype=np.float32)
    A_log = np.asarray(L0_attn_A_log, dtype=np.float32)
    D_param = np.asarray(L0_attn_D_param, dtype=np.float32)
    w_gate = np.asarray(L0_ffn_w_gate)
    w_up = np.asarray(L0_ffn_w_up)
    w_down = np.asarray(L0_ffn_w_down)
    s_gate = np.asarray(L0_ffn_s_gate, dtype=np.float32)
    s_up = np.asarray(L0_ffn_s_up, dtype=np.float32)
    s_down = np.asarray(L0_ffn_s_down, dtype=np.float32)

    nc = build_program()

    wxpT = np.ascontiguousarray(W_xproj.T)
    wgp = np.ascontiguousarray(
        w_gate.reshape(16, 128, 32, 256).transpose(2, 1, 0, 3)
    ).reshape(NSP, 128, NDT * 256)
    wup = np.ascontiguousarray(
        w_up.reshape(16, 128, 32, 256).transpose(2, 1, 0, 3)
    ).reshape(NSP, 128, NDT * 256)
    wdp = np.ascontiguousarray(
        w_down.reshape(64, 128, 16, 128).transpose(2, 1, 0, 3)
    ).reshape(16, 128, INTER)
    idn = np.eye(128, dtype=np.float32)
    sg_c = np.ascontiguousarray(s_gate.reshape(64, 128).T)
    su_c = np.ascontiguousarray(s_up.reshape(64, 128).T)
    sd_c = np.ascontiguousarray(s_down.reshape(16, 128).T)

    in_maps = []
    for k in range(8):
        b, r = k // 4, k % 4
        dlo = r * DS
        cp = np.zeros((128, CPW), dtype=np.float32)
        cp[:, EPS_] = EPS
        cp[:, ONE_] = 1.0
        cp[:, NEG1_] = -1.0
        cp[:, BDT_:BDT_ + 4] = b_dt[dlo:dlo + DS].reshape(4, 128).T
        cp[:, ALOG_:ALOG_ + 64] = (
            A_log[dlo:dlo + DS].reshape(4, 128, 16).transpose(1, 0, 2)
            .reshape(128, 64))
        cp[:, DPAR_:DPAR_ + 4] = D_param[dlo:dlo + DS].reshape(4, 128).T
        cp[:, SG_:SG_ + 64] = sg_c
        cp[:, SU_:SU_ + 64] = su_c
        cp[:, SD_:SD_ + 16] = sd_c
        cp[:, MASK_ + r] = 1.0
        in_maps.append(dict(
            xb=np.ascontiguousarray(x[b]),
            xbs=np.ascontiguousarray(x[b][:, dlo:dlo + DS]),
            xq=np.ascontiguousarray(x[b, r * TQ:(r + 1) * TQ]),
            wxpT=wxpT,
            wdtT=np.ascontiguousarray(W_dt[dlo:dlo + DS].T),
            cpack=cp,
            idn=idn,
            wgp=wgp,
            wup=wup,
            wdp=wdp,
        ))

    res = run_bass_kernel_spmd(nc, in_maps, core_ids=list(range(8)))

    out = np.empty((2, L, D), dtype=np.float32)
    for k in range(8):
        b, r = k // 4, k % 4
        out[b, r * TQ:(r + 1) * TQ] = np.asarray(res.results[k]["outq"])
    return out


# version 16
# speedup vs baseline: 1.1221x; 1.1221x over previous
import numpy as np
import concourse.bass as bass
import concourse.mybir as mybir
from concourse.bass_utils import run_bass_kernel_spmd
from contextlib import ExitStack

F32 = mybir.dt.float32
BF16 = mybir.dt.bfloat16
I8 = mybir.dt.int8
AF = mybir.ActivationFunctionType
ALU = mybir.AluOpType
AX = mybir.AxisListType

L = 16384
D = 2048
T = 512
NCH = L // T
G = 4
DS = 512
NJ = 4
NDT = 16
NST = 16
NP = 8
EPS = 1e-5
TQ = 4096
FTC = TQ // T
NSP = 32
NIT = 64
INTER = 8192

BDT_, ALOG_, DPAR_, SG_, SU_, SD_, MASK_ = 0, 4, 68, 72, 136, 200, 216
EPS_, ONE_, NEG1_ = 220, 221, 222
CPW = 224

ENGS = ("sync", "gpsimd", "scalar", "vector", "tensor")
USE_ACCUM_LN = True
PROD_ON_GPSIMD = True
PIECED_GATHER = False


class Step:
    __slots__ = ("eng", "fn", "inc", "dma", "r", "w", "cc_wait", "cc_inc",
                 "barrier")

    def __init__(self, eng, fn, inc, dma, r, w, cc_wait, cc_inc, barrier):
        self.eng, self.fn, self.inc, self.dma = eng, fn, inc, dma
        self.r, self.w = r, w
        self.cc_wait, self.cc_inc, self.barrier = cc_wait, cc_inc, barrier


def build_program():
    nc = bass.Bass("TRN2", target_bir_lowering=False, debug=False)

    xb = nc.dram_tensor("xb", [L, D], F32, kind="ExternalInput").ap()
    xbs = nc.dram_tensor("xbs", [L, DS], F32, kind="ExternalInput").ap()
    xq = nc.dram_tensor("xq", [TQ, D], F32, kind="ExternalInput").ap()
    wxpT = nc.dram_tensor("wxpT", [D, 96], F32, kind="ExternalInput").ap()
    wdtT = nc.dram_tensor("wdtT", [64, DS], F32, kind="ExternalInput").ap()
    cpack = nc.dram_tensor("cpack", [128, CPW], F32, kind="ExternalInput").ap()
    idn = nc.dram_tensor("idn", [128, 128], F32, kind="ExternalInput").ap()
    wgp = nc.dram_tensor("wgp", [NSP, 128, NDT * 256], I8,
                         kind="ExternalInput").ap()
    wup = nc.dram_tensor("wup", [NSP, 128, NDT * 256], I8,
                         kind="ExternalInput").ap()
    wdp = nc.dram_tensor("wdp", [16, 128, INTER], I8, kind="ExternalInput").ap()
    outq = nc.dram_tensor("outq", [TQ, D], F32, kind="ExternalOutput").ap()

    bcd = nc.dram_tensor("bcd", [2, 32, T], F32, kind="Internal")
    s_d = nc.dram_tensor("s_d", [L, DS], F32, kind="Internal")
    s_g = nc.dram_tensor("s_g", [8, 4 * 2048, DS], F32, kind="Internal")
    fuse_d = nc.dram_tensor("fuse_d", [INTER, TQ], BF16, kind="Internal")

    steps = []

    def step(eng, r, w, n_dma=0, cc_wait=0, cc_inc=False):
        def deco(fn):
            inc = 0 if cc_inc else (16 * n_dma if n_dma else 1)
            steps.append(Step(eng, fn, inc, n_dma > 0, tuple(r), tuple(w),
                              cc_wait, cc_inc, False))
        return deco

    def barrier():
        for eng in ENGS:
            steps.append(Step(eng, None, 1, False, (), (), False, False, True))

    with ExitStack() as top:
        sbuf = lambda n, s, d=F32: top.enter_context(nc.sbuf_tensor(n, s, d))
        c_wxpT = sbuf("c_wxpT", [128, NDT * 96], BF16)
        c_wdtT = sbuf("c_wdtT", [64, DS], BF16)
        c_pack = sbuf("c_pack", [128, CPW])
        c_acol = sbuf("c_acol", [128, 64])
        c_idn = sbuf("c_idn", [128, 128])

        semE = {e: top.enter_context(nc.semaphore(f"sem_{e}")) for e in ENGS}
        semC = top.enter_context(nc.semaphore("semC"))

        with ExitStack() as pre:
            c_wxpT_f = pre.enter_context(
                nc.sbuf_tensor("c_wxpT_f", [128, NDT * 96], F32))
            c_wdtT_f = pre.enter_context(
                nc.sbuf_tensor("c_wdtT_f", [64, DS], F32))

            @step("sync", r=(), w=("c_wxpT_f", "c_wdtT_f", "c_pack", "c_idn"),
                  n_dma=4)
            def _(e):
                return [
                    e.dma_start(out=c_wxpT_f[:, :]
                                .rearrange("p (j e) -> p j e", j=NDT),
                                in_=wxpT.rearrange("(j p) e -> p j e", p=128)),
                    e.dma_start(out=c_wdtT_f[:, :], in_=wdtT),
                    e.dma_start(out=c_pack[:, :], in_=cpack),
                    e.dma_start(out=c_idn[:, :], in_=idn),
                ]

            @step("scalar", r=("c_wxpT_f", "c_wdtT_f", "c_pack"),
                  w=("c_wxpT", "c_wdtT", "c_acol"))
            def _(e):
                e.activation(out=c_wxpT[:, :], in_=c_wxpT_f[:, :], func=AF.Copy)
                e.activation(out=c_wdtT[:, :], in_=c_wdtT_f[:, :], func=AF.Copy)
                return e.activation(out=c_acol[:, :],
                                    in_=c_pack[:, ALOG_:ALOG_ + 64],
                                    func=AF.Exp)

            barrier()

        with ExitStack() as ms:
            msb = lambda n, s, d=F32: ms.enter_context(nc.sbuf_tensor(n, s, d))
            psT = ms.enter_context(nc.psum_tensor("psT", [128, 512], F32))
            psE = ms.enter_context(nc.psum_tensor("psE", [96, 512], F32))
            psBm = ms.enter_context(nc.psum_tensor("psBm", [128, 512], F32))
            psG = [ms.enter_context(nc.psum_tensor(f"psG{i}", [128, 512], F32))
                   for i in range(2)]
            psU = [ms.enter_context(nc.psum_tensor(f"psU{i}", [128, 512], F32))
                   for i in range(2)]
            xchH = [msb(f"xchH{i}", [128, 2 * D]) for i in range(2)]
            xbschH = [msb("xbschH", [128, 2 * DS])] * 2
            stH = [msb(f"stH{i}", [128, 16]) for i in range(2)]
            xnbf = msb("xnbf", [128, 2 * D], BF16)
            xnTbf = msb("xnTbf", [128, NDT * 256], BF16)
            xnsT = msb("xnsT", [128, G * DS])
            xnfT = msb("xnfT", [128, NJ * T])
            dtrawbf = msb("dtrawbf", [64, T], BF16)
            dt4 = msb("dt4", [128, NJ * T])
            dtx4 = msb("dtx4", [128, NJ * T])
            brep = [msb(f"brep{i}", [128, 2 * T]) for i in range(2)]
            crep = [msb(f"crep{i}", [128, 2 * T]) for i in range(2)]
            aq = [msb(f"aq{i}", [128, 2 * T]) for i in range(2)]
            uq = msb("uq", [128, T])
            hq = [msb(f"hq{i}", [128, 2 * T]) for i in range(3)]
            hps = [msb(f"hps{i}", [128, NJ * T]) for i in range(2)]
            yb = msb("yb", [128, NJ * T])
            hst = msb("hst", [128, 64])
            zb = aq[0][:, 0:T]
            ebuf = aq[0][:, T:2 * T]
            bcsb = msb("bcsb", [32, T])
            xnT = msb("xnT", [128, NDT * T], BF16)
            scr = None if USE_ACCUM_LN else msb("scr", [128, D])
            wgb = [msb(f"wgb{i}", [128, NDT * 256], BF16) for i in range(2)]
            wub = [msb(f"wub{i}", [128, NDT * 256], BF16) for i in range(2)]
            gbx = msb("gbx", [128, 512])
            fusebf = msb("fusebf", [128, 512], BF16)

            xnsT3 = xnsT[:, :].rearrange("p (g d) -> p g d", g=G)
            xnT4m = xnTbf[:, :].rearrange("p (j t) -> p j t", j=NDT)
            fxnT4 = xnT[:, :].rearrange("p (j gg t) -> p j gg t", j=NDT, gg=G)

            @step("vector", r=("c_acol",), w=("c_acol", "hst"))
            def _(e):
                e.tensor_scalar_mul(c_acol[:, :], c_acol[:, :], -1.0)
                return e.memset(hst[:, :], 0.0)

            def emit_ln(XT, STT_, XNB, xch3, st_, xnb3, src_rows, extra=None):
                @step("sync", r=(), w=(XT,) + ((extra[0],) if extra else ()),
                      n_dma=2 if extra else 1)
                def _(e):
                    out = [e.dma_start(out=xch3,
                                       in_=src_rows
                                       .rearrange("(g p) d -> p g d", p=128))]
                    if extra:
                        out.append(e.dma_start(
                            out=extra[2],
                            in_=extra[1].rearrange("(g p) d -> p g d", p=128)))
                    return out

                for gg in range(2):
                    if USE_ACCUM_LN:
                        @step("scalar", r=(XT,), w=(XNB, STT_))
                        def _(e, gg=gg):
                            return e.activation(out=xnb3[:, gg, :],
                                                in_=xch3[:, gg, :],
                                                func=AF.Square,
                                                accum_out=st_[:, 2 + gg:3 + gg])

                        @step("vector", r=(XT,), w=(STT_,))
                        def _(e, gg=gg):
                            return e.reduce_sum(st_[:, gg:gg + 1],
                                                xch3[:, gg, :], axis=AX.X)
                    else:
                        @step("scalar", r=(XT,), w=("scr",))
                        def _(e, gg=gg):
                            return e.activation(out=scr[:, :],
                                                in_=xch3[:, gg, :],
                                                func=AF.Square)

                        @step("vector", r=(XT, "scr"), w=(STT_,))
                        def _(e, gg=gg):
                            e.reduce_sum(st_[:, gg:gg + 1], xch3[:, gg, :],
                                         axis=AX.X)
                            return e.reduce_sum(st_[:, 2 + gg:3 + gg],
                                                scr[:, :], axis=AX.X)

                @step("vector", r=(STT_,), w=(STT_,))
                def _(e):
                    e.tensor_scalar_mul(st_[:, 4:6], st_[:, 0:2], 1.0 / D)
                    return e.tensor_scalar_mul(st_[:, 6:8], st_[:, 2:4],
                                               1.0 / D)

                @step("vector", r=(STT_,), w=(STT_,))
                def _(e):
                    return e.tensor_tensor(out=st_[:, 8:10], in0=st_[:, 4:6],
                                           in1=st_[:, 4:6], op=ALU.mult)

                @step("vector", r=(STT_,), w=(STT_,))
                def _(e):
                    return e.tensor_tensor(out=st_[:, 6:8], in0=st_[:, 6:8],
                                           in1=st_[:, 8:10], op=ALU.subtract)

                @step("scalar", r=(STT_, "c_pack"), w=(STT_,))
                def _(e):
                    return e.activation(out=st_[:, 8:10], in_=st_[:, 6:8],
                                        func=AF.Sqrt,
                                        bias=c_pack[:, EPS_:EPS_ + 1])

                @step("vector", r=(STT_,), w=(STT_,))
                def _(e):
                    return e.reciprocal(st_[:, 10:12], st_[:, 8:10])

                @step("vector", r=(STT_,), w=(STT_,))
                def _(e):
                    return e.tensor_tensor(out=st_[:, 12:14], in0=st_[:, 4:6],
                                           in1=st_[:, 10:12], op=ALU.mult)

                @step("vector", r=(STT_,), w=(STT_,))
                def _(e):
                    return e.tensor_scalar_mul(st_[:, 12:14], st_[:, 12:14],
                                               -1.0)

            def emit_fa_ln(tc):
                for hh in range(2):
                    lo = tc * T + hh * 256
                    XT, STT_ = f"xchH{hh}", f"stH{hh}"
                    fxch3 = xchH[hh][:, :].rearrange("p (g d) -> p g d", g=2)
                    fxnbf3 = xnbf[:, :].rearrange("p (g d) -> p g d", g=2)
                    emit_ln(XT, STT_, "xnbf", fxch3, stH[hh],
                            fxnbf3, xq[lo:lo + 256, :])

                    @step("scalar", r=(XT, STT_), w=("xnbf",))
                    def _(e, hh=hh, fxch3=fxch3, fxnbf3=fxnbf3):
                        r = None
                        for gg in range(2):
                            r = e.activation(out=fxnbf3[:, gg, :],
                                             in_=fxch3[:, gg, :],
                                             func=AF.Identity,
                                             scale=stH[hh][:, 10 + gg:11 + gg],
                                             bias=stH[hh][:, 12 + gg:13 + gg])
                        return r

                    @step("sync", r=("xnbf",), w=("xnT",), n_dma=2)
                    def _(e, hh=hh, fxnbf3=fxnbf3):
                        return [
                            e.dma_start(out=fxnT4[:, :, 2 * hh + gg, :],
                                        in_=fxnbf3[:, gg, :]
                                        .rearrange("t (j p) -> t j p", j=NDT),
                                        transpose=True)
                            for gg in range(2)
                        ]

            def emit_fa_span(tc, sp):
                pw = sp % 2
                WGB, WUB = f"wgb{pw}", f"wub{pw}"
                wgb3 = wgb[pw][:, :].rearrange("p (j i) -> p j i", j=NDT)
                wub3 = wub[pw][:, :].rearrange("p (j i) -> p j i", j=NDT)

                @step("gpsimd", r=(), w=(WGB, WUB), n_dma=2)
                def _(e, sp=sp, pw=pw):
                    return [
                        e.dma_start(out=wgb[pw][:, :], in_=wgp[sp, :, :]),
                        e.dma_start(out=wub[pw][:, :], in_=wup[sp, :, :]),
                    ]

                for it in range(2):
                    pg = it % 2
                    PG, PU = f"psG{pg}", f"psU{pg}"

                    @step("tensor", r=(WGB, WUB, "xnT"), w=(PG, PU))
                    def _(e, it=it, pg=pg, wgb3=wgb3, wub3=wub3):
                        r = None
                        for j in range(NDT):
                            e.matmul(psG[pg][:, :],
                                     wgb3[:, j, it * 128:(it + 1) * 128],
                                     xnT[:, j * T:(j + 1) * T],
                                     start=(j == 0), stop=(j == NDT - 1))
                        for j in range(NDT):
                            r = e.matmul(
                                psU[pg][:, :],
                                wub3[:, j, it * 128:(it + 1) * 128],
                                xnT[:, j * T:(j + 1) * T],
                                start=(j == 0), stop=(j == NDT - 1))
                        return r

                    @step("scalar", r=(PG, "c_pack"), w=("gbx",))
                    def _(e, sp=sp, it=it, pg=pg):
                        col = SG_ + sp * 2 + it
                        return e.activation(out=gbx[:, :],
                                            in_=psG[pg][:, :],
                                            func=AF.Silu,
                                            scale=c_pack[:, col:col + 1])

                    @step("vector", r=(PU, "gbx", "c_pack"), w=("fusebf",))
                    def _(e, sp=sp, it=it, pg=pg):
                        col = SU_ + sp * 2 + it
                        return e.scalar_tensor_tensor(
                            out=fusebf[:, :], in0=psU[pg][:, :],
                            scalar=c_pack[:, col:col + 1],
                            in1=gbx[:, :], op0=ALU.mult, op1=ALU.mult)

                    @step("sync", r=("fusebf",), w=("fuse_d",), n_dma=1)
                    def _(e, sp=sp, it=it, tc=tc):
                        row = (sp * 2 + it) * 128
                        return e.dma_start(
                            out=fuse_d.ap()[row:row + 128,
                                            tc * T:(tc + 1) * T],
                            in_=fusebf[:, :])

            emit_fa_ln(0)
            for c in range(NCH):
                p = c % 2
                BCD = f"bcd{p}"
                tc_cur = c // 4

                for hh in range(2):
                    XT, XBT, STT_ = f"xchH{hh}", "xbschH", f"stH{hh}"
                    xch3 = xchH[hh][:, :].rearrange("p (g d) -> p g d", g=2)
                    xbs3 = xbschH[hh][:, :].rearrange("p (g d) -> p g d", g=2)
                    xnb3 = xnbf[:, :].rearrange("p (g d) -> p g d", g=2)
                    lo = c * T + hh * 256

                    emit_ln(XT, STT_, "xnbf", xch3, stH[hh], xnb3,
                            xb[lo:lo + 256, :],
                            extra=(XBT, xbs[lo:lo + 256, :], xbs3))

                    @step("scalar", r=(XT, XBT, STT_), w=("xnbf", "xnsT"))
                    def _(e, hh=hh, xch3=xch3, xbs3=xbs3, xnb3=xnb3):
                        r = None
                        for gg in range(2):
                            g = 2 * hh + gg
                            e.activation(out=xnb3[:, gg, :], in_=xch3[:, gg, :],
                                         func=AF.Identity,
                                         scale=stH[hh][:, 10 + gg:11 + gg],
                                         bias=stH[hh][:, 12 + gg:13 + gg])
                            r = e.activation(out=xnsT3[:, g, :],
                                             in_=xbs3[:, gg, :],
                                             func=AF.Identity,
                                             scale=stH[hh][:, 10 + gg:11 + gg],
                                             bias=stH[hh][:, 12 + gg:13 + gg])
                        return r

                    @step("sync", r=("xnbf",), w=("xnTbf",), n_dma=2)
                    def _(e, xnb3=xnb3):
                        return [
                            e.dma_start(out=xnT4m[:, :, gg * 128:(gg + 1) * 128],
                                        in_=xnb3[:, gg, :]
                                        .rearrange("t (j p) -> t j p", j=NDT),
                                        transpose=True)
                            for gg in range(2)
                        ]

                    @step("tensor", r=("xnTbf", "c_wxpT"), w=("psE",))
                    def _(e):
                        r = None
                        for j2 in range(NDT):
                            r = e.matmul(psE[:, 0:256],
                                         c_wxpT[:, j2 * 96:(j2 + 1) * 96],
                                         xnT4m[:, j2, :],
                                         start=(j2 == 0), stop=(j2 == NDT - 1))
                        return r

                    @step("scalar", r=("psE",), w=("dtrawbf",))
                    def _(e, hh=hh):
                        return e.activation(
                            out=dtrawbf[:, hh * 256:(hh + 1) * 256],
                            in_=psE[0:64, 0:256], func=AF.Copy)

                    @step("vector", r=("psE",), w=("bcsb",))
                    def _(e, hh=hh):
                        return e.tensor_copy(
                            out=bcsb[:, hh * 256:(hh + 1) * 256],
                            in_=psE[64:96, 0:256])

                @step("sync", r=("bcsb",), w=(BCD,), n_dma=1)
                def _(e, p=p):
                    return e.dma_start(out=bcd.ap()[p], in_=bcsb[:, :])

                for j in range(NJ):
                    @step("tensor", r=("xnsT", "c_idn"), w=("psT",))
                    def _(e, j=j):
                        r = None
                        for g in range(G):
                            r = e.transpose(psT[:, g * 128:(g + 1) * 128],
                                            xnsT3[:, g, j * 128:(j + 1) * 128],
                                            c_idn[:, :])
                        return r

                    @step("vector", r=("psT",), w=("xnfT",))
                    def _(e, j=j):
                        return e.tensor_copy(out=xnfT[:, j * T:(j + 1) * T],
                                             in_=psT[:, :])

                for j in range(NJ):
                    @step("tensor", r=("dtrawbf", "c_wdtT"), w=("psBm",))
                    def _(e, j=j):
                        return e.matmul(psBm[:, :],
                                        c_wdtT[:, j * 128:(j + 1) * 128],
                                        dtrawbf[:, :], start=True, stop=True)

                    @step("scalar", r=("psBm", "c_pack"), w=("aq0",))
                    def _(e, j=j):
                        return e.activation(
                            out=zb, in_=psBm[:, :], func=AF.Identity,
                            bias=c_pack[:, BDT_ + j:BDT_ + j + 1])

                    @step("scalar", r=("aq0", "c_pack"), w=("aq0",))
                    def _(e):
                        e.activation(out=ebuf, in_=zb, func=AF.Abs)
                        e.activation(out=ebuf, in_=ebuf, func=AF.Exp,
                                     scale=c_pack[:, NEG1_:NEG1_ + 1])
                        return e.activation(out=ebuf, in_=ebuf, func=AF.Ln,
                                            bias=c_pack[:, ONE_:ONE_ + 1])

                    @step("vector", r=("aq0", "xnfT"), w=("dt4", "dtx4"))
                    def _(e, j=j):
                        e.scalar_tensor_tensor(
                            out=dt4[:, j * T:(j + 1) * T], in0=zb,
                            scalar=0.0, in1=ebuf, op0=ALU.max, op1=ALU.add)
                        return e.tensor_tensor(
                            out=dtx4[:, j * T:(j + 1) * T],
                            in0=dt4[:, j * T:(j + 1) * T],
                            in1=xnfT[:, j * T:(j + 1) * T], op=ALU.mult)

                @step("vector", r=(), w=("yb",))
                def _(e):
                    return e.memset(yb[:, :], 0.0)

                def emit_bc_load(np_, p):
                    pq = np_ % 2

                    @step("scalar", r=(BCD,), w=(f"brep{pq}", f"crep{pq}"),
                          n_dma=2)
                    def _(e, np_=np_, p=p, pq=pq):
                        return [
                            e.dma_start(
                                out=brep[pq][:, :]
                                .rearrange("p (k t) -> p k t", k=2),
                                in_=bcd.ap()[p, np_ * 2:(np_ + 1) * 2, :]
                                .unsqueeze(0).broadcast_to([128, 2, T])),
                            e.dma_start(
                                out=crep[pq][:, :]
                                .rearrange("p (k t) -> p k t", k=2),
                                in_=bcd.ap()[p, 16 + np_ * 2:18 + np_ * 2, :]
                                .unsqueeze(0).broadcast_to([128, 2, T])),
                        ]

                emit_bc_load(0, p)
                for np_ in range(NP):
                    pq = np_ % 2
                    BR, CR = f"brep{pq}", f"crep{pq}"

                    if np_ % 2 == 1:
                        sp0 = (c % 4) * 8 + (np_ // 2) * 2
                        emit_fa_span(tc_cur, sp0)
                        emit_fa_span(tc_cur, sp0 + 1)

                    for j in range(NJ):
                        pa = (np_ * NJ + j) % 2
                        ph = (np_ * NJ + j) % 3
                        AQ, HQ = f"aq{pa}", f"hq{ph}"
                        HPS = f"hps{np_ % 2}"
                        aq3 = aq[pa][:, :].rearrange("p (k t) -> p k t", k=2)
                        hq3 = hq[ph][:, :].rearrange("p (k t) -> p k t", k=2)

                        @step("scalar", r=("dt4", "c_acol"), w=(AQ,))
                        def _(e, np_=np_, j=j, aq3=aq3):
                            r = None
                            for k in range(2):
                                idx = j * NST + np_ * 2 + k
                                r = e.activation(
                                    out=aq3[:, k, :],
                                    in_=dt4[:, j * T:(j + 1) * T],
                                    func=AF.Exp,
                                    scale=c_acol[:, idx:idx + 1])
                            return r

                        if j == 0 and np_ + 1 < NP:
                            emit_bc_load(np_ + 1, p)

                        @step("vector", r=("dtx4", BR, AQ, "hst"),
                              w=("uq", HQ))
                        def _(e, np_=np_, j=j, pq=pq, aq3=aq3, hq3=hq3):
                            r = None
                            for k in range(2):
                                idx = j * NST + np_ * 2 + k
                                e.tensor_tensor(
                                    out=uq[:, :],
                                    in0=dtx4[:, j * T:(j + 1) * T],
                                    in1=brep[pq][:, k * T:(k + 1) * T],
                                    op=ALU.mult)
                                r = e.tensor_tensor_scan(
                                    out=hq3[:, k, :], data0=aq3[:, k, :],
                                    data1=uq[:, :],
                                    initial=hst[:, idx:idx + 1],
                                    op0=ALU.mult, op1=ALU.add)
                            return r

                        if np_ > 0:
                            @step("vector", r=(f"hps{1 - np_ % 2}", "yb"),
                                  w=("yb",))
                            def _(e, np_=np_, j=j):
                                return e.tensor_tensor(
                                    out=yb[:, j * T:(j + 1) * T],
                                    in0=yb[:, j * T:(j + 1) * T],
                                    in1=hps[1 - np_ % 2]
                                    [:, j * T:(j + 1) * T],
                                    op=ALU.add)

                        @step("gpsimd", r=(HQ, CR), w=(HQ, HPS, "hst"))
                        def _(e, np_=np_, j=j, ph=ph, pq=pq, hq3=hq3):
                            base = j * NST + np_ * 2
                            e.tensor_copy(out=hst[:, base:base + 2],
                                          in_=hq3[:, :, T - 1:T])
                            e.tensor_tensor(out=hq[ph][:, :],
                                            in0=hq[ph][:, :],
                                            in1=crep[pq][:, :],
                                            op=ALU.mult)
                            return e.tensor_tensor(
                                out=hps[np_ % 2][:, j * T:(j + 1) * T],
                                in0=hq[ph][:, 0:T], in1=hq[ph][:, T:2 * T],
                                op=ALU.add)

                @step("vector", r=("hps1", "yb"), w=("yb",))
                def _(e):
                    r = None
                    for j in range(NJ):
                        r = e.tensor_tensor(
                            out=yb[:, j * T:(j + 1) * T],
                            in0=yb[:, j * T:(j + 1) * T],
                            in1=hps[1][:, j * T:(j + 1) * T], op=ALU.add)
                    return r

                @step("vector", r=("xnfT", "c_pack", "yb"), w=("yb",))
                def _(e):
                    r = None
                    for j in range(NJ):
                        r = e.scalar_tensor_tensor(
                            out=yb[:, j * T:(j + 1) * T],
                            in0=xnfT[:, j * T:(j + 1) * T],
                            scalar=c_pack[:, DPAR_ + j:DPAR_ + j + 1],
                            in1=yb[:, j * T:(j + 1) * T],
                            op0=ALU.mult, op1=ALU.add)
                    return r

                for g in range(G):
                    @step("tensor", r=("yb", "c_idn"), w=("psT",))
                    def _(e, g=g):
                        r = None
                        for jj in range(NJ):
                            r = e.transpose(
                                psT[:, jj * 128:(jj + 1) * 128],
                                yb[:, jj * T + g * 128:jj * T + (g + 1) * 128],
                                c_idn[:, :])
                        return r

                    @step("vector", r=("psT",), w=("xnsT",))
                    def _(e, g=g):
                        return e.tensor_copy(
                            out=xnsT[:, g * DS:(g + 1) * DS], in_=psT[:, :])

                @step("gpsimd", r=("xnsT",), w=("s_d",), n_dma=1)
                def _(e, c=c):
                    return e.dma_start(
                        out=s_d.ap()[c * T:(c + 1) * T, :]
                        .rearrange("(g p) d -> p g d", p=128),
                        in_=xnsT[:, :].rearrange("p (g d) -> p g d", g=G))

                if c % 4 == 3:
                    if PIECED_GATHER:
                        @step("gpsimd", r=("s_d",), w=(), cc_inc=True)
                        def _(e, j=c // 4):
                            return e.collective_compute(
                                "AllGather", ALU.bypass,
                                replica_groups=[[0, 1, 2, 3], [4, 5, 6, 7]],
                                ins=[s_d.ap()[j * 2048:(j + 1) * 2048, :]
                                     .opt()],
                                outs=[s_g.ap()[j].opt()])

                    if c + 1 < NCH:
                        emit_fa_ln(c // 4 + 1)

            if not PIECED_GATHER:
                @step("gpsimd", r=("s_d",), w=(), cc_inc=True)
                def _(e):
                    return e.collective_compute(
                        "AllGather", ALU.bypass,
                        replica_groups=[[0, 1, 2, 3], [4, 5, 6, 7]],
                        ins=[s_d.ap().opt()],
                        outs=[s_g.ap().rearrange("w t d -> (w t) d").opt()])

            barrier()

        with ExitStack() as fb:
            fsb2 = lambda n, s, d=F32: fb.enter_context(
                nc.sbuf_tensor(n, s, d))
            fusesb = fsb2("fusesb", [128, NIT * T], BF16)
            wdb = [fsb2(f"wdb{i}", [128, NIT * 128], BF16) for i in range(2)]
            gbuf = fsb2("gbuf", [128, NDT * T])
            gTt = fsb2("gTt", [128, D])
            ssel = fsb2("ssel", [128, 16 * DS])
            sacc = fsb2("sacc", [128, D])
            xt = fsb2("xt", [128, D])
            psC = fb.enter_context(nc.psum_tensor("psC", [128, 512], F32))
            psT2 = fb.enter_context(nc.psum_tensor("psT2", [128, 512], F32))

            fusesb3 = fusesb[:, :].rearrange("p (it t) -> p it t", it=NIT)
            gbuf3 = gbuf[:, :].rearrange("p (j t) -> p j t", j=NDT)
            ssel4 = ssel[:, :].rearrange("p (r q d) -> p r q d", r=4, q=4)
            if PIECED_GATHER:
                sgv = s_g.ap().rearrange(
                    "(q tw) (r c gg p) d -> p r q tw c gg d",
                    q=4, tw=2, r=4, c=4, gg=G, p=128)
            else:
                sgv = s_g.ap().rearrange("w t d -> (w t) d").rearrange(
                    "(r q c gg p) d -> p r q c gg d",
                    r=4, q=4, c=FTC, gg=G, p=128)

            for tc in range(FTC):
                @step("sync", r=("fuse_d",), w=("fusesb",), n_dma=1)
                def _(e, tc=tc):
                    return e.dma_start(
                        out=fusesb3,
                        in_=fuse_d.ap()[:, tc * T:(tc + 1) * T]
                        .rearrange("(it p) t -> p it t", p=128))

                for dt in range(NDT):
                    pw = dt % 2
                    WDB = f"wdb{pw}"
                    wdb3 = wdb[pw][:, :].rearrange("p (it d) -> p it d",
                                                   it=NIT)

                    @step("gpsimd", r=(), w=(WDB,), n_dma=1)
                    def _(e, dt=dt, pw=pw):
                        return e.dma_start(out=wdb[pw][:, :],
                                           in_=wdp[dt, :, :])

                    @step("tensor", r=(WDB, "fusesb"), w=("psC",))
                    def _(e, dt=dt, wdb3=wdb3):
                        r = None
                        for it in range(NIT):
                            r = e.matmul(psC[:, :], wdb3[:, it, :],
                                         fusesb3[:, it, :],
                                         start=(it == 0),
                                         stop=(it == NIT - 1))
                        return r

                    @step("scalar", r=("psC", "c_pack"), w=("gbuf",))
                    def _(e, dt=dt):
                        col = SD_ + dt
                        return e.activation(out=gbuf3[:, dt, :],
                                            in_=psC[:, :], func=AF.Silu,
                                            scale=c_pack[:, col:col + 1])

                for g in range(G):
                    if PIECED_GATHER:
                        @step("sync", r=(), w=("ssel", "xt"), n_dma=5,
                              cc_wait=7 + tc // 4)
                        def _(e, tc=tc, g=g):
                            out = [
                                e.dma_start(out=ssel4[:, :, q, :],
                                            in_=sgv[:, :, q, tc // 4,
                                                    tc % 4, g, :])
                                for q in range(4)
                            ]
                            out.append(
                                e.dma_start(out=xt[:, :],
                                            in_=xq[tc * T + g * 128:
                                                   tc * T + (g + 1) * 128,
                                                   :]))
                            return out
                    else:
                        @step("sync", r=(), w=("ssel", "xt"), n_dma=2,
                              cc_wait=1)
                        def _(e, tc=tc, g=g):
                            return [
                                e.dma_start(out=ssel4,
                                            in_=sgv[:, :, :, tc, g, :]),
                                e.dma_start(out=xt[:, :],
                                            in_=xq[tc * T + g * 128:
                                                   tc * T + (g + 1) * 128,
                                                   :]),
                            ]

                    for rr in range(4):
                        @step("tensor", r=("gbuf", "c_idn"), w=("psT2",))
                        def _(e, g=g, rr=rr):
                            r = None
                            for u in range(4):
                                dt = rr * 4 + u
                                r = e.transpose(
                                    psT2[:, u * 128:(u + 1) * 128],
                                    gbuf[:, dt * T + g * 128:
                                         dt * T + (g + 1) * 128],
                                    c_idn[:, :])
                            return r

                        @step("vector", r=("psT2",), w=("gTt",))
                        def _(e, rr=rr):
                            return e.tensor_copy(
                                out=gTt[:, rr * 512:(rr + 1) * 512],
                                in_=psT2[:, :])

                    @step("vector", r=("ssel", "c_pack"), w=("sacc",))
                    def _(e):
                        r = None
                        for r4 in range(4):
                            e.tensor_single_scalar(
                                sacc[:, r4 * DS:(r4 + 1) * DS],
                                ssel4[:, r4, 0, :],
                                c_pack[:, MASK_:MASK_ + 1], ALU.mult)
                            for Q in range(1, 4):
                                r = e.scalar_tensor_tensor(
                                    out=sacc[:, r4 * DS:(r4 + 1) * DS],
                                    in0=ssel4[:, r4, Q, :],
                                    scalar=c_pack[:, MASK_ + Q:MASK_ + Q + 1],
                                    in1=sacc[:, r4 * DS:(r4 + 1) * DS],
                                    op0=ALU.mult, op1=ALU.add)
                        return r

                    @step("vector", r=("gTt", "xt", "sacc"), w=("sacc",))
                    def _(e):
                        e.tensor_tensor(out=sacc[:, :], in0=sacc[:, :],
                                        in1=gTt[:, :], op=ALU.mult)
                        return e.tensor_tensor(out=sacc[:, :], in0=sacc[:, :],
                                               in1=xt[:, :], op=ALU.add)

                    @step("gpsimd", r=("sacc",), w=(), n_dma=1)
                    def _(e, tc=tc, g=g):
                        return e.dma_start(
                            out=outq[tc * T + g * 128:tc * T + (g + 1) * 128,
                                     :],
                            in_=sacc[:, :])

            barrier()

        RELAXED = set() if False else {
            "aq0", "aq1", "hq0", "hq1", "hq2", "uq", "yb", "hps0", "hps1",
            "hst", "dt4", "dtx4", "xnfT", "xnsT", "xnbf", "xnTbf", "xnT",
            "gbx", "fusebf", "fusesb", "gbuf", "gTt", "sacc", "ssel", "xt",
            "brep0", "brep1", "crep0", "crep1",
            "wgb0", "wgb1", "wub0", "wub1", "wdb0", "wdb1",
        }
        engcnt = {e: 0 for e in ENGS}
        last_w = {}
        readers = {}
        stepidx = {e: 0 for e in ENGS}
        plans = []

        for s in steps:
            waits = {}

            def add(eng2, c2):
                if waits.get(eng2, 0) < c2:
                    waits[eng2] = c2

            if s.barrier:
                for e2 in ENGS:
                    if e2 != s.eng and engcnt[e2] > 0:
                        add(e2, engcnt[e2])
            else:
                for t in s.r:
                    if t in last_w:
                        e2, c2, si2, wdma = last_w[t]
                        if e2 != s.eng:
                            add(e2, c2)
                        elif wdma or (stepidx[s.eng] - si2 <= 2
                                      and t not in RELAXED):
                            add(e2, c2)
                    if t.startswith("ps"):
                        for e2, c2 in readers.get(t, {}).items():
                            if e2 != s.eng:
                                add(e2, c2)
                for t in s.w:
                    if t in last_w:
                        e2, c2, si2, wdma = last_w[t]
                        if e2 != s.eng:
                            add(e2, c2)
                        elif wdma:
                            add(e2, c2)
                    for e2, c2 in readers.get(t, {}).items():
                        if e2 != s.eng:
                            add(e2, c2)
                        elif s.dma:
                            add(e2, c2)

            plans.append(waits)
            engcnt[s.eng] += s.inc
            si = stepidx[s.eng]
            stepidx[s.eng] += 1
            if not s.barrier:
                for t in s.w:
                    last_w[t] = (s.eng, engcnt[s.eng], si, s.dma)
                    readers[t] = {}
                for t in s.r:
                    readers.setdefault(t, {})[s.eng] = engcnt[s.eng]
                    if t.startswith("ps") and t not in s.w:
                        pass

        totals = dict(engcnt)

        with nc.Block() as block:
            def make_body(engname):
                def body(eng):
                    watermark = {e2: 0 for e2 in ENGS}
                    for s, waits in zip(steps, plans):
                        if s.eng != engname:
                            continue
                        for e2 in ENGS:
                            c2 = waits.get(e2, 0)
                            if c2 > watermark[e2]:
                                eng.wait_ge(semE[e2], c2)
                                watermark[e2] = c2
                        if s.cc_wait:
                            eng.wait_ge(semC, s.cc_wait)
                        if s.barrier:
                            eng.nop().then_inc(semE[engname], 1)
                            continue
                        ret = s.fn(eng)
                        if s.cc_inc:
                            ret.then_inc(semC, 1)
                        elif s.dma:
                            instrs = ret if isinstance(ret, list) else [ret]
                            assert 16 * len(instrs) == s.inc
                            for ins_ in instrs:
                                ins_.then_inc(semE[engname], 16)
                        else:
                            ret.then_inc(semE[engname], 1)
                    for e2 in ENGS:
                        if totals[e2] > watermark[e2]:
                            eng.wait_ge(semE[e2], totals[e2])
                return body

            block.sync(make_body("sync"))
            block.gpsimd(make_body("gpsimd"))
            block.scalar(make_body("scalar"))
            block.vector(make_body("vector"))
            block.tensor(make_body("tensor"))

    return nc


def kernel(L0_attn_x, L0_attn_W_xproj, L0_attn_W_dt, L0_attn_b_dt,
           L0_attn_A_log, L0_attn_D_param, L0_ffn_w_gate, L0_ffn_w_up,
           L0_ffn_w_down, L0_ffn_s_gate, L0_ffn_s_up, L0_ffn_s_down):
    x = np.asarray(L0_attn_x, dtype=np.float32)
    W_xproj = np.asarray(L0_attn_W_xproj, dtype=np.float32)
    W_dt = np.asarray(L0_attn_W_dt, dtype=np.float32)
    b_dt = np.asarray(L0_attn_b_dt, dtype=np.float32)
    A_log = np.asarray(L0_attn_A_log, dtype=np.float32)
    D_param = np.asarray(L0_attn_D_param, dtype=np.float32)
    w_gate = np.asarray(L0_ffn_w_gate)
    w_up = np.asarray(L0_ffn_w_up)
    w_down = np.asarray(L0_ffn_w_down)
    s_gate = np.asarray(L0_ffn_s_gate, dtype=np.float32)
    s_up = np.asarray(L0_ffn_s_up, dtype=np.float32)
    s_down = np.asarray(L0_ffn_s_down, dtype=np.float32)

    nc = build_program()

    wxpT = np.ascontiguousarray(W_xproj.T)
    wgp = np.ascontiguousarray(
        w_gate.reshape(16, 128, 32, 256).transpose(2, 1, 0, 3)
    ).reshape(NSP, 128, NDT * 256)
    wup = np.ascontiguousarray(
        w_up.reshape(16, 128, 32, 256).transpose(2, 1, 0, 3)
    ).reshape(NSP, 128, NDT * 256)
    wdp = np.ascontiguousarray(
        w_down.reshape(64, 128, 16, 128).transpose(2, 1, 0, 3)
    ).reshape(16, 128, INTER)
    idn = np.eye(128, dtype=np.float32)
    sg_c = np.ascontiguousarray(s_gate.reshape(64, 128).T)
    su_c = np.ascontiguousarray(s_up.reshape(64, 128).T)
    sd_c = np.ascontiguousarray(s_down.reshape(16, 128).T)

    in_maps = []
    for k in range(8):
        b, r = k // 4, k % 4
        dlo = r * DS
        cp = np.zeros((128, CPW), dtype=np.float32)
        cp[:, EPS_] = EPS
        cp[:, ONE_] = 1.0
        cp[:, NEG1_] = -1.0
        cp[:, BDT_:BDT_ + 4] = b_dt[dlo:dlo + DS].reshape(4, 128).T
        cp[:, ALOG_:ALOG_ + 64] = (
            A_log[dlo:dlo + DS].reshape(4, 128, 16).transpose(1, 0, 2)
            .reshape(128, 64))
        cp[:, DPAR_:DPAR_ + 4] = D_param[dlo:dlo + DS].reshape(4, 128).T
        cp[:, SG_:SG_ + 64] = sg_c
        cp[:, SU_:SU_ + 64] = su_c
        cp[:, SD_:SD_ + 16] = sd_c
        cp[:, MASK_ + r] = 1.0
        in_maps.append(dict(
            xb=np.ascontiguousarray(x[b]),
            xbs=np.ascontiguousarray(x[b][:, dlo:dlo + DS]),
            xq=np.ascontiguousarray(x[b, r * TQ:(r + 1) * TQ]),
            wxpT=wxpT,
            wdtT=np.ascontiguousarray(W_dt[dlo:dlo + DS].T),
            cpack=cp,
            idn=idn,
            wgp=wgp,
            wup=wup,
            wdp=wdp,
        ))

    res = run_bass_kernel_spmd(nc, in_maps, core_ids=list(range(8)))

    out = np.empty((2, L, D), dtype=np.float32)
    for k in range(8):
        b, r = k // 4, k % 4
        out[b, r * TQ:(r + 1) * TQ] = np.asarray(res.results[k]["outq"])
    return out


# version 17
# speedup vs baseline: 1.1636x; 1.0370x over previous
import numpy as np
import concourse.bass as bass
import concourse.mybir as mybir
from concourse.bass_utils import run_bass_kernel_spmd
from contextlib import ExitStack

F32 = mybir.dt.float32
BF16 = mybir.dt.bfloat16
I8 = mybir.dt.int8
AF = mybir.ActivationFunctionType
ALU = mybir.AluOpType
AX = mybir.AxisListType

L = 16384
D = 2048
T = 512
NCH = L // T
G = 4
DS = 512
NJ = 4
NDT = 16
NST = 16
NP = 8
EPS = 1e-5
TQ = 4096
FTC = TQ // T
NSP = 32
NIT = 64
INTER = 8192

BDT_, ALOG_, DPAR_, SG_, SU_, SD_, MASK_ = 0, 4, 68, 72, 136, 200, 216
EPS_, ONE_, NEG1_ = 220, 221, 222
CPW = 224

ENGS = ("sync", "gpsimd", "scalar", "vector", "tensor")
USE_ACCUM_LN = True
PROD_ON_GPSIMD = True
PIECED_GATHER = False


class Step:
    __slots__ = ("eng", "fn", "inc", "dma", "r", "w", "cc_wait", "cc_inc",
                 "barrier")

    def __init__(self, eng, fn, inc, dma, r, w, cc_wait, cc_inc, barrier):
        self.eng, self.fn, self.inc, self.dma = eng, fn, inc, dma
        self.r, self.w = r, w
        self.cc_wait, self.cc_inc, self.barrier = cc_wait, cc_inc, barrier


def build_program():
    nc = bass.Bass("TRN2", target_bir_lowering=False, debug=False)

    xb = nc.dram_tensor("xb", [L, D], F32, kind="ExternalInput").ap()
    xbs = nc.dram_tensor("xbs", [L, DS], F32, kind="ExternalInput").ap()
    xq = nc.dram_tensor("xq", [TQ, D], F32, kind="ExternalInput").ap()
    wxpT = nc.dram_tensor("wxpT", [D, 96], F32, kind="ExternalInput").ap()
    wdtT = nc.dram_tensor("wdtT", [64, DS], F32, kind="ExternalInput").ap()
    cpack = nc.dram_tensor("cpack", [128, CPW], F32, kind="ExternalInput").ap()
    idn = nc.dram_tensor("idn", [128, 128], F32, kind="ExternalInput").ap()
    wgp = nc.dram_tensor("wgp", [NSP, 128, NDT * 256], I8,
                         kind="ExternalInput").ap()
    wup = nc.dram_tensor("wup", [NSP, 128, NDT * 256], I8,
                         kind="ExternalInput").ap()
    wdp = nc.dram_tensor("wdp", [16, 128, INTER], I8, kind="ExternalInput").ap()
    outq = nc.dram_tensor("outq", [TQ, D], F32, kind="ExternalOutput").ap()

    bcd = nc.dram_tensor("bcd", [2, 32, T], F32, kind="Internal")
    s_d = nc.dram_tensor("s_d", [L, DS], F32, kind="Internal")
    s_g = nc.dram_tensor("s_g", [8, 4 * 2048, DS], F32, kind="Internal")
    fuse_d = nc.dram_tensor("fuse_d", [INTER, TQ], BF16, kind="Internal")

    steps = []

    def step(eng, r, w, n_dma=0, cc_wait=0, cc_inc=False):
        def deco(fn):
            inc = 0 if cc_inc else (16 * n_dma if n_dma else 1)
            steps.append(Step(eng, fn, inc, n_dma > 0, tuple(r), tuple(w),
                              cc_wait, cc_inc, False))
        return deco

    def barrier():
        for eng in ENGS:
            steps.append(Step(eng, None, 1, False, (), (), False, False, True))

    with ExitStack() as top:
        sbuf = lambda n, s, d=F32: top.enter_context(nc.sbuf_tensor(n, s, d))
        c_wxpT = sbuf("c_wxpT", [128, NDT * 96], BF16)
        c_wdtT = sbuf("c_wdtT", [64, DS], BF16)
        c_pack = sbuf("c_pack", [128, CPW])
        c_acol = sbuf("c_acol", [128, 64])
        c_idn = sbuf("c_idn", [128, 128])

        semE = {e: top.enter_context(nc.semaphore(f"sem_{e}")) for e in ENGS}
        semC = top.enter_context(nc.semaphore("semC"))

        with ExitStack() as pre:
            c_wxpT_f = pre.enter_context(
                nc.sbuf_tensor("c_wxpT_f", [128, NDT * 96], F32))
            c_wdtT_f = pre.enter_context(
                nc.sbuf_tensor("c_wdtT_f", [64, DS], F32))

            @step("sync", r=(), w=("c_wxpT_f", "c_wdtT_f", "c_pack", "c_idn"),
                  n_dma=4)
            def _(e):
                return [
                    e.dma_start(out=c_wxpT_f[:, :]
                                .rearrange("p (j e) -> p j e", j=NDT),
                                in_=wxpT.rearrange("(j p) e -> p j e", p=128)),
                    e.dma_start(out=c_wdtT_f[:, :], in_=wdtT),
                    e.dma_start(out=c_pack[:, :], in_=cpack),
                    e.dma_start(out=c_idn[:, :], in_=idn),
                ]

            @step("scalar", r=("c_wxpT_f", "c_wdtT_f", "c_pack"),
                  w=("c_wxpT", "c_wdtT", "c_acol"))
            def _(e):
                e.activation(out=c_wxpT[:, :], in_=c_wxpT_f[:, :], func=AF.Copy)
                e.activation(out=c_wdtT[:, :], in_=c_wdtT_f[:, :], func=AF.Copy)
                return e.activation(out=c_acol[:, :],
                                    in_=c_pack[:, ALOG_:ALOG_ + 64],
                                    func=AF.Exp)

            barrier()

        with ExitStack() as ms:
            msb = lambda n, s, d=F32: ms.enter_context(nc.sbuf_tensor(n, s, d))
            psT = ms.enter_context(nc.psum_tensor("psT", [128, 512], F32))
            psE = ms.enter_context(nc.psum_tensor("psE", [96, 512], F32))
            psBm = ms.enter_context(nc.psum_tensor("psBm", [128, 512], F32))
            psG = [ms.enter_context(nc.psum_tensor(f"psG{i}", [128, 512], F32))
                   for i in range(2)]
            psU = [ms.enter_context(nc.psum_tensor(f"psU{i}", [128, 512], F32))
                   for i in range(2)]
            xchH = [msb(f"xchH{i}", [128, 2 * D]) for i in range(2)]
            xbschH = [msb("xbschH", [128, 2 * DS])] * 2
            stH = [msb(f"stH{i}", [128, 16]) for i in range(2)]
            xnbf = msb("xnbf", [128, 2 * D], BF16)
            xnTbf = msb("xnTbf", [128, NDT * 256], BF16)
            xnsT = msb("xnsT", [128, G * DS])
            xnfT = msb("xnfT", [128, NJ * T])
            dtrawbf = msb("dtrawbf", [64, T], BF16)
            dt4 = msb("dt4", [128, NJ * T])
            dtx4 = msb("dtx4", [128, NJ * T])
            brep = [msb(f"brep{i}", [128, 2 * T]) for i in range(2)]
            crep = [msb(f"crep{i}", [128, 2 * T]) for i in range(2)]
            aq = [msb(f"aq{i}", [128, 2 * T]) for i in range(2)]
            uq = msb("uq", [128, T])
            hq = [msb(f"hq{i}", [128, 2 * T]) for i in range(3)]
            hps = [msb(f"hps{i}", [128, NJ * T]) for i in range(2)]
            yb = msb("yb", [128, NJ * T])
            hst = msb("hst", [128, 64])
            zb = aq[0][:, 0:T]
            ebuf = aq[0][:, T:2 * T]
            bcsb = msb("bcsb", [32, T])
            xnT = msb("xnT", [128, NDT * T], BF16)
            scr = None if USE_ACCUM_LN else msb("scr", [128, D])
            wgb = [msb(f"wgb{i}", [128, NDT * 256], BF16) for i in range(2)]
            wub = [msb(f"wub{i}", [128, NDT * 256], BF16) for i in range(2)]
            gbx = msb("gbx", [128, 512])
            fusebf = [msb(f"fusebf{i}", [128, 512], BF16) for i in range(2)]

            xnsT3 = xnsT[:, :].rearrange("p (g d) -> p g d", g=G)
            xnT4m = xnTbf[:, :].rearrange("p (j t) -> p j t", j=NDT)
            fxnT4 = xnT[:, :].rearrange("p (j gg t) -> p j gg t", j=NDT, gg=G)

            @step("vector", r=("c_acol",),
                  w=("c_acol",) + tuple(f"hst_{n}_{j}" for n in range(NP)
                                        for j in range(NJ)))
            def _(e):
                e.tensor_scalar_mul(c_acol[:, :], c_acol[:, :], -1.0)
                return e.memset(hst[:, :], 0.0)

            def emit_ln(XT, STT_, XNB, xch3, st_, xnb3, src_rows, extra=None):
                @step("sync", r=(), w=(XT,) + ((extra[0],) if extra else ()),
                      n_dma=2 if extra else 1)
                def _(e):
                    out = [e.dma_start(out=xch3,
                                       in_=src_rows
                                       .rearrange("(g p) d -> p g d", p=128))]
                    if extra:
                        out.append(e.dma_start(
                            out=extra[2],
                            in_=extra[1].rearrange("(g p) d -> p g d", p=128)))
                    return out

                for gg in range(2):
                    if USE_ACCUM_LN:
                        @step("scalar", r=(XT,), w=(XNB, STT_))
                        def _(e, gg=gg):
                            return e.activation(out=xnb3[:, gg, :],
                                                in_=xch3[:, gg, :],
                                                func=AF.Square,
                                                accum_out=st_[:, 2 + gg:3 + gg])

                        @step("vector", r=(XT,), w=(STT_,))
                        def _(e, gg=gg):
                            return e.reduce_sum(st_[:, gg:gg + 1],
                                                xch3[:, gg, :], axis=AX.X)
                    else:
                        @step("scalar", r=(XT,), w=("scr",))
                        def _(e, gg=gg):
                            return e.activation(out=scr[:, :],
                                                in_=xch3[:, gg, :],
                                                func=AF.Square)

                        @step("vector", r=(XT, "scr"), w=(STT_,))
                        def _(e, gg=gg):
                            e.reduce_sum(st_[:, gg:gg + 1], xch3[:, gg, :],
                                         axis=AX.X)
                            return e.reduce_sum(st_[:, 2 + gg:3 + gg],
                                                scr[:, :], axis=AX.X)

                @step("vector", r=(STT_,), w=(STT_,))
                def _(e):
                    e.tensor_scalar_mul(st_[:, 4:6], st_[:, 0:2], 1.0 / D)
                    return e.tensor_scalar_mul(st_[:, 6:8], st_[:, 2:4],
                                               1.0 / D)

                @step("vector", r=(STT_,), w=(STT_,))
                def _(e):
                    return e.tensor_tensor(out=st_[:, 8:10], in0=st_[:, 4:6],
                                           in1=st_[:, 4:6], op=ALU.mult)

                @step("vector", r=(STT_,), w=(STT_,))
                def _(e):
                    return e.tensor_tensor(out=st_[:, 6:8], in0=st_[:, 6:8],
                                           in1=st_[:, 8:10], op=ALU.subtract)

                @step("scalar", r=(STT_, "c_pack"), w=(STT_,))
                def _(e):
                    return e.activation(out=st_[:, 8:10], in_=st_[:, 6:8],
                                        func=AF.Sqrt,
                                        bias=c_pack[:, EPS_:EPS_ + 1])

                @step("vector", r=(STT_,), w=(STT_,))
                def _(e):
                    return e.reciprocal(st_[:, 10:12], st_[:, 8:10])

                @step("vector", r=(STT_,), w=(STT_,))
                def _(e):
                    return e.tensor_tensor(out=st_[:, 12:14], in0=st_[:, 4:6],
                                           in1=st_[:, 10:12], op=ALU.mult)

                @step("vector", r=(STT_,), w=(STT_,))
                def _(e):
                    return e.tensor_scalar_mul(st_[:, 12:14], st_[:, 12:14],
                                               -1.0)

            def emit_fa_ln(tc):
                for hh in range(2):
                    lo = tc * T + hh * 256
                    XT, STT_ = f"xchH{hh}", f"stH{hh}"
                    fxch3 = xchH[hh][:, :].rearrange("p (g d) -> p g d", g=2)
                    fxnbf3 = xnbf[:, :].rearrange("p (g d) -> p g d", g=2)
                    emit_ln(XT, STT_, "xnbf", fxch3, stH[hh],
                            fxnbf3, xq[lo:lo + 256, :])

                    @step("scalar", r=(XT, STT_), w=("xnbf",))
                    def _(e, hh=hh, fxch3=fxch3, fxnbf3=fxnbf3):
                        r = None
                        for gg in range(2):
                            r = e.activation(out=fxnbf3[:, gg, :],
                                             in_=fxch3[:, gg, :],
                                             func=AF.Identity,
                                             scale=stH[hh][:, 10 + gg:11 + gg],
                                             bias=stH[hh][:, 12 + gg:13 + gg])
                        return r

                    @step("sync", r=("xnbf",), w=("xnT",), n_dma=2)
                    def _(e, hh=hh, fxnbf3=fxnbf3):
                        return [
                            e.dma_start(out=fxnT4[:, :, 2 * hh + gg, :],
                                        in_=fxnbf3[:, gg, :]
                                        .rearrange("t (j p) -> t j p", j=NDT),
                                        transpose=True)
                            for gg in range(2)
                        ]

            def emit_fa_span(tc, sp):
                pw = sp % 2
                WGB, WUB = f"wgb{pw}", f"wub{pw}"
                wgb3 = wgb[pw][:, :].rearrange("p (j i) -> p j i", j=NDT)
                wub3 = wub[pw][:, :].rearrange("p (j i) -> p j i", j=NDT)

                @step("gpsimd", r=(), w=(WGB, WUB), n_dma=2)
                def _(e, sp=sp, pw=pw):
                    return [
                        e.dma_start(out=wgb[pw][:, :], in_=wgp[sp, :, :]),
                        e.dma_start(out=wub[pw][:, :], in_=wup[sp, :, :]),
                    ]

                for it in range(2):
                    pg = it % 2
                    PG, PU = f"psG{pg}", f"psU{pg}"

                    @step("tensor", r=(WGB, WUB, "xnT"), w=(PG, PU))
                    def _(e, it=it, pg=pg, wgb3=wgb3, wub3=wub3):
                        r = None
                        for j in range(NDT):
                            e.matmul(psG[pg][:, :],
                                     wgb3[:, j, it * 128:(it + 1) * 128],
                                     xnT[:, j * T:(j + 1) * T],
                                     start=(j == 0), stop=(j == NDT - 1))
                        for j in range(NDT):
                            r = e.matmul(
                                psU[pg][:, :],
                                wub3[:, j, it * 128:(it + 1) * 128],
                                xnT[:, j * T:(j + 1) * T],
                                start=(j == 0), stop=(j == NDT - 1))
                        return r

                    @step("scalar", r=(PG, "c_pack"), w=("gbx",))
                    def _(e, sp=sp, it=it, pg=pg):
                        col = SG_ + sp * 2 + it
                        return e.activation(out=gbx[:, :],
                                            in_=psG[pg][:, :],
                                            func=AF.Silu,
                                            scale=c_pack[:, col:col + 1])

                    pf = it % 2
                    FB = f"fusebf{pf}"

                    @step("vector", r=(PU, "gbx", "c_pack"), w=(FB,))
                    def _(e, sp=sp, it=it, pg=pg, pf=pf):
                        col = SU_ + sp * 2 + it
                        return e.scalar_tensor_tensor(
                            out=fusebf[pf][:, :], in0=psU[pg][:, :],
                            scalar=c_pack[:, col:col + 1],
                            in1=gbx[:, :], op0=ALU.mult, op1=ALU.mult)

                    @step("sync", r=(FB,), w=("fuse_d",), n_dma=1)
                    def _(e, sp=sp, it=it, tc=tc, pf=pf):
                        row = (sp * 2 + it) * 128
                        return e.dma_start(
                            out=fuse_d.ap()[row:row + 128,
                                            tc * T:(tc + 1) * T],
                            in_=fusebf[pf][:, :])

            emit_fa_ln(0)
            for c in range(NCH):
                p = c % 2
                BCD = f"bcd{p}"
                tc_cur = c // 4

                for hh in range(2):
                    XT, XBT, STT_ = f"xchH{hh}", "xbschH", f"stH{hh}"
                    xch3 = xchH[hh][:, :].rearrange("p (g d) -> p g d", g=2)
                    xbs3 = xbschH[hh][:, :].rearrange("p (g d) -> p g d", g=2)
                    xnb3 = xnbf[:, :].rearrange("p (g d) -> p g d", g=2)
                    lo = c * T + hh * 256

                    emit_ln(XT, STT_, "xnbf", xch3, stH[hh], xnb3,
                            xb[lo:lo + 256, :],
                            extra=(XBT, xbs[lo:lo + 256, :], xbs3))

                    @step("scalar", r=(XT, XBT, STT_), w=("xnbf", "xnsT"))
                    def _(e, hh=hh, xch3=xch3, xbs3=xbs3, xnb3=xnb3):
                        r = None
                        for gg in range(2):
                            g = 2 * hh + gg
                            e.activation(out=xnb3[:, gg, :], in_=xch3[:, gg, :],
                                         func=AF.Identity,
                                         scale=stH[hh][:, 10 + gg:11 + gg],
                                         bias=stH[hh][:, 12 + gg:13 + gg])
                            r = e.activation(out=xnsT3[:, g, :],
                                             in_=xbs3[:, gg, :],
                                             func=AF.Identity,
                                             scale=stH[hh][:, 10 + gg:11 + gg],
                                             bias=stH[hh][:, 12 + gg:13 + gg])
                        return r

                    @step("sync", r=("xnbf",), w=("xnTbf",), n_dma=2)
                    def _(e, xnb3=xnb3):
                        return [
                            e.dma_start(out=xnT4m[:, :, gg * 128:(gg + 1) * 128],
                                        in_=xnb3[:, gg, :]
                                        .rearrange("t (j p) -> t j p", j=NDT),
                                        transpose=True)
                            for gg in range(2)
                        ]

                    @step("tensor", r=("xnTbf", "c_wxpT"), w=("psE",))
                    def _(e):
                        r = None
                        for j2 in range(NDT):
                            r = e.matmul(psE[:, 0:256],
                                         c_wxpT[:, j2 * 96:(j2 + 1) * 96],
                                         xnT4m[:, j2, :],
                                         start=(j2 == 0), stop=(j2 == NDT - 1))
                        return r

                    @step("scalar", r=("psE",), w=("dtrawbf",))
                    def _(e, hh=hh):
                        return e.activation(
                            out=dtrawbf[:, hh * 256:(hh + 1) * 256],
                            in_=psE[0:64, 0:256], func=AF.Copy)

                    @step("vector", r=("psE",), w=("bcsb",))
                    def _(e, hh=hh):
                        return e.tensor_copy(
                            out=bcsb[:, hh * 256:(hh + 1) * 256],
                            in_=psE[64:96, 0:256])

                @step("sync", r=("bcsb",), w=(BCD,), n_dma=1)
                def _(e, p=p):
                    return e.dma_start(out=bcd.ap()[p], in_=bcsb[:, :])

                for j in range(NJ):
                    @step("tensor", r=("xnsT", "c_idn"), w=("psT",))
                    def _(e, j=j):
                        r = None
                        for g in range(G):
                            r = e.transpose(psT[:, g * 128:(g + 1) * 128],
                                            xnsT3[:, g, j * 128:(j + 1) * 128],
                                            c_idn[:, :])
                        return r

                    @step("vector", r=("psT",), w=("xnfT",))
                    def _(e, j=j):
                        return e.tensor_copy(out=xnfT[:, j * T:(j + 1) * T],
                                             in_=psT[:, :])

                for j in range(NJ):
                    @step("tensor", r=("dtrawbf", "c_wdtT"), w=("psBm",))
                    def _(e, j=j):
                        return e.matmul(psBm[:, :],
                                        c_wdtT[:, j * 128:(j + 1) * 128],
                                        dtrawbf[:, :], start=True, stop=True)

                    @step("scalar", r=("psBm", "c_pack"), w=("aq0",))
                    def _(e, j=j):
                        return e.activation(
                            out=zb, in_=psBm[:, :], func=AF.Identity,
                            bias=c_pack[:, BDT_ + j:BDT_ + j + 1])

                    @step("scalar", r=("aq0", "c_pack"), w=("aq0",))
                    def _(e):
                        e.activation(out=ebuf, in_=zb, func=AF.Abs)
                        e.activation(out=ebuf, in_=ebuf, func=AF.Exp,
                                     scale=c_pack[:, NEG1_:NEG1_ + 1])
                        return e.activation(out=ebuf, in_=ebuf, func=AF.Ln,
                                            bias=c_pack[:, ONE_:ONE_ + 1])

                    @step("vector", r=("aq0", "xnfT"), w=("dt4", "dtx4"))
                    def _(e, j=j):
                        e.scalar_tensor_tensor(
                            out=dt4[:, j * T:(j + 1) * T], in0=zb,
                            scalar=0.0, in1=ebuf, op0=ALU.max, op1=ALU.add)
                        return e.tensor_tensor(
                            out=dtx4[:, j * T:(j + 1) * T],
                            in0=dt4[:, j * T:(j + 1) * T],
                            in1=xnfT[:, j * T:(j + 1) * T], op=ALU.mult)

                @step("vector", r=(), w=("yb",))
                def _(e):
                    return e.memset(yb[:, :], 0.0)

                def emit_bc_load(np_, p):
                    pq = np_ % 2

                    @step("scalar", r=(BCD,), w=(f"brep{pq}", f"crep{pq}"),
                          n_dma=2)
                    def _(e, np_=np_, p=p, pq=pq):
                        return [
                            e.dma_start(
                                out=brep[pq][:, :]
                                .rearrange("p (k t) -> p k t", k=2),
                                in_=bcd.ap()[p, np_ * 2:(np_ + 1) * 2, :]
                                .unsqueeze(0).broadcast_to([128, 2, T])),
                            e.dma_start(
                                out=crep[pq][:, :]
                                .rearrange("p (k t) -> p k t", k=2),
                                in_=bcd.ap()[p, 16 + np_ * 2:18 + np_ * 2, :]
                                .unsqueeze(0).broadcast_to([128, 2, T])),
                        ]

                emit_bc_load(0, p)
                for np_ in range(NP):
                    pq = np_ % 2
                    BR, CR = f"brep{pq}", f"crep{pq}"

                    if np_ % 2 == 1:
                        sp0 = (c % 4) * 8 + (np_ // 2) * 2
                        emit_fa_span(tc_cur, sp0)
                        emit_fa_span(tc_cur, sp0 + 1)

                    for j in range(NJ):
                        pa = (np_ * NJ + j) % 2
                        ph = (np_ * NJ + j) % 3
                        AQ, HQ = f"aq{pa}", f"hq{ph}"
                        HPS = f"hps{np_ % 2}"
                        aq3 = aq[pa][:, :].rearrange("p (k t) -> p k t", k=2)
                        hq3 = hq[ph][:, :].rearrange("p (k t) -> p k t", k=2)

                        @step("scalar", r=("dt4", "c_acol"), w=(AQ,))
                        def _(e, np_=np_, j=j, aq3=aq3):
                            r = None
                            for k in range(2):
                                idx = j * NST + np_ * 2 + k
                                r = e.activation(
                                    out=aq3[:, k, :],
                                    in_=dt4[:, j * T:(j + 1) * T],
                                    func=AF.Exp,
                                    scale=c_acol[:, idx:idx + 1])
                            return r

                        if j == 0 and np_ + 1 < NP:
                            emit_bc_load(np_ + 1, p)

                        @step("vector", r=("dtx4", BR, AQ, f"hst_{np_}_{j}"),
                              w=("uq", HQ))
                        def _(e, np_=np_, j=j, pq=pq, aq3=aq3, hq3=hq3):
                            r = None
                            for k in range(2):
                                idx = j * NST + np_ * 2 + k
                                e.tensor_tensor(
                                    out=uq[:, :],
                                    in0=dtx4[:, j * T:(j + 1) * T],
                                    in1=brep[pq][:, k * T:(k + 1) * T],
                                    op=ALU.mult)
                                r = e.tensor_tensor_scan(
                                    out=hq3[:, k, :], data0=aq3[:, k, :],
                                    data1=uq[:, :],
                                    initial=hst[:, idx:idx + 1],
                                    op0=ALU.mult, op1=ALU.add)
                            return r

                        if np_ > 0:
                            @step("vector", r=(f"hps{1 - np_ % 2}", "yb"),
                                  w=("yb",))
                            def _(e, np_=np_, j=j):
                                return e.tensor_tensor(
                                    out=yb[:, j * T:(j + 1) * T],
                                    in0=yb[:, j * T:(j + 1) * T],
                                    in1=hps[1 - np_ % 2]
                                    [:, j * T:(j + 1) * T],
                                    op=ALU.add)

                        @step("gpsimd", r=(HQ, CR),
                              w=(HQ, HPS, f"hst_{np_}_{j}"))
                        def _(e, np_=np_, j=j, ph=ph, pq=pq, hq3=hq3):
                            base = j * NST + np_ * 2
                            e.tensor_copy(out=hst[:, base:base + 2],
                                          in_=hq3[:, :, T - 1:T])
                            e.tensor_tensor(out=hq[ph][:, :],
                                            in0=hq[ph][:, :],
                                            in1=crep[pq][:, :],
                                            op=ALU.mult)
                            return e.tensor_tensor(
                                out=hps[np_ % 2][:, j * T:(j + 1) * T],
                                in0=hq[ph][:, 0:T], in1=hq[ph][:, T:2 * T],
                                op=ALU.add)

                @step("vector", r=("hps1", "yb"), w=("yb",))
                def _(e):
                    r = None
                    for j in range(NJ):
                        r = e.tensor_tensor(
                            out=yb[:, j * T:(j + 1) * T],
                            in0=yb[:, j * T:(j + 1) * T],
                            in1=hps[1][:, j * T:(j + 1) * T], op=ALU.add)
                    return r

                @step("vector", r=("xnfT", "c_pack", "yb"), w=("yb",))
                def _(e):
                    r = None
                    for j in range(NJ):
                        r = e.scalar_tensor_tensor(
                            out=yb[:, j * T:(j + 1) * T],
                            in0=xnfT[:, j * T:(j + 1) * T],
                            scalar=c_pack[:, DPAR_ + j:DPAR_ + j + 1],
                            in1=yb[:, j * T:(j + 1) * T],
                            op0=ALU.mult, op1=ALU.add)
                    return r

                for g in range(G):
                    @step("tensor", r=("yb", "c_idn"), w=("psT",))
                    def _(e, g=g):
                        r = None
                        for jj in range(NJ):
                            r = e.transpose(
                                psT[:, jj * 128:(jj + 1) * 128],
                                yb[:, jj * T + g * 128:jj * T + (g + 1) * 128],
                                c_idn[:, :])
                        return r

                    @step("vector", r=("psT",), w=("xnsT",))
                    def _(e, g=g):
                        return e.tensor_copy(
                            out=xnsT[:, g * DS:(g + 1) * DS], in_=psT[:, :])

                @step("gpsimd", r=("xnsT",), w=("s_d",), n_dma=1)
                def _(e, c=c):
                    return e.dma_start(
                        out=s_d.ap()[c * T:(c + 1) * T, :]
                        .rearrange("(g p) d -> p g d", p=128),
                        in_=xnsT[:, :].rearrange("p (g d) -> p g d", g=G))

                if c % 4 == 3:
                    if PIECED_GATHER:
                        @step("gpsimd", r=("s_d",), w=(), cc_inc=True)
                        def _(e, j=c // 4):
                            return e.collective_compute(
                                "AllGather", ALU.bypass,
                                replica_groups=[[0, 1, 2, 3], [4, 5, 6, 7]],
                                ins=[s_d.ap()[j * 2048:(j + 1) * 2048, :]
                                     .opt()],
                                outs=[s_g.ap()[j].opt()])

                    if c + 1 < NCH:
                        emit_fa_ln(c // 4 + 1)

            if not PIECED_GATHER:
                @step("gpsimd", r=("s_d",), w=(), cc_inc=True)
                def _(e):
                    return e.collective_compute(
                        "AllGather", ALU.bypass,
                        replica_groups=[[0, 1, 2, 3], [4, 5, 6, 7]],
                        ins=[s_d.ap().opt()],
                        outs=[s_g.ap().rearrange("w t d -> (w t) d").opt()])

            barrier()

        with ExitStack() as fb:
            fsb2 = lambda n, s, d=F32: fb.enter_context(
                nc.sbuf_tensor(n, s, d))
            fusesb = fsb2("fusesb", [128, NIT * T], BF16)
            wdb = [fsb2(f"wdb{i}", [128, NIT * 128], BF16) for i in range(2)]
            gbuf = fsb2("gbuf", [128, NDT * T])
            gTt = fsb2("gTt", [128, D])
            ssel = fsb2("ssel", [128, 16 * DS])
            sacc = fsb2("sacc", [128, D])
            xt = fsb2("xt", [128, D])
            psC = fb.enter_context(nc.psum_tensor("psC", [128, 512], F32))
            psT2 = fb.enter_context(nc.psum_tensor("psT2", [128, 512], F32))

            fusesb3 = fusesb[:, :].rearrange("p (it t) -> p it t", it=NIT)
            gbuf3 = gbuf[:, :].rearrange("p (j t) -> p j t", j=NDT)
            ssel4 = ssel[:, :].rearrange("p (r q d) -> p r q d", r=4, q=4)
            if PIECED_GATHER:
                sgv = s_g.ap().rearrange(
                    "(q tw) (r c gg p) d -> p r q tw c gg d",
                    q=4, tw=2, r=4, c=4, gg=G, p=128)
            else:
                sgv = s_g.ap().rearrange("w t d -> (w t) d").rearrange(
                    "(r q c gg p) d -> p r q c gg d",
                    r=4, q=4, c=FTC, gg=G, p=128)

            for tc in range(FTC):
                @step("sync", r=("fuse_d",), w=("fusesb",), n_dma=1)
                def _(e, tc=tc):
                    return e.dma_start(
                        out=fusesb3,
                        in_=fuse_d.ap()[:, tc * T:(tc + 1) * T]
                        .rearrange("(it p) t -> p it t", p=128))

                for dt in range(NDT):
                    pw = dt % 2
                    WDB = f"wdb{pw}"
                    wdb3 = wdb[pw][:, :].rearrange("p (it d) -> p it d",
                                                   it=NIT)

                    @step("gpsimd", r=(), w=(WDB,), n_dma=1)
                    def _(e, dt=dt, pw=pw):
                        return e.dma_start(out=wdb[pw][:, :],
                                           in_=wdp[dt, :, :])

                    @step("tensor", r=(WDB, "fusesb"), w=("psC",))
                    def _(e, dt=dt, wdb3=wdb3):
                        r = None
                        for it in range(NIT):
                            r = e.matmul(psC[:, :], wdb3[:, it, :],
                                         fusesb3[:, it, :],
                                         start=(it == 0),
                                         stop=(it == NIT - 1))
                        return r

                    @step("scalar", r=("psC", "c_pack"), w=("gbuf",))
                    def _(e, dt=dt):
                        col = SD_ + dt
                        return e.activation(out=gbuf3[:, dt, :],
                                            in_=psC[:, :], func=AF.Silu,
                                            scale=c_pack[:, col:col + 1])

                for g in range(G):
                    if PIECED_GATHER:
                        @step("sync", r=(), w=("ssel", "xt"), n_dma=5,
                              cc_wait=7 + tc // 4)
                        def _(e, tc=tc, g=g):
                            out = [
                                e.dma_start(out=ssel4[:, :, q, :],
                                            in_=sgv[:, :, q, tc // 4,
                                                    tc % 4, g, :])
                                for q in range(4)
                            ]
                            out.append(
                                e.dma_start(out=xt[:, :],
                                            in_=xq[tc * T + g * 128:
                                                   tc * T + (g + 1) * 128,
                                                   :]))
                            return out
                    else:
                        @step("sync", r=(), w=("ssel", "xt"), n_dma=2,
                              cc_wait=1)
                        def _(e, tc=tc, g=g):
                            return [
                                e.dma_start(out=ssel4,
                                            in_=sgv[:, :, :, tc, g, :]),
                                e.dma_start(out=xt[:, :],
                                            in_=xq[tc * T + g * 128:
                                                   tc * T + (g + 1) * 128,
                                                   :]),
                            ]

                    for rr in range(4):
                        @step("tensor", r=("gbuf", "c_idn"), w=("psT2",))
                        def _(e, g=g, rr=rr):
                            r = None
                            for u in range(4):
                                dt = rr * 4 + u
                                r = e.transpose(
                                    psT2[:, u * 128:(u + 1) * 128],
                                    gbuf[:, dt * T + g * 128:
                                         dt * T + (g + 1) * 128],
                                    c_idn[:, :])
                            return r

                        @step("vector", r=("psT2",), w=("gTt",))
                        def _(e, rr=rr):
                            return e.tensor_copy(
                                out=gTt[:, rr * 512:(rr + 1) * 512],
                                in_=psT2[:, :])

                    @step("vector", r=("ssel", "c_pack"), w=("sacc",))
                    def _(e):
                        r = None
                        for r4 in range(4):
                            e.tensor_single_scalar(
                                sacc[:, r4 * DS:(r4 + 1) * DS],
                                ssel4[:, r4, 0, :],
                                c_pack[:, MASK_:MASK_ + 1], ALU.mult)
                            for Q in range(1, 4):
                                r = e.scalar_tensor_tensor(
                                    out=sacc[:, r4 * DS:(r4 + 1) * DS],
                                    in0=ssel4[:, r4, Q, :],
                                    scalar=c_pack[:, MASK_ + Q:MASK_ + Q + 1],
                                    in1=sacc[:, r4 * DS:(r4 + 1) * DS],
                                    op0=ALU.mult, op1=ALU.add)
                        return r

                    @step("vector", r=("gTt", "xt", "sacc"), w=("sacc",))
                    def _(e):
                        e.tensor_tensor(out=sacc[:, :], in0=sacc[:, :],
                                        in1=gTt[:, :], op=ALU.mult)
                        return e.tensor_tensor(out=sacc[:, :], in0=sacc[:, :],
                                               in1=xt[:, :], op=ALU.add)

                    @step("gpsimd", r=("sacc",), w=(), n_dma=1)
                    def _(e, tc=tc, g=g):
                        return e.dma_start(
                            out=outq[tc * T + g * 128:tc * T + (g + 1) * 128,
                                     :],
                            in_=sacc[:, :])

            barrier()

        RELAXED = set() if False else {
            "aq0", "aq1", "hq0", "hq1", "hq2", "uq", "yb", "hps0", "hps1",
            "hst", "dt4", "dtx4", "xnfT", "xnsT", "xnbf", "xnTbf", "xnT",
            "gbx", "fusebf", "fusesb", "gbuf", "gTt", "sacc", "ssel", "xt",
            "brep0", "brep1", "crep0", "crep1",
            "wgb0", "wgb1", "wub0", "wub1", "wdb0", "wdb1",
        }
        engcnt = {e: 0 for e in ENGS}
        last_w = {}
        readers = {}
        stepidx = {e: 0 for e in ENGS}
        plans = []

        for s in steps:
            waits = {}

            def add(eng2, c2):
                if waits.get(eng2, 0) < c2:
                    waits[eng2] = c2

            if s.barrier:
                for e2 in ENGS:
                    if e2 != s.eng and engcnt[e2] > 0:
                        add(e2, engcnt[e2])
            else:
                for t in s.r:
                    if t in last_w:
                        e2, c2, si2, wdma = last_w[t]
                        if e2 != s.eng:
                            add(e2, c2)
                        elif wdma or (stepidx[s.eng] - si2 <= 2
                                      and t not in RELAXED):
                            add(e2, c2)
                    if t.startswith("ps"):
                        for e2, c2 in readers.get(t, {}).items():
                            if e2 != s.eng:
                                add(e2, c2)
                for t in s.w:
                    if t in last_w:
                        e2, c2, si2, wdma = last_w[t]
                        if e2 != s.eng:
                            add(e2, c2)
                        elif wdma:
                            add(e2, c2)
                    for e2, c2 in readers.get(t, {}).items():
                        if e2 != s.eng:
                            add(e2, c2)
                        elif s.dma:
                            add(e2, c2)

            plans.append(waits)
            engcnt[s.eng] += s.inc
            si = stepidx[s.eng]
            stepidx[s.eng] += 1
            if not s.barrier:
                for t in s.w:
                    last_w[t] = (s.eng, engcnt[s.eng], si, s.dma)
                    readers[t] = {}
                for t in s.r:
                    readers.setdefault(t, {})[s.eng] = engcnt[s.eng]
                    if t.startswith("ps") and t not in s.w:
                        pass

        totals = dict(engcnt)

        with nc.Block() as block:
            def make_body(engname):
                def body(eng):
                    watermark = {e2: 0 for e2 in ENGS}
                    for s, waits in zip(steps, plans):
                        if s.eng != engname:
                            continue
                        for e2 in ENGS:
                            c2 = waits.get(e2, 0)
                            if c2 > watermark[e2]:
                                eng.wait_ge(semE[e2], c2)
                                watermark[e2] = c2
                        if s.cc_wait:
                            eng.wait_ge(semC, s.cc_wait)
                        if s.barrier:
                            eng.nop().then_inc(semE[engname], 1)
                            continue
                        ret = s.fn(eng)
                        if s.cc_inc:
                            ret.then_inc(semC, 1)
                        elif s.dma:
                            instrs = ret if isinstance(ret, list) else [ret]
                            assert 16 * len(instrs) == s.inc
                            for ins_ in instrs:
                                ins_.then_inc(semE[engname], 16)
                        else:
                            ret.then_inc(semE[engname], 1)
                    for e2 in ENGS:
                        if totals[e2] > watermark[e2]:
                            eng.wait_ge(semE[e2], totals[e2])
                return body

            block.sync(make_body("sync"))
            block.gpsimd(make_body("gpsimd"))
            block.scalar(make_body("scalar"))
            block.vector(make_body("vector"))
            block.tensor(make_body("tensor"))

    return nc


def kernel(L0_attn_x, L0_attn_W_xproj, L0_attn_W_dt, L0_attn_b_dt,
           L0_attn_A_log, L0_attn_D_param, L0_ffn_w_gate, L0_ffn_w_up,
           L0_ffn_w_down, L0_ffn_s_gate, L0_ffn_s_up, L0_ffn_s_down):
    x = np.asarray(L0_attn_x, dtype=np.float32)
    W_xproj = np.asarray(L0_attn_W_xproj, dtype=np.float32)
    W_dt = np.asarray(L0_attn_W_dt, dtype=np.float32)
    b_dt = np.asarray(L0_attn_b_dt, dtype=np.float32)
    A_log = np.asarray(L0_attn_A_log, dtype=np.float32)
    D_param = np.asarray(L0_attn_D_param, dtype=np.float32)
    w_gate = np.asarray(L0_ffn_w_gate)
    w_up = np.asarray(L0_ffn_w_up)
    w_down = np.asarray(L0_ffn_w_down)
    s_gate = np.asarray(L0_ffn_s_gate, dtype=np.float32)
    s_up = np.asarray(L0_ffn_s_up, dtype=np.float32)
    s_down = np.asarray(L0_ffn_s_down, dtype=np.float32)

    nc = build_program()

    wxpT = np.ascontiguousarray(W_xproj.T)
    wgp = np.ascontiguousarray(
        w_gate.reshape(16, 128, 32, 256).transpose(2, 1, 0, 3)
    ).reshape(NSP, 128, NDT * 256)
    wup = np.ascontiguousarray(
        w_up.reshape(16, 128, 32, 256).transpose(2, 1, 0, 3)
    ).reshape(NSP, 128, NDT * 256)
    wdp = np.ascontiguousarray(
        w_down.reshape(64, 128, 16, 128).transpose(2, 1, 0, 3)
    ).reshape(16, 128, INTER)
    idn = np.eye(128, dtype=np.float32)
    sg_c = np.ascontiguousarray(s_gate.reshape(64, 128).T)
    su_c = np.ascontiguousarray(s_up.reshape(64, 128).T)
    sd_c = np.ascontiguousarray(s_down.reshape(16, 128).T)

    in_maps = []
    for k in range(8):
        b, r = k // 4, k % 4
        dlo = r * DS
        cp = np.zeros((128, CPW), dtype=np.float32)
        cp[:, EPS_] = EPS
        cp[:, ONE_] = 1.0
        cp[:, NEG1_] = -1.0
        cp[:, BDT_:BDT_ + 4] = b_dt[dlo:dlo + DS].reshape(4, 128).T
        cp[:, ALOG_:ALOG_ + 64] = (
            A_log[dlo:dlo + DS].reshape(4, 128, 16).transpose(1, 0, 2)
            .reshape(128, 64))
        cp[:, DPAR_:DPAR_ + 4] = D_param[dlo:dlo + DS].reshape(4, 128).T
        cp[:, SG_:SG_ + 64] = sg_c
        cp[:, SU_:SU_ + 64] = su_c
        cp[:, SD_:SD_ + 16] = sd_c
        cp[:, MASK_ + r] = 1.0
        in_maps.append(dict(
            xb=np.ascontiguousarray(x[b]),
            xbs=np.ascontiguousarray(x[b][:, dlo:dlo + DS]),
            xq=np.ascontiguousarray(x[b, r * TQ:(r + 1) * TQ]),
            wxpT=wxpT,
            wdtT=np.ascontiguousarray(W_dt[dlo:dlo + DS].T),
            cpack=cp,
            idn=idn,
            wgp=wgp,
            wup=wup,
            wdp=wdp,
        ))

    res = run_bass_kernel_spmd(nc, in_maps, core_ids=list(range(8)))

    out = np.empty((2, L, D), dtype=np.float32)
    for k in range(8):
        b, r = k // 4, k % 4
        out[b, r * TQ:(r + 1) * TQ] = np.asarray(res.results[k]["outq"])
    return out


# version 19
# speedup vs baseline: 1.4604x; 1.2550x over previous
import numpy as np
import concourse.bass as bass
import concourse.mybir as mybir
from concourse.bass_utils import run_bass_kernel_spmd
from contextlib import ExitStack

F32 = mybir.dt.float32
BF16 = mybir.dt.bfloat16
I8 = mybir.dt.int8
AF = mybir.ActivationFunctionType
ALU = mybir.AluOpType
AX = mybir.AxisListType

L = 16384
D = 2048
T = 512
NCH = L // T
G = 4
DS = 512
NJ = 4
NDT = 16
NST = 16
NP = 8
EPS = 1e-5
TQ = 4096
FTC = TQ // T
NSP = 32
NIT = 64
INTER = 8192

BDT_, ALOG_, DPAR_, SG_, SU_, SD_, MASK_ = 0, 4, 68, 72, 136, 200, 216
EPS_, ONE_, NEG1_ = 220, 221, 222
CPW = 224

ENGS = ("sync", "gpsimd", "scalar", "vector", "tensor")
USE_ACCUM_LN = True
PROD_ON_GPSIMD = True
PIECED_GATHER = False


class Step:
    __slots__ = ("eng", "fn", "inc", "dma", "r", "w", "cc_wait", "cc_inc",
                 "barrier")

    def __init__(self, eng, fn, inc, dma, r, w, cc_wait, cc_inc, barrier):
        self.eng, self.fn, self.inc, self.dma = eng, fn, inc, dma
        self.r, self.w = r, w
        self.cc_wait, self.cc_inc, self.barrier = cc_wait, cc_inc, barrier


def build_program():
    nc = bass.Bass("TRN2", target_bir_lowering=False, debug=False)

    xb = nc.dram_tensor("xb", [L, D], F32, kind="ExternalInput").ap()
    xbs = nc.dram_tensor("xbs", [L, DS], F32, kind="ExternalInput").ap()
    xq = nc.dram_tensor("xq", [TQ, D], F32, kind="ExternalInput").ap()
    wxpT = nc.dram_tensor("wxpT", [D, 96], F32, kind="ExternalInput").ap()
    wdtT = nc.dram_tensor("wdtT", [64, DS], F32, kind="ExternalInput").ap()
    cpack = nc.dram_tensor("cpack", [128, CPW], F32, kind="ExternalInput").ap()
    idn = nc.dram_tensor("idn", [128, 128], F32, kind="ExternalInput").ap()
    wgp = nc.dram_tensor("wgp", [NSP, 128, NDT * 256], I8,
                         kind="ExternalInput").ap()
    wup = nc.dram_tensor("wup", [NSP, 128, NDT * 256], I8,
                         kind="ExternalInput").ap()
    wdp = nc.dram_tensor("wdp", [16, 128, INTER], I8, kind="ExternalInput").ap()
    outq = nc.dram_tensor("outq", [TQ, D], F32, kind="ExternalOutput").ap()

    bcd = nc.dram_tensor("bcd", [2, 32, T], F32, kind="Internal")
    s_d = nc.dram_tensor("s_d", [L, DS], F32, kind="Internal")
    s_g = nc.dram_tensor("s_g", [8, 4 * 2048, DS], F32, kind="Internal")
    fuse_d = nc.dram_tensor("fuse_d", [INTER, TQ], BF16, kind="Internal")

    steps = []

    def step(eng, r, w, n_dma=0, cc_wait=0, cc_inc=False):
        def deco(fn):
            inc = 0 if cc_inc else (16 * n_dma if n_dma else 1)
            steps.append(Step(eng, fn, inc, n_dma > 0, tuple(r), tuple(w),
                              cc_wait, cc_inc, False))
        return deco

    def barrier():
        for eng in ENGS:
            steps.append(Step(eng, None, 1, False, (), (), False, False, True))

    with ExitStack() as top:
        sbuf = lambda n, s, d=F32: top.enter_context(nc.sbuf_tensor(n, s, d))
        c_wxpT = sbuf("c_wxpT", [128, NDT * 96], BF16)
        c_wdtT = sbuf("c_wdtT", [64, DS], BF16)
        c_pack = sbuf("c_pack", [128, CPW])
        c_acol = sbuf("c_acol", [128, 64])
        c_idn = sbuf("c_idn", [128, 128])

        semE = {e: top.enter_context(nc.semaphore(f"sem_{e}")) for e in ENGS}
        semC = top.enter_context(nc.semaphore("semC"))

        with ExitStack() as pre:
            c_wxpT_f = pre.enter_context(
                nc.sbuf_tensor("c_wxpT_f", [128, NDT * 96], F32))
            c_wdtT_f = pre.enter_context(
                nc.sbuf_tensor("c_wdtT_f", [64, DS], F32))

            @step("sync", r=(), w=("c_wxpT_f", "c_wdtT_f", "c_pack", "c_idn"),
                  n_dma=4)
            def _(e):
                return [
                    e.dma_start(out=c_wxpT_f[:, :]
                                .rearrange("p (j e) -> p j e", j=NDT),
                                in_=wxpT.rearrange("(j p) e -> p j e", p=128)),
                    e.dma_start(out=c_wdtT_f[:, :], in_=wdtT),
                    e.dma_start(out=c_pack[:, :], in_=cpack),
                    e.dma_start(out=c_idn[:, :], in_=idn),
                ]

            @step("scalar", r=("c_wxpT_f", "c_wdtT_f", "c_pack"),
                  w=("c_wxpT", "c_wdtT", "c_acol"))
            def _(e):
                e.activation(out=c_wxpT[:, :], in_=c_wxpT_f[:, :], func=AF.Copy)
                e.activation(out=c_wdtT[:, :], in_=c_wdtT_f[:, :], func=AF.Copy)
                return e.activation(out=c_acol[:, :],
                                    in_=c_pack[:, ALOG_:ALOG_ + 64],
                                    func=AF.Exp)

            barrier()

        with ExitStack() as ms:
            msb = lambda n, s, d=F32: ms.enter_context(nc.sbuf_tensor(n, s, d))
            psT = ms.enter_context(nc.psum_tensor("psT", [128, 512], F32))
            psE = ms.enter_context(nc.psum_tensor("psE", [96, 512], F32))
            psBm = ms.enter_context(nc.psum_tensor("psBm", [128, 512], F32))
            psG = [ms.enter_context(nc.psum_tensor(f"psG{i}", [128, 512], F32))
                   for i in range(2)]
            psU = [ms.enter_context(nc.psum_tensor(f"psU{i}", [128, 512], F32))
                   for i in range(2)]
            xchH = [msb(f"xchH{i}", [128, 2 * D]) for i in range(2)]
            xbschH = [msb("xbschH", [128, 2 * DS])] * 2
            stH = [msb(f"stH{i}", [128, 16]) for i in range(2)]
            xnbf = msb("xnbf", [128, 2 * D], BF16)
            xnTbf = msb("xnTbf", [128, NDT * 256], BF16)
            xnsT = msb("xnsT", [128, G * DS])
            xnfT = msb("xnfT", [128, NJ * T])
            dtrawbf = msb("dtrawbf", [64, T], BF16)
            dt4 = msb("dt4", [128, NJ * T])
            dtx4 = msb("dtx4", [128, NJ * T])
            brep = [msb(f"brep{i}", [128, 2 * T]) for i in range(2)]
            crep = [msb(f"crep{i}", [128, 2 * T]) for i in range(2)]
            aq = [msb(f"aq{i}", [128, 2 * T]) for i in range(2)]
            uq = msb("uq", [128, T])
            hq = [msb(f"hq{i}", [128, 2 * T]) for i in range(3)]
            hps = [msb(f"hps{i}", [128, NJ * T]) for i in range(2)]
            yb = msb("yb", [128, NJ * T])
            hst = msb("hst", [128, 64])
            zb = aq[0][:, 0:T]
            ebuf = aq[0][:, T:2 * T]
            bcsb = msb("bcsb", [32, T])
            xnT = msb("xnT", [128, NDT * T], BF16)
            scr = None if USE_ACCUM_LN else msb("scr", [128, D])
            wgb = [msb(f"wgb{i}", [128, NDT * 256], BF16) for i in range(2)]
            wub = [msb(f"wub{i}", [128, NDT * 256], BF16) for i in range(2)]
            gbx = msb("gbx", [128, 512])
            fusebf = [msb(f"fusebf{i}", [128, 512], BF16) for i in range(2)]

            xnsT3 = xnsT[:, :].rearrange("p (g d) -> p g d", g=G)
            xnT4m = xnTbf[:, :].rearrange("p (j t) -> p j t", j=NDT)
            fxnT4 = xnT[:, :].rearrange("p (j gg t) -> p j gg t", j=NDT, gg=G)

            @step("vector", r=("c_acol",),
                  w=("c_acol",) + tuple(f"hst_{n}_{j}" for n in range(NP)
                                        for j in range(NJ)))
            def _(e):
                e.tensor_scalar_mul(c_acol[:, :], c_acol[:, :], -1.0)
                return e.memset(hst[:, :], 0.0)

            def emit_ln(XT, STT_, XNB, xch3, st_, xnb3, src_rows, extra=None):
                @step("sync", r=(), w=(XT,) + ((extra[0],) if extra else ()),
                      n_dma=2 if extra else 1)
                def _(e):
                    out = [e.dma_start(out=xch3,
                                       in_=src_rows
                                       .rearrange("(g p) d -> p g d", p=128))]
                    if extra:
                        out.append(e.dma_start(
                            out=extra[2],
                            in_=extra[1].rearrange("(g p) d -> p g d", p=128)))
                    return out

                for gg in range(2):
                    if USE_ACCUM_LN:
                        @step("scalar", r=(XT,), w=(XNB, STT_))
                        def _(e, gg=gg):
                            return e.activation(out=xnb3[:, gg, :],
                                                in_=xch3[:, gg, :],
                                                func=AF.Square,
                                                accum_out=st_[:, 2 + gg:3 + gg])

                        @step("vector", r=(XT,), w=(STT_,))
                        def _(e, gg=gg):
                            return e.reduce_sum(st_[:, gg:gg + 1],
                                                xch3[:, gg, :], axis=AX.X)
                    else:
                        @step("scalar", r=(XT,), w=("scr",))
                        def _(e, gg=gg):
                            return e.activation(out=scr[:, :],
                                                in_=xch3[:, gg, :],
                                                func=AF.Square)

                        @step("vector", r=(XT, "scr"), w=(STT_,))
                        def _(e, gg=gg):
                            e.reduce_sum(st_[:, gg:gg + 1], xch3[:, gg, :],
                                         axis=AX.X)
                            return e.reduce_sum(st_[:, 2 + gg:3 + gg],
                                                scr[:, :], axis=AX.X)

                @step("vector", r=(STT_,), w=(STT_,))
                def _(e):
                    e.tensor_scalar_mul(st_[:, 4:6], st_[:, 0:2], 1.0 / D)
                    return e.tensor_scalar_mul(st_[:, 6:8], st_[:, 2:4],
                                               1.0 / D)

                @step("vector", r=(STT_,), w=(STT_,))
                def _(e):
                    return e.tensor_tensor(out=st_[:, 8:10], in0=st_[:, 4:6],
                                           in1=st_[:, 4:6], op=ALU.mult)

                @step("vector", r=(STT_,), w=(STT_,))
                def _(e):
                    return e.tensor_tensor(out=st_[:, 6:8], in0=st_[:, 6:8],
                                           in1=st_[:, 8:10], op=ALU.subtract)

                @step("scalar", r=(STT_, "c_pack"), w=(STT_,))
                def _(e):
                    return e.activation(out=st_[:, 8:10], in_=st_[:, 6:8],
                                        func=AF.Sqrt,
                                        bias=c_pack[:, EPS_:EPS_ + 1])

                @step("vector", r=(STT_,), w=(STT_,))
                def _(e):
                    return e.reciprocal(st_[:, 10:12], st_[:, 8:10])

                @step("vector", r=(STT_,), w=(STT_,))
                def _(e):
                    return e.tensor_tensor(out=st_[:, 12:14], in0=st_[:, 4:6],
                                           in1=st_[:, 10:12], op=ALU.mult)

                @step("vector", r=(STT_,), w=(STT_,))
                def _(e):
                    return e.tensor_scalar_mul(st_[:, 12:14], st_[:, 12:14],
                                               -1.0)

            def emit_fa_ln(tc):
                for hh in range(2):
                    lo = tc * T + hh * 256
                    XT, STT_ = f"xchH{hh}", f"stH{hh}"
                    fxch3 = xchH[hh][:, :].rearrange("p (g d) -> p g d", g=2)
                    fxnbf3 = xnbf[:, :].rearrange("p (g d) -> p g d", g=2)
                    emit_ln(XT, STT_, "xnbf", fxch3, stH[hh],
                            fxnbf3, xq[lo:lo + 256, :])

                    @step("scalar", r=(XT, STT_), w=("xnbf",))
                    def _(e, hh=hh, fxch3=fxch3, fxnbf3=fxnbf3):
                        r = None
                        for gg in range(2):
                            r = e.activation(out=fxnbf3[:, gg, :],
                                             in_=fxch3[:, gg, :],
                                             func=AF.Identity,
                                             scale=stH[hh][:, 10 + gg:11 + gg],
                                             bias=stH[hh][:, 12 + gg:13 + gg])
                        return r

                    @step("sync", r=("xnbf",), w=("xnT",), n_dma=2)
                    def _(e, hh=hh, fxnbf3=fxnbf3):
                        return [
                            e.dma_start(out=fxnT4[:, :, 2 * hh + gg, :],
                                        in_=fxnbf3[:, gg, :]
                                        .rearrange("t (j p) -> t j p", j=NDT),
                                        transpose=True)
                            for gg in range(2)
                        ]

            def emit_fa_wload(sp):
                pw = sp % 2

                @step("gpsimd", r=(), w=(f"wgb{pw}", f"wub{pw}"), n_dma=2)
                def _(e, sp=sp, pw=pw):
                    return [
                        e.dma_start(out=wgb[pw][:, :], in_=wgp[sp, :, :]),
                        e.dma_start(out=wub[pw][:, :], in_=wup[sp, :, :]),
                    ]

            def emit_fa_mm(sp, it):
                pw, pg = sp % 2, it % 2
                wgb3 = wgb[pw][:, :].rearrange("p (j i) -> p j i", j=NDT)
                wub3 = wub[pw][:, :].rearrange("p (j i) -> p j i", j=NDT)

                @step("tensor", r=(f"wgb{pw}", f"wub{pw}", "xnT"),
                      w=(f"psG{pg}", f"psU{pg}"))
                def _(e, it=it, pg=pg, wgb3=wgb3, wub3=wub3):
                    r = None
                    for j in range(NDT):
                        e.matmul(psG[pg][:, :],
                                 wgb3[:, j, it * 128:(it + 1) * 128],
                                 xnT[:, j * T:(j + 1) * T],
                                 start=(j == 0), stop=(j == NDT - 1))
                    for j in range(NDT):
                        r = e.matmul(
                            psU[pg][:, :],
                            wub3[:, j, it * 128:(it + 1) * 128],
                            xnT[:, j * T:(j + 1) * T],
                            start=(j == 0), stop=(j == NDT - 1))
                    return r

            def emit_fa_silu(sp, it):
                pg = it % 2

                @step("scalar", r=(f"psG{pg}", "c_pack"), w=("gbx",))
                def _(e, sp=sp, it=it, pg=pg):
                    col = SG_ + sp * 2 + it
                    return e.activation(out=gbx[:, :],
                                        in_=psG[pg][:, :],
                                        func=AF.Silu,
                                        scale=c_pack[:, col:col + 1])

            def emit_fa_evac(tc, sp, it):
                pg, pf = it % 2, it % 2
                FB = f"fusebf{pf}"

                @step("vector", r=(f"psU{pg}", "gbx", "c_pack"), w=(FB,))
                def _(e, sp=sp, it=it, pg=pg, pf=pf):
                    col = SU_ + sp * 2 + it
                    return e.scalar_tensor_tensor(
                        out=fusebf[pf][:, :], in0=psU[pg][:, :],
                        scalar=c_pack[:, col:col + 1],
                        in1=gbx[:, :], op0=ALU.mult, op1=ALU.mult)

                @step("sync", r=(FB,), w=("fuse_d",), n_dma=1)
                def _(e, sp=sp, it=it, tc=tc, pf=pf):
                    row = (sp * 2 + it) * 128
                    return e.dma_start(
                        out=fuse_d.ap()[row:row + 128,
                                        tc * T:(tc + 1) * T],
                        in_=fusebf[pf][:, :])

            emit_fa_ln(0)
            for c in range(NCH):
                p = c % 2
                BCD = f"bcd{p}"
                tc_cur = c // 4

                for hh in range(2):
                    XT, XBT, STT_ = f"xchH{hh}", "xbschH", f"stH{hh}"
                    xch3 = xchH[hh][:, :].rearrange("p (g d) -> p g d", g=2)
                    xbs3 = xbschH[hh][:, :].rearrange("p (g d) -> p g d", g=2)
                    xnb3 = xnbf[:, :].rearrange("p (g d) -> p g d", g=2)
                    lo = c * T + hh * 256

                    emit_ln(XT, STT_, "xnbf", xch3, stH[hh], xnb3,
                            xb[lo:lo + 256, :],
                            extra=(XBT, xbs[lo:lo + 256, :], xbs3))

                    @step("scalar", r=(XT, XBT, STT_), w=("xnbf", "xnsT"))
                    def _(e, hh=hh, xch3=xch3, xbs3=xbs3, xnb3=xnb3):
                        r = None
                        for gg in range(2):
                            g = 2 * hh + gg
                            e.activation(out=xnb3[:, gg, :], in_=xch3[:, gg, :],
                                         func=AF.Identity,
                                         scale=stH[hh][:, 10 + gg:11 + gg],
                                         bias=stH[hh][:, 12 + gg:13 + gg])
                            r = e.activation(out=xnsT3[:, g, :],
                                             in_=xbs3[:, gg, :],
                                             func=AF.Identity,
                                             scale=stH[hh][:, 10 + gg:11 + gg],
                                             bias=stH[hh][:, 12 + gg:13 + gg])
                        return r

                    @step("sync", r=("xnbf",), w=("xnTbf",), n_dma=2)
                    def _(e, xnb3=xnb3):
                        return [
                            e.dma_start(out=xnT4m[:, :, gg * 128:(gg + 1) * 128],
                                        in_=xnb3[:, gg, :]
                                        .rearrange("t (j p) -> t j p", j=NDT),
                                        transpose=True)
                            for gg in range(2)
                        ]

                    @step("tensor", r=("xnTbf", "c_wxpT"), w=("psE",))
                    def _(e):
                        r = None
                        for j2 in range(NDT):
                            r = e.matmul(psE[:, 0:256],
                                         c_wxpT[:, j2 * 96:(j2 + 1) * 96],
                                         xnT4m[:, j2, :],
                                         start=(j2 == 0), stop=(j2 == NDT - 1))
                        return r

                    @step("scalar", r=("psE",), w=("dtrawbf",))
                    def _(e, hh=hh):
                        return e.activation(
                            out=dtrawbf[:, hh * 256:(hh + 1) * 256],
                            in_=psE[0:64, 0:256], func=AF.Copy)

                    @step("vector", r=("psE",), w=("bcsb",))
                    def _(e, hh=hh):
                        return e.tensor_copy(
                            out=bcsb[:, hh * 256:(hh + 1) * 256],
                            in_=psE[64:96, 0:256])

                @step("sync", r=("bcsb",), w=(BCD,), n_dma=1)
                def _(e, p=p):
                    return e.dma_start(out=bcd.ap()[p], in_=bcsb[:, :])

                for j in range(NJ):
                    @step("tensor", r=("xnsT", "c_idn"), w=("psT",))
                    def _(e, j=j):
                        r = None
                        for g in range(G):
                            r = e.transpose(psT[:, g * 128:(g + 1) * 128],
                                            xnsT3[:, g, j * 128:(j + 1) * 128],
                                            c_idn[:, :])
                        return r

                    @step("vector", r=("psT",), w=("xnfT",))
                    def _(e, j=j):
                        return e.tensor_copy(out=xnfT[:, j * T:(j + 1) * T],
                                             in_=psT[:, :])

                for j in range(NJ):
                    @step("tensor", r=("dtrawbf", "c_wdtT"), w=("psBm",))
                    def _(e, j=j):
                        return e.matmul(psBm[:, :],
                                        c_wdtT[:, j * 128:(j + 1) * 128],
                                        dtrawbf[:, :], start=True, stop=True)

                    @step("scalar", r=("psBm", "c_pack"), w=("aq0",))
                    def _(e, j=j):
                        return e.activation(
                            out=zb, in_=psBm[:, :], func=AF.Identity,
                            bias=c_pack[:, BDT_ + j:BDT_ + j + 1])

                    @step("scalar", r=("aq0", "c_pack"), w=("aq0",))
                    def _(e):
                        e.activation(out=ebuf, in_=zb, func=AF.Abs)
                        e.activation(out=ebuf, in_=ebuf, func=AF.Exp,
                                     scale=c_pack[:, NEG1_:NEG1_ + 1])
                        return e.activation(out=ebuf, in_=ebuf, func=AF.Ln,
                                            bias=c_pack[:, ONE_:ONE_ + 1])

                    @step("vector", r=("aq0", "xnfT"), w=("dt4", "dtx4"))
                    def _(e, j=j):
                        e.scalar_tensor_tensor(
                            out=dt4[:, j * T:(j + 1) * T], in0=zb,
                            scalar=0.0, in1=ebuf, op0=ALU.max, op1=ALU.add)
                        return e.tensor_tensor(
                            out=dtx4[:, j * T:(j + 1) * T],
                            in0=dt4[:, j * T:(j + 1) * T],
                            in1=xnfT[:, j * T:(j + 1) * T], op=ALU.mult)

                @step("vector", r=(), w=("yb",))
                def _(e):
                    return e.memset(yb[:, :], 0.0)

                def emit_bc_load(np_, p):
                    pq = np_ % 2

                    @step("scalar", r=(BCD,), w=(f"brep{pq}", f"crep{pq}"),
                          n_dma=2)
                    def _(e, np_=np_, p=p, pq=pq):
                        return [
                            e.dma_start(
                                out=brep[pq][:, :]
                                .rearrange("p (k t) -> p k t", k=2),
                                in_=bcd.ap()[p, np_ * 2:(np_ + 1) * 2, :]
                                .unsqueeze(0).broadcast_to([128, 2, T])),
                            e.dma_start(
                                out=crep[pq][:, :]
                                .rearrange("p (k t) -> p k t", k=2),
                                in_=bcd.ap()[p, 16 + np_ * 2:18 + np_ * 2, :]
                                .unsqueeze(0).broadcast_to([128, 2, T])),
                        ]

                def emit_aexp(np_, j):
                    pa = (np_ * NJ + j) % 2
                    aq3 = aq[pa][:, :].rearrange("p (k t) -> p k t", k=2)

                    @step("scalar", r=("dt4", "c_acol"), w=(f"aq{pa}",))
                    def _(e, np_=np_, j=j, aq3=aq3):
                        r = None
                        for k in range(2):
                            idx = j * NST + np_ * 2 + k
                            r = e.activation(
                                out=aq3[:, k, :],
                                in_=dt4[:, j * T:(j + 1) * T],
                                func=AF.Exp,
                                scale=c_acol[:, idx:idx + 1])
                        return r

                def emit_uqscan(np_, j):
                    pa = (np_ * NJ + j) % 2
                    ph = (np_ * NJ + j) % 3
                    pq = np_ % 2
                    aq3 = aq[pa][:, :].rearrange("p (k t) -> p k t", k=2)
                    hq3 = hq[ph][:, :].rearrange("p (k t) -> p k t", k=2)

                    @step("vector", r=("dtx4", f"brep{pq}", f"aq{pa}",
                                       f"hst_{np_}_{j}"),
                          w=("uq", f"hq{ph}"))
                    def _(e, np_=np_, j=j, pq=pq, aq3=aq3, hq3=hq3):
                        r = None
                        for k in range(2):
                            idx = j * NST + np_ * 2 + k
                            e.tensor_tensor(
                                out=uq[:, :],
                                in0=dtx4[:, j * T:(j + 1) * T],
                                in1=brep[pq][:, k * T:(k + 1) * T],
                                op=ALU.mult)
                            r = e.tensor_tensor_scan(
                                out=hq3[:, k, :], data0=aq3[:, k, :],
                                data1=uq[:, :],
                                initial=hst[:, idx:idx + 1],
                                op0=ALU.mult, op1=ALU.add)
                        return r

                def emit_ysum(np_, j):
                    if np_ == 0:
                        return

                    @step("vector", r=(f"hps{1 - np_ % 2}", "yb"), w=("yb",))
                    def _(e, np_=np_, j=j):
                        return e.tensor_tensor(
                            out=yb[:, j * T:(j + 1) * T],
                            in0=yb[:, j * T:(j + 1) * T],
                            in1=hps[1 - np_ % 2][:, j * T:(j + 1) * T],
                            op=ALU.add)

                def emit_prod(np_, j):
                    ph = (np_ * NJ + j) % 3
                    pq = np_ % 2
                    hq3 = hq[ph][:, :].rearrange("p (k t) -> p k t", k=2)

                    @step("gpsimd", r=(f"hq{ph}", f"crep{pq}"),
                          w=(f"hq{ph}", f"hps{np_ % 2}", f"hst_{np_}_{j}"))
                    def _(e, np_=np_, j=j, ph=ph, pq=pq, hq3=hq3):
                        base = j * NST + np_ * 2
                        e.tensor_copy(out=hst[:, base:base + 2],
                                      in_=hq3[:, :, T - 1:T])
                        e.tensor_tensor(out=hq[ph][:, :],
                                        in0=hq[ph][:, :],
                                        in1=crep[pq][:, :],
                                        op=ALU.mult)
                        return e.tensor_tensor(
                            out=hps[np_ % 2][:, j * T:(j + 1) * T],
                            in0=hq[ph][:, 0:T], in1=hq[ph][:, T:2 * T],
                            op=ALU.add)

                emit_bc_load(0, p)
                if c == 0:
                    emit_fa_wload(0)
                    emit_fa_wload(1)
                for np_ in range(NP):
                    odd = np_ % 2 == 1
                    if odd:
                        tcs, sp0 = tc_cur, (c % 4) * 8 + (np_ // 2) * 2
                        units = [(sp0, 0), (sp0, 1), (sp0 + 1, 0),
                                 (sp0 + 1, 1)]
                        if np_ < 7:
                            nx0 = (c % 4) * 8 + (np_ // 2 + 1) * 2
                        elif c + 1 < NCH:
                            nx0 = ((c + 1) % 4) * 8
                        else:
                            nx0 = None
                        emit_fa_mm(*units[0])
                        emit_fa_mm(*units[1])
                        if nx0 is not None:
                            emit_fa_wload(nx0)

                    emit_aexp(np_, 0)
                    emit_aexp(np_, 1)
                    if np_ + 1 < NP:
                        emit_bc_load(np_ + 1, p)

                    for j in range(NJ):
                        emit_uqscan(np_, j)
                        if j + 2 < NJ:
                            emit_aexp(np_, j + 2)
                        emit_ysum(np_, j)
                        emit_prod(np_, j)
                        if odd and j >= 1:
                            u = j - 1
                            emit_fa_silu(*units[u])
                            emit_fa_evac(tcs, *units[u])
                            if j <= 2:
                                emit_fa_mm(*units[j + 1])
                    if odd:
                        emit_fa_silu(*units[3])
                        emit_fa_evac(tcs, *units[3])
                        if nx0 is not None:
                            emit_fa_wload(nx0 + 1)

                @step("vector", r=("hps1", "yb"), w=("yb",))
                def _(e):
                    r = None
                    for j in range(NJ):
                        r = e.tensor_tensor(
                            out=yb[:, j * T:(j + 1) * T],
                            in0=yb[:, j * T:(j + 1) * T],
                            in1=hps[1][:, j * T:(j + 1) * T], op=ALU.add)
                    return r

                @step("vector", r=("xnfT", "c_pack", "yb"), w=("yb",))
                def _(e):
                    r = None
                    for j in range(NJ):
                        r = e.scalar_tensor_tensor(
                            out=yb[:, j * T:(j + 1) * T],
                            in0=xnfT[:, j * T:(j + 1) * T],
                            scalar=c_pack[:, DPAR_ + j:DPAR_ + j + 1],
                            in1=yb[:, j * T:(j + 1) * T],
                            op0=ALU.mult, op1=ALU.add)
                    return r

                for g in range(G):
                    @step("tensor", r=("yb", "c_idn"), w=("psT",))
                    def _(e, g=g):
                        r = None
                        for jj in range(NJ):
                            r = e.transpose(
                                psT[:, jj * 128:(jj + 1) * 128],
                                yb[:, jj * T + g * 128:jj * T + (g + 1) * 128],
                                c_idn[:, :])
                        return r

                    @step("vector", r=("psT",), w=("xnsT",))
                    def _(e, g=g):
                        return e.tensor_copy(
                            out=xnsT[:, g * DS:(g + 1) * DS], in_=psT[:, :])

                @step("gpsimd", r=("xnsT",), w=("s_d",), n_dma=1)
                def _(e, c=c):
                    return e.dma_start(
                        out=s_d.ap()[c * T:(c + 1) * T, :]
                        .rearrange("(g p) d -> p g d", p=128),
                        in_=xnsT[:, :].rearrange("p (g d) -> p g d", g=G))

                if c % 4 == 3:
                    if PIECED_GATHER:
                        @step("gpsimd", r=("s_d",), w=(), cc_inc=True)
                        def _(e, j=c // 4):
                            return e.collective_compute(
                                "AllGather", ALU.bypass,
                                replica_groups=[[0, 1, 2, 3], [4, 5, 6, 7]],
                                ins=[s_d.ap()[j * 2048:(j + 1) * 2048, :]
                                     .opt()],
                                outs=[s_g.ap()[j].opt()])

                    if c + 1 < NCH:
                        emit_fa_ln(c // 4 + 1)

            if not PIECED_GATHER:
                @step("gpsimd", r=("s_d",), w=(), cc_inc=True)
                def _(e):
                    return e.collective_compute(
                        "AllGather", ALU.bypass,
                        replica_groups=[[0, 1, 2, 3], [4, 5, 6, 7]],
                        ins=[s_d.ap().opt()],
                        outs=[s_g.ap().rearrange("w t d -> (w t) d").opt()])

            barrier()

        with ExitStack() as fb:
            fsb2 = lambda n, s, d=F32: fb.enter_context(
                nc.sbuf_tensor(n, s, d))
            fusesb = fsb2("fusesb", [128, NIT * T], BF16)
            wdb = [fsb2(f"wdb{i}", [128, NIT * 128], BF16) for i in range(2)]
            gbuf = fsb2("gbuf", [128, NDT * T])
            gTt = fsb2("gTt", [128, D])
            ssel = fsb2("ssel", [128, 16 * DS])
            sacc = fsb2("sacc", [128, D])
            xt = fsb2("xt", [128, D])
            psC = fb.enter_context(nc.psum_tensor("psC", [128, 512], F32))
            psT2 = fb.enter_context(nc.psum_tensor("psT2", [128, 512], F32))

            fusesb3 = fusesb[:, :].rearrange("p (it t) -> p it t", it=NIT)
            gbuf3 = gbuf[:, :].rearrange("p (j t) -> p j t", j=NDT)
            ssel4 = ssel[:, :].rearrange("p (r q d) -> p r q d", r=4, q=4)
            if PIECED_GATHER:
                sgv = s_g.ap().rearrange(
                    "(q tw) (r c gg p) d -> p r q tw c gg d",
                    q=4, tw=2, r=4, c=4, gg=G, p=128)
            else:
                sgv = s_g.ap().rearrange("w t d -> (w t) d").rearrange(
                    "(r q c gg p) d -> p r q c gg d",
                    r=4, q=4, c=FTC, gg=G, p=128)

            for tc in range(FTC):
                @step("sync", r=("fuse_d",), w=("fusesb",), n_dma=1)
                def _(e, tc=tc):
                    return e.dma_start(
                        out=fusesb3,
                        in_=fuse_d.ap()[:, tc * T:(tc + 1) * T]
                        .rearrange("(it p) t -> p it t", p=128))

                for dt in range(NDT):
                    pw = dt % 2
                    WDB = f"wdb{pw}"
                    wdb3 = wdb[pw][:, :].rearrange("p (it d) -> p it d",
                                                   it=NIT)

                    @step("gpsimd", r=(), w=(WDB,), n_dma=1)
                    def _(e, dt=dt, pw=pw):
                        return e.dma_start(out=wdb[pw][:, :],
                                           in_=wdp[dt, :, :])

                    @step("tensor", r=(WDB, "fusesb"), w=("psC",))
                    def _(e, dt=dt, wdb3=wdb3):
                        r = None
                        for it in range(NIT):
                            r = e.matmul(psC[:, :], wdb3[:, it, :],
                                         fusesb3[:, it, :],
                                         start=(it == 0),
                                         stop=(it == NIT - 1))
                        return r

                    @step("scalar", r=("psC", "c_pack"), w=("gbuf",))
                    def _(e, dt=dt):
                        col = SD_ + dt
                        return e.activation(out=gbuf3[:, dt, :],
                                            in_=psC[:, :], func=AF.Silu,
                                            scale=c_pack[:, col:col + 1])

                for g in range(G):
                    if PIECED_GATHER:
                        @step("sync", r=(), w=("ssel", "xt"), n_dma=5,
                              cc_wait=7 + tc // 4)
                        def _(e, tc=tc, g=g):
                            out = [
                                e.dma_start(out=ssel4[:, :, q, :],
                                            in_=sgv[:, :, q, tc // 4,
                                                    tc % 4, g, :])
                                for q in range(4)
                            ]
                            out.append(
                                e.dma_start(out=xt[:, :],
                                            in_=xq[tc * T + g * 128:
                                                   tc * T + (g + 1) * 128,
                                                   :]))
                            return out
                    else:
                        @step("sync", r=(), w=("ssel", "xt"), n_dma=2,
                              cc_wait=1)
                        def _(e, tc=tc, g=g):
                            return [
                                e.dma_start(out=ssel4,
                                            in_=sgv[:, :, :, tc, g, :]),
                                e.dma_start(out=xt[:, :],
                                            in_=xq[tc * T + g * 128:
                                                   tc * T + (g + 1) * 128,
                                                   :]),
                            ]

                    for rr in range(4):
                        @step("tensor", r=("gbuf", "c_idn"), w=("psT2",))
                        def _(e, g=g, rr=rr):
                            r = None
                            for u in range(4):
                                dt = rr * 4 + u
                                r = e.transpose(
                                    psT2[:, u * 128:(u + 1) * 128],
                                    gbuf[:, dt * T + g * 128:
                                         dt * T + (g + 1) * 128],
                                    c_idn[:, :])
                            return r

                        @step("vector", r=("psT2",), w=("gTt",))
                        def _(e, rr=rr):
                            return e.tensor_copy(
                                out=gTt[:, rr * 512:(rr + 1) * 512],
                                in_=psT2[:, :])

                    @step("vector", r=("ssel", "c_pack"), w=("sacc",))
                    def _(e):
                        r = None
                        for r4 in range(4):
                            e.tensor_single_scalar(
                                sacc[:, r4 * DS:(r4 + 1) * DS],
                                ssel4[:, r4, 0, :],
                                c_pack[:, MASK_:MASK_ + 1], ALU.mult)
                            for Q in range(1, 4):
                                r = e.scalar_tensor_tensor(
                                    out=sacc[:, r4 * DS:(r4 + 1) * DS],
                                    in0=ssel4[:, r4, Q, :],
                                    scalar=c_pack[:, MASK_ + Q:MASK_ + Q + 1],
                                    in1=sacc[:, r4 * DS:(r4 + 1) * DS],
                                    op0=ALU.mult, op1=ALU.add)
                        return r

                    @step("vector", r=("gTt", "xt", "sacc"), w=("sacc",))
                    def _(e):
                        e.tensor_tensor(out=sacc[:, :], in0=sacc[:, :],
                                        in1=gTt[:, :], op=ALU.mult)
                        return e.tensor_tensor(out=sacc[:, :], in0=sacc[:, :],
                                               in1=xt[:, :], op=ALU.add)

                    @step("gpsimd", r=("sacc",), w=(), n_dma=1)
                    def _(e, tc=tc, g=g):
                        return e.dma_start(
                            out=outq[tc * T + g * 128:tc * T + (g + 1) * 128,
                                     :],
                            in_=sacc[:, :])

            barrier()

        RELAXED = set() if False else {
            "aq0", "aq1", "hq0", "hq1", "hq2", "uq", "yb", "hps0", "hps1",
            "hst", "dt4", "dtx4", "xnfT", "xnsT", "xnbf", "xnTbf", "xnT",
            "gbx", "fusebf", "fusesb", "gbuf", "gTt", "sacc", "ssel", "xt",
            "brep0", "brep1", "crep0", "crep1",
            "wgb0", "wgb1", "wub0", "wub1", "wdb0", "wdb1",
        }
        engcnt = {e: 0 for e in ENGS}
        last_w = {}
        readers = {}
        stepidx = {e: 0 for e in ENGS}
        plans = []

        for s in steps:
            waits = {}

            def add(eng2, c2):
                if waits.get(eng2, 0) < c2:
                    waits[eng2] = c2

            if s.barrier:
                for e2 in ENGS:
                    if e2 != s.eng and engcnt[e2] > 0:
                        add(e2, engcnt[e2])
            else:
                for t in s.r:
                    if t in last_w:
                        e2, c2, si2, wdma = last_w[t]
                        if e2 != s.eng:
                            add(e2, c2)
                        elif wdma or (stepidx[s.eng] - si2 <= 2
                                      and t not in RELAXED):
                            add(e2, c2)
                    if t.startswith("ps"):
                        for e2, c2 in readers.get(t, {}).items():
                            if e2 != s.eng:
                                add(e2, c2)
                for t in s.w:
                    if t in last_w:
                        e2, c2, si2, wdma = last_w[t]
                        if e2 != s.eng:
                            add(e2, c2)
                        elif wdma:
                            add(e2, c2)
                    for e2, c2 in readers.get(t, {}).items():
                        if e2 != s.eng:
                            add(e2, c2)
                        elif s.dma:
                            add(e2, c2)

            plans.append(waits)
            engcnt[s.eng] += s.inc
            si = stepidx[s.eng]
            stepidx[s.eng] += 1
            if not s.barrier:
                for t in s.w:
                    last_w[t] = (s.eng, engcnt[s.eng], si, s.dma)
                    readers[t] = {}
                for t in s.r:
                    readers.setdefault(t, {})[s.eng] = engcnt[s.eng]
                    if t.startswith("ps") and t not in s.w:
                        pass

        totals = dict(engcnt)

        with nc.Block() as block:
            def make_body(engname):
                def body(eng):
                    watermark = {e2: 0 for e2 in ENGS}
                    for s, waits in zip(steps, plans):
                        if s.eng != engname:
                            continue
                        for e2 in ENGS:
                            c2 = waits.get(e2, 0)
                            if c2 > watermark[e2]:
                                eng.wait_ge(semE[e2], c2)
                                watermark[e2] = c2
                        if s.cc_wait:
                            eng.wait_ge(semC, s.cc_wait)
                        if s.barrier:
                            eng.nop().then_inc(semE[engname], 1)
                            continue
                        ret = s.fn(eng)
                        if s.cc_inc:
                            ret.then_inc(semC, 1)
                        elif s.dma:
                            instrs = ret if isinstance(ret, list) else [ret]
                            assert 16 * len(instrs) == s.inc
                            for ins_ in instrs:
                                ins_.then_inc(semE[engname], 16)
                        else:
                            ret.then_inc(semE[engname], 1)
                    for e2 in ENGS:
                        if totals[e2] > watermark[e2]:
                            eng.wait_ge(semE[e2], totals[e2])
                return body

            block.sync(make_body("sync"))
            block.gpsimd(make_body("gpsimd"))
            block.scalar(make_body("scalar"))
            block.vector(make_body("vector"))
            block.tensor(make_body("tensor"))

    return nc


def kernel(L0_attn_x, L0_attn_W_xproj, L0_attn_W_dt, L0_attn_b_dt,
           L0_attn_A_log, L0_attn_D_param, L0_ffn_w_gate, L0_ffn_w_up,
           L0_ffn_w_down, L0_ffn_s_gate, L0_ffn_s_up, L0_ffn_s_down):
    x = np.asarray(L0_attn_x, dtype=np.float32)
    W_xproj = np.asarray(L0_attn_W_xproj, dtype=np.float32)
    W_dt = np.asarray(L0_attn_W_dt, dtype=np.float32)
    b_dt = np.asarray(L0_attn_b_dt, dtype=np.float32)
    A_log = np.asarray(L0_attn_A_log, dtype=np.float32)
    D_param = np.asarray(L0_attn_D_param, dtype=np.float32)
    w_gate = np.asarray(L0_ffn_w_gate)
    w_up = np.asarray(L0_ffn_w_up)
    w_down = np.asarray(L0_ffn_w_down)
    s_gate = np.asarray(L0_ffn_s_gate, dtype=np.float32)
    s_up = np.asarray(L0_ffn_s_up, dtype=np.float32)
    s_down = np.asarray(L0_ffn_s_down, dtype=np.float32)

    nc = build_program()

    wxpT = np.ascontiguousarray(W_xproj.T)
    wgp = np.ascontiguousarray(
        w_gate.reshape(16, 128, 32, 256).transpose(2, 1, 0, 3)
    ).reshape(NSP, 128, NDT * 256)
    wup = np.ascontiguousarray(
        w_up.reshape(16, 128, 32, 256).transpose(2, 1, 0, 3)
    ).reshape(NSP, 128, NDT * 256)
    wdp = np.ascontiguousarray(
        w_down.reshape(64, 128, 16, 128).transpose(2, 1, 0, 3)
    ).reshape(16, 128, INTER)
    idn = np.eye(128, dtype=np.float32)
    sg_c = np.ascontiguousarray(s_gate.reshape(64, 128).T)
    su_c = np.ascontiguousarray(s_up.reshape(64, 128).T)
    sd_c = np.ascontiguousarray(s_down.reshape(16, 128).T)

    in_maps = []
    for k in range(8):
        b, r = k // 4, k % 4
        dlo = r * DS
        cp = np.zeros((128, CPW), dtype=np.float32)
        cp[:, EPS_] = EPS
        cp[:, ONE_] = 1.0
        cp[:, NEG1_] = -1.0
        cp[:, BDT_:BDT_ + 4] = b_dt[dlo:dlo + DS].reshape(4, 128).T
        cp[:, ALOG_:ALOG_ + 64] = (
            A_log[dlo:dlo + DS].reshape(4, 128, 16).transpose(1, 0, 2)
            .reshape(128, 64))
        cp[:, DPAR_:DPAR_ + 4] = D_param[dlo:dlo + DS].reshape(4, 128).T
        cp[:, SG_:SG_ + 64] = sg_c
        cp[:, SU_:SU_ + 64] = su_c
        cp[:, SD_:SD_ + 16] = sd_c
        cp[:, MASK_ + r] = 1.0
        in_maps.append(dict(
            xb=np.ascontiguousarray(x[b]),
            xbs=np.ascontiguousarray(x[b][:, dlo:dlo + DS]),
            xq=np.ascontiguousarray(x[b, r * TQ:(r + 1) * TQ]),
            wxpT=wxpT,
            wdtT=np.ascontiguousarray(W_dt[dlo:dlo + DS].T),
            cpack=cp,
            idn=idn,
            wgp=wgp,
            wup=wup,
            wdp=wdp,
        ))

    res = run_bass_kernel_spmd(nc, in_maps, core_ids=list(range(8)))

    out = np.empty((2, L, D), dtype=np.float32)
    for k in range(8):
        b, r = k // 4, k % 4
        out[b, r * TQ:(r + 1) * TQ] = np.asarray(res.results[k]["outq"])
    return out


# version 21
# speedup vs baseline: 1.5155x; 1.0377x over previous
import numpy as np
import concourse.bass as bass
import concourse.mybir as mybir
from concourse.bass_utils import run_bass_kernel_spmd
from contextlib import ExitStack

F32 = mybir.dt.float32
BF16 = mybir.dt.bfloat16
I8 = mybir.dt.int8
AF = mybir.ActivationFunctionType
ALU = mybir.AluOpType
AX = mybir.AxisListType

L = 16384
D = 2048
T = 512
NCH = L // T
G = 4
DS = 512
NJ = 4
NDT = 16
NST = 16
NP = 8
EPS = 1e-5
TQ = 4096
FTC = TQ // T
NSP = 32
NIT = 64
INTER = 8192

BDT_, ALOG_, DPAR_, SG_, SU_, SD_, MASK_ = 0, 4, 68, 72, 136, 200, 216
EPS_, ONE_, NEG1_ = 220, 221, 222
CPW = 224

ENGS = ("sync", "gpsimd", "scalar", "vector", "tensor")
USE_ACCUM_LN = True
PROD_ON_GPSIMD = True
PIECED_GATHER = False


class Step:
    __slots__ = ("eng", "fn", "inc", "dma", "r", "w", "cc_wait", "cc_inc",
                 "barrier")

    def __init__(self, eng, fn, inc, dma, r, w, cc_wait, cc_inc, barrier):
        self.eng, self.fn, self.inc, self.dma = eng, fn, inc, dma
        self.r, self.w = r, w
        self.cc_wait, self.cc_inc, self.barrier = cc_wait, cc_inc, barrier


def build_program():
    nc = bass.Bass("TRN2", target_bir_lowering=False, debug=False)

    xb = nc.dram_tensor("xb", [L, D], F32, kind="ExternalInput").ap()
    xbs = nc.dram_tensor("xbs", [L, DS], F32, kind="ExternalInput").ap()
    xq = nc.dram_tensor("xq", [TQ, D], F32, kind="ExternalInput").ap()
    wxpT = nc.dram_tensor("wxpT", [D, 96], F32, kind="ExternalInput").ap()
    wdtT = nc.dram_tensor("wdtT", [64, DS], F32, kind="ExternalInput").ap()
    cpack = nc.dram_tensor("cpack", [128, CPW], F32, kind="ExternalInput").ap()
    idn = nc.dram_tensor("idn", [128, 128], F32, kind="ExternalInput").ap()
    wgp = nc.dram_tensor("wgp", [NSP, 128, NDT * 256], I8,
                         kind="ExternalInput").ap()
    wup = nc.dram_tensor("wup", [NSP, 128, NDT * 256], I8,
                         kind="ExternalInput").ap()
    wdp = nc.dram_tensor("wdp", [16, 128, INTER], I8, kind="ExternalInput").ap()
    outq = nc.dram_tensor("outq", [TQ, D], F32, kind="ExternalOutput").ap()

    bcd = nc.dram_tensor("bcd", [2, 32, T], F32, kind="Internal")
    s_d = nc.dram_tensor("s_d", [L, DS], F32, kind="Internal")
    s_g = nc.dram_tensor("s_g", [4 * L, DS], F32, kind="Internal")
    fuse_d = nc.dram_tensor("fuse_d", [INTER, TQ], BF16, kind="Internal")

    steps = []

    def step(eng, r, w, n_dma=0, cc_wait=0, cc_inc=False):
        def deco(fn):
            inc = 0 if cc_inc else (16 * n_dma if n_dma else 1)
            steps.append(Step(eng, fn, inc, n_dma > 0, tuple(r), tuple(w),
                              cc_wait, cc_inc, False))
        return deco

    def barrier():
        for eng in ENGS:
            steps.append(Step(eng, None, 1, False, (), (), False, False, True))

    with ExitStack() as top:
        sbuf = lambda n, s, d=F32: top.enter_context(nc.sbuf_tensor(n, s, d))
        c_wxpT = sbuf("c_wxpT", [128, NDT * 96], BF16)
        c_wdtT = sbuf("c_wdtT", [64, DS], BF16)
        c_pack = sbuf("c_pack", [128, CPW])
        c_acol = sbuf("c_acol", [128, 64])
        c_idn = sbuf("c_idn", [128, 128])

        semE = {e: top.enter_context(nc.semaphore(f"sem_{e}")) for e in ENGS}
        semC = top.enter_context(nc.semaphore("semC"))

        with ExitStack() as pre:
            c_wxpT_f = pre.enter_context(
                nc.sbuf_tensor("c_wxpT_f", [128, NDT * 96], F32))
            c_wdtT_f = pre.enter_context(
                nc.sbuf_tensor("c_wdtT_f", [64, DS], F32))

            @step("sync", r=(), w=("c_wxpT_f", "c_wdtT_f", "c_pack", "c_idn"),
                  n_dma=4)
            def _(e):
                return [
                    e.dma_start(out=c_wxpT_f[:, :]
                                .rearrange("p (j e) -> p j e", j=NDT),
                                in_=wxpT.rearrange("(j p) e -> p j e", p=128)),
                    e.dma_start(out=c_wdtT_f[:, :], in_=wdtT),
                    e.dma_start(out=c_pack[:, :], in_=cpack),
                    e.dma_start(out=c_idn[:, :], in_=idn),
                ]

            @step("scalar", r=("c_wxpT_f", "c_wdtT_f", "c_pack"),
                  w=("c_wxpT", "c_wdtT", "c_acol"))
            def _(e):
                e.activation(out=c_wxpT[:, :], in_=c_wxpT_f[:, :], func=AF.Copy)
                e.activation(out=c_wdtT[:, :], in_=c_wdtT_f[:, :], func=AF.Copy)
                return e.activation(out=c_acol[:, :],
                                    in_=c_pack[:, ALOG_:ALOG_ + 64],
                                    func=AF.Exp)

            barrier()

        with ExitStack() as ms:
            msb = lambda n, s, d=F32: ms.enter_context(nc.sbuf_tensor(n, s, d))
            psT = ms.enter_context(nc.psum_tensor("psT", [128, 512], F32))
            psE = ms.enter_context(nc.psum_tensor("psE", [96, 512], F32))
            psBm = ms.enter_context(nc.psum_tensor("psBm", [128, 512], F32))
            psG = [ms.enter_context(nc.psum_tensor(f"psG{i}", [128, 512], F32))
                   for i in range(2)]
            psU = [ms.enter_context(nc.psum_tensor(f"psU{i}", [128, 512], F32))
                   for i in range(2)]
            xchH = [msb(f"xchH{i}", [128, 2 * D]) for i in range(2)]
            xbschH = [msb("xbschH", [128, 2 * DS])] * 2
            stH = [msb(f"stH{i}", [128, 16]) for i in range(2)]
            xnbf = msb("xnbf", [128, 2 * D], BF16)
            xnTbf = msb("xnTbf", [128, NDT * 256], BF16)
            xnsT = msb("xnsT", [128, G * DS])
            xnfT = msb("xnfT", [128, NJ * T])
            dtrawbf = msb("dtrawbf", [64, T], BF16)
            dt4 = msb("dt4", [128, NJ * T])
            dtx4 = msb("dtx4", [128, NJ * T])
            brep = [msb(f"brep{i}", [128, 2 * T]) for i in range(2)]
            crep = [msb(f"crep{i}", [128, 2 * T]) for i in range(2)]
            aq = [msb(f"aq{i}", [128, 2 * T]) for i in range(2)]
            uq = msb("uq", [128, T])
            hq = [msb(f"hq{i}", [128, 2 * T]) for i in range(3)]
            hps = [msb(f"hps{i}", [128, NJ * T]) for i in range(2)]
            yb = msb("yb", [128, NJ * T])
            hst = msb("hst", [128, 64])
            zb = aq[0][:, 0:T]
            ebuf = aq[0][:, T:2 * T]
            bcsb = msb("bcsb", [32, T])
            xnT = msb("xnT", [128, NDT * T], BF16)
            scr = None if USE_ACCUM_LN else msb("scr", [128, D])
            wgb = [msb(f"wgb{i}", [128, NDT * 256], BF16) for i in range(2)]
            wub = [msb(f"wub{i}", [128, NDT * 256], BF16) for i in range(2)]
            gbx = msb("gbx", [128, 512])
            fusebf = [msb(f"fusebf{i}", [128, 512], BF16) for i in range(2)]

            xnsT3 = xnsT[:, :].rearrange("p (g d) -> p g d", g=G)
            xnT4m = xnTbf[:, :].rearrange("p (j t) -> p j t", j=NDT)
            fxnT4 = xnT[:, :].rearrange("p (j gg t) -> p j gg t", j=NDT, gg=G)

            @step("vector", r=("c_acol",),
                  w=("c_acol",) + tuple(f"hst_{n}_{j}" for n in range(NP)
                                        for j in range(NJ)))
            def _(e):
                e.tensor_scalar_mul(c_acol[:, :], c_acol[:, :], -1.0)
                return e.memset(hst[:, :], 0.0)

            def emit_ln(XT, STT_, XNB, xch3, st_, xnb3, src_rows, extra=None):
                @step("sync", r=(), w=(XT,) + ((extra[0],) if extra else ()),
                      n_dma=2 if extra else 1)
                def _(e):
                    out = [e.dma_start(out=xch3,
                                       in_=src_rows
                                       .rearrange("(g p) d -> p g d", p=128))]
                    if extra:
                        out.append(e.dma_start(
                            out=extra[2],
                            in_=extra[1].rearrange("(g p) d -> p g d", p=128)))
                    return out

                for gg in range(2):
                    if USE_ACCUM_LN:
                        @step("scalar", r=(XT,), w=(XNB, STT_))
                        def _(e, gg=gg):
                            return e.activation(out=xnb3[:, gg, :],
                                                in_=xch3[:, gg, :],
                                                func=AF.Square,
                                                accum_out=st_[:, 2 + gg:3 + gg])

                        @step("vector", r=(XT,), w=(STT_,))
                        def _(e, gg=gg):
                            return e.reduce_sum(st_[:, gg:gg + 1],
                                                xch3[:, gg, :], axis=AX.X)
                    else:
                        @step("scalar", r=(XT,), w=("scr",))
                        def _(e, gg=gg):
                            return e.activation(out=scr[:, :],
                                                in_=xch3[:, gg, :],
                                                func=AF.Square)

                        @step("vector", r=(XT, "scr"), w=(STT_,))
                        def _(e, gg=gg):
                            e.reduce_sum(st_[:, gg:gg + 1], xch3[:, gg, :],
                                         axis=AX.X)
                            return e.reduce_sum(st_[:, 2 + gg:3 + gg],
                                                scr[:, :], axis=AX.X)

                @step("vector", r=(STT_,), w=(STT_,))
                def _(e):
                    e.tensor_scalar_mul(st_[:, 4:6], st_[:, 0:2], 1.0 / D)
                    return e.tensor_scalar_mul(st_[:, 6:8], st_[:, 2:4],
                                               1.0 / D)

                @step("vector", r=(STT_,), w=(STT_,))
                def _(e):
                    return e.tensor_tensor(out=st_[:, 8:10], in0=st_[:, 4:6],
                                           in1=st_[:, 4:6], op=ALU.mult)

                @step("vector", r=(STT_,), w=(STT_,))
                def _(e):
                    return e.tensor_tensor(out=st_[:, 6:8], in0=st_[:, 6:8],
                                           in1=st_[:, 8:10], op=ALU.subtract)

                @step("scalar", r=(STT_, "c_pack"), w=(STT_,))
                def _(e):
                    return e.activation(out=st_[:, 8:10], in_=st_[:, 6:8],
                                        func=AF.Sqrt,
                                        bias=c_pack[:, EPS_:EPS_ + 1])

                @step("vector", r=(STT_,), w=(STT_,))
                def _(e):
                    return e.reciprocal(st_[:, 10:12], st_[:, 8:10])

                @step("vector", r=(STT_,), w=(STT_,))
                def _(e):
                    return e.tensor_tensor(out=st_[:, 12:14], in0=st_[:, 4:6],
                                           in1=st_[:, 10:12], op=ALU.mult)

                @step("vector", r=(STT_,), w=(STT_,))
                def _(e):
                    return e.tensor_scalar_mul(st_[:, 12:14], st_[:, 12:14],
                                               -1.0)

            def emit_fa_ln(tc):
                for hh in range(2):
                    lo = tc * T + hh * 256
                    XT, STT_ = f"xchH{hh}", f"stH{hh}"
                    fxch3 = xchH[hh][:, :].rearrange("p (g d) -> p g d", g=2)
                    fxnbf3 = xnbf[:, :].rearrange("p (g d) -> p g d", g=2)
                    emit_ln(XT, STT_, "xnbf", fxch3, stH[hh],
                            fxnbf3, xq[lo:lo + 256, :])

                    @step("scalar", r=(XT, STT_), w=("xnbf",))
                    def _(e, hh=hh, fxch3=fxch3, fxnbf3=fxnbf3):
                        r = None
                        for gg in range(2):
                            r = e.activation(out=fxnbf3[:, gg, :],
                                             in_=fxch3[:, gg, :],
                                             func=AF.Identity,
                                             scale=stH[hh][:, 10 + gg:11 + gg],
                                             bias=stH[hh][:, 12 + gg:13 + gg])
                        return r

                    @step("sync", r=("xnbf",), w=("xnT",), n_dma=2)
                    def _(e, hh=hh, fxnbf3=fxnbf3):
                        return [
                            e.dma_start(out=fxnT4[:, :, 2 * hh + gg, :],
                                        in_=fxnbf3[:, gg, :]
                                        .rearrange("t (j p) -> t j p", j=NDT),
                                        transpose=True)
                            for gg in range(2)
                        ]

            def emit_fa_wload(sp):
                pw = sp % 2

                @step("gpsimd", r=(), w=(f"wgb{pw}", f"wub{pw}"), n_dma=2)
                def _(e, sp=sp, pw=pw):
                    return [
                        e.dma_start(out=wgb[pw][:, :], in_=wgp[sp, :, :]),
                        e.dma_start(out=wub[pw][:, :], in_=wup[sp, :, :]),
                    ]

            def emit_fa_mm(sp, it):
                pw, pg = sp % 2, it % 2
                wgb3 = wgb[pw][:, :].rearrange("p (j i) -> p j i", j=NDT)
                wub3 = wub[pw][:, :].rearrange("p (j i) -> p j i", j=NDT)

                @step("tensor", r=(f"wgb{pw}", f"wub{pw}", "xnT"),
                      w=(f"psG{pg}", f"psU{pg}"))
                def _(e, it=it, pg=pg, wgb3=wgb3, wub3=wub3):
                    r = None
                    for j in range(NDT):
                        e.matmul(psG[pg][:, :],
                                 wgb3[:, j, it * 128:(it + 1) * 128],
                                 xnT[:, j * T:(j + 1) * T],
                                 start=(j == 0), stop=(j == NDT - 1))
                    for j in range(NDT):
                        r = e.matmul(
                            psU[pg][:, :],
                            wub3[:, j, it * 128:(it + 1) * 128],
                            xnT[:, j * T:(j + 1) * T],
                            start=(j == 0), stop=(j == NDT - 1))
                    return r

            def emit_fa_silu(sp, it):
                pg = it % 2

                @step("scalar", r=(f"psG{pg}", "c_pack"), w=("gbx",))
                def _(e, sp=sp, it=it, pg=pg):
                    col = SG_ + sp * 2 + it
                    return e.activation(out=gbx[:, :],
                                        in_=psG[pg][:, :],
                                        func=AF.Silu,
                                        scale=c_pack[:, col:col + 1])

            def emit_fa_evac(tc, sp, it):
                pg, pf = it % 2, it % 2
                FB = f"fusebf{pf}"

                @step("vector", r=(f"psU{pg}", "gbx", "c_pack"), w=(FB,))
                def _(e, sp=sp, it=it, pg=pg, pf=pf):
                    col = SU_ + sp * 2 + it
                    return e.scalar_tensor_tensor(
                        out=fusebf[pf][:, :], in0=psU[pg][:, :],
                        scalar=c_pack[:, col:col + 1],
                        in1=gbx[:, :], op0=ALU.mult, op1=ALU.mult)

                @step("sync", r=(FB,), w=("fuse_d",), n_dma=1)
                def _(e, sp=sp, it=it, tc=tc, pf=pf):
                    row = (sp * 2 + it) * 128
                    return e.dma_start(
                        out=fuse_d.ap()[row:row + 128,
                                        tc * T:(tc + 1) * T],
                        in_=fusebf[pf][:, :])

            emit_fa_ln(0)
            for c in range(NCH):
                p = c % 2
                BCD = f"bcd{p}"
                tc_cur = c // 4

                for hh in range(2):
                    XT, XBT, STT_ = f"xchH{hh}", "xbschH", f"stH{hh}"
                    xch3 = xchH[hh][:, :].rearrange("p (g d) -> p g d", g=2)
                    xbs3 = xbschH[hh][:, :].rearrange("p (g d) -> p g d", g=2)
                    xnb3 = xnbf[:, :].rearrange("p (g d) -> p g d", g=2)
                    lo = c * T + hh * 256

                    emit_ln(XT, STT_, "xnbf", xch3, stH[hh], xnb3,
                            xb[lo:lo + 256, :],
                            extra=(XBT, xbs[lo:lo + 256, :], xbs3))

                    @step("scalar", r=(XT, XBT, STT_), w=("xnbf", "xnsT"))
                    def _(e, hh=hh, xch3=xch3, xbs3=xbs3, xnb3=xnb3):
                        r = None
                        for gg in range(2):
                            g = 2 * hh + gg
                            e.activation(out=xnb3[:, gg, :], in_=xch3[:, gg, :],
                                         func=AF.Identity,
                                         scale=stH[hh][:, 10 + gg:11 + gg],
                                         bias=stH[hh][:, 12 + gg:13 + gg])
                            r = e.activation(out=xnsT3[:, g, :],
                                             in_=xbs3[:, gg, :],
                                             func=AF.Identity,
                                             scale=stH[hh][:, 10 + gg:11 + gg],
                                             bias=stH[hh][:, 12 + gg:13 + gg])
                        return r

                    @step("sync", r=("xnbf",), w=("xnTbf",), n_dma=2)
                    def _(e, xnb3=xnb3):
                        return [
                            e.dma_start(out=xnT4m[:, :, gg * 128:(gg + 1) * 128],
                                        in_=xnb3[:, gg, :]
                                        .rearrange("t (j p) -> t j p", j=NDT),
                                        transpose=True)
                            for gg in range(2)
                        ]

                    @step("tensor", r=("xnTbf", "c_wxpT"), w=("psE",))
                    def _(e):
                        r = None
                        for j2 in range(NDT):
                            r = e.matmul(psE[:, 0:256],
                                         c_wxpT[:, j2 * 96:(j2 + 1) * 96],
                                         xnT4m[:, j2, :],
                                         start=(j2 == 0), stop=(j2 == NDT - 1))
                        return r

                    @step("scalar", r=("psE",), w=("dtrawbf",))
                    def _(e, hh=hh):
                        return e.activation(
                            out=dtrawbf[:, hh * 256:(hh + 1) * 256],
                            in_=psE[0:64, 0:256], func=AF.Copy)

                    @step("vector", r=("psE",), w=("bcsb",))
                    def _(e, hh=hh):
                        return e.tensor_copy(
                            out=bcsb[:, hh * 256:(hh + 1) * 256],
                            in_=psE[64:96, 0:256])

                @step("sync", r=("bcsb",), w=(BCD,), n_dma=1)
                def _(e, p=p):
                    return e.dma_start(out=bcd.ap()[p], in_=bcsb[:, :])

                for j in range(NJ):
                    @step("tensor", r=("xnsT", "c_idn"), w=("psT",))
                    def _(e, j=j):
                        r = None
                        for g in range(G):
                            r = e.transpose(psT[:, g * 128:(g + 1) * 128],
                                            xnsT3[:, g, j * 128:(j + 1) * 128],
                                            c_idn[:, :])
                        return r

                    @step("vector", r=("psT",), w=("xnfT",))
                    def _(e, j=j):
                        return e.tensor_copy(out=xnfT[:, j * T:(j + 1) * T],
                                             in_=psT[:, :])

                for j in range(NJ):
                    @step("tensor", r=("dtrawbf", "c_wdtT"), w=("psBm",))
                    def _(e, j=j):
                        return e.matmul(psBm[:, :],
                                        c_wdtT[:, j * 128:(j + 1) * 128],
                                        dtrawbf[:, :], start=True, stop=True)

                    @step("scalar", r=("psBm", "c_pack"), w=("aq0",))
                    def _(e, j=j):
                        return e.activation(
                            out=zb, in_=psBm[:, :], func=AF.Identity,
                            bias=c_pack[:, BDT_ + j:BDT_ + j + 1])

                    @step("scalar", r=("aq0", "c_pack"), w=("aq0",))
                    def _(e):
                        e.activation(out=ebuf, in_=zb, func=AF.Abs)
                        e.activation(out=ebuf, in_=ebuf, func=AF.Exp,
                                     scale=c_pack[:, NEG1_:NEG1_ + 1])
                        return e.activation(out=ebuf, in_=ebuf, func=AF.Ln,
                                            bias=c_pack[:, ONE_:ONE_ + 1])

                    @step("vector", r=("aq0", "xnfT"), w=("dt4", "dtx4"))
                    def _(e, j=j):
                        e.scalar_tensor_tensor(
                            out=dt4[:, j * T:(j + 1) * T], in0=zb,
                            scalar=0.0, in1=ebuf, op0=ALU.max, op1=ALU.add)
                        return e.tensor_tensor(
                            out=dtx4[:, j * T:(j + 1) * T],
                            in0=dt4[:, j * T:(j + 1) * T],
                            in1=xnfT[:, j * T:(j + 1) * T], op=ALU.mult)

                @step("vector", r=(), w=("yb",))
                def _(e):
                    return e.memset(yb[:, :], 0.0)

                def emit_bc_load(np_, p):
                    pq = np_ % 2

                    @step("scalar", r=(BCD,), w=(f"brep{pq}", f"crep{pq}"),
                          n_dma=2)
                    def _(e, np_=np_, p=p, pq=pq):
                        return [
                            e.dma_start(
                                out=brep[pq][:, :]
                                .rearrange("p (k t) -> p k t", k=2),
                                in_=bcd.ap()[p, np_ * 2:(np_ + 1) * 2, :]
                                .unsqueeze(0).broadcast_to([128, 2, T])),
                            e.dma_start(
                                out=crep[pq][:, :]
                                .rearrange("p (k t) -> p k t", k=2),
                                in_=bcd.ap()[p, 16 + np_ * 2:18 + np_ * 2, :]
                                .unsqueeze(0).broadcast_to([128, 2, T])),
                        ]

                def emit_aexp(np_, j):
                    pa = (np_ * NJ + j) % 2
                    aq3 = aq[pa][:, :].rearrange("p (k t) -> p k t", k=2)

                    @step("scalar", r=("dt4", "c_acol"), w=(f"aq{pa}",))
                    def _(e, np_=np_, j=j, aq3=aq3):
                        r = None
                        for k in range(2):
                            idx = j * NST + np_ * 2 + k
                            r = e.activation(
                                out=aq3[:, k, :],
                                in_=dt4[:, j * T:(j + 1) * T],
                                func=AF.Exp,
                                scale=c_acol[:, idx:idx + 1])
                        return r

                def emit_uqscan(np_, j):
                    pa = (np_ * NJ + j) % 2
                    ph = (np_ * NJ + j) % 3
                    pq = np_ % 2
                    aq3 = aq[pa][:, :].rearrange("p (k t) -> p k t", k=2)
                    hq3 = hq[ph][:, :].rearrange("p (k t) -> p k t", k=2)

                    @step("vector", r=("dtx4", f"brep{pq}", f"aq{pa}",
                                       f"hst_{np_}_{j}"),
                          w=("uq", f"hq{ph}"))
                    def _(e, np_=np_, j=j, pq=pq, aq3=aq3, hq3=hq3):
                        r = None
                        for k in range(2):
                            idx = j * NST + np_ * 2 + k
                            e.tensor_tensor(
                                out=uq[:, :],
                                in0=dtx4[:, j * T:(j + 1) * T],
                                in1=brep[pq][:, k * T:(k + 1) * T],
                                op=ALU.mult)
                            r = e.tensor_tensor_scan(
                                out=hq3[:, k, :], data0=aq3[:, k, :],
                                data1=uq[:, :],
                                initial=hst[:, idx:idx + 1],
                                op0=ALU.mult, op1=ALU.add)
                        return r

                def emit_ysum(np_, j):
                    if np_ == 0:
                        return

                    @step("vector", r=(f"hps{1 - np_ % 2}", "yb"), w=("yb",))
                    def _(e, np_=np_, j=j):
                        return e.tensor_tensor(
                            out=yb[:, j * T:(j + 1) * T],
                            in0=yb[:, j * T:(j + 1) * T],
                            in1=hps[1 - np_ % 2][:, j * T:(j + 1) * T],
                            op=ALU.add)

                def emit_prod(np_, j):
                    ph = (np_ * NJ + j) % 3
                    pq = np_ % 2
                    hq3 = hq[ph][:, :].rearrange("p (k t) -> p k t", k=2)

                    @step("gpsimd", r=(f"hq{ph}", f"crep{pq}"),
                          w=(f"hq{ph}", f"hps{np_ % 2}", f"hst_{np_}_{j}"))
                    def _(e, np_=np_, j=j, ph=ph, pq=pq, hq3=hq3):
                        base = j * NST + np_ * 2
                        e.tensor_copy(out=hst[:, base:base + 2],
                                      in_=hq3[:, :, T - 1:T])
                        e.tensor_tensor(out=hq[ph][:, :],
                                        in0=hq[ph][:, :],
                                        in1=crep[pq][:, :],
                                        op=ALU.mult)
                        return e.tensor_tensor(
                            out=hps[np_ % 2][:, j * T:(j + 1) * T],
                            in0=hq[ph][:, 0:T], in1=hq[ph][:, T:2 * T],
                            op=ALU.add)

                emit_bc_load(0, p)
                if c == 0:
                    emit_fa_wload(0)
                    emit_fa_wload(1)
                for np_ in range(NP):
                    odd = np_ % 2 == 1
                    if odd:
                        tcs, sp0 = tc_cur, (c % 4) * 8 + (np_ // 2) * 2
                        units = [(sp0, 0), (sp0, 1), (sp0 + 1, 0),
                                 (sp0 + 1, 1)]
                        if np_ < 7:
                            nx0 = (c % 4) * 8 + (np_ // 2 + 1) * 2
                        elif c + 1 < NCH:
                            nx0 = ((c + 1) % 4) * 8
                        else:
                            nx0 = None
                        emit_fa_mm(*units[0])
                        emit_fa_mm(*units[1])
                        if nx0 is not None:
                            emit_fa_wload(nx0)

                    emit_aexp(np_, 0)
                    emit_aexp(np_, 1)
                    if np_ + 1 < NP:
                        emit_bc_load(np_ + 1, p)

                    for j in range(NJ):
                        emit_uqscan(np_, j)
                        if j + 2 < NJ:
                            emit_aexp(np_, j + 2)
                        emit_ysum(np_, j)
                        emit_prod(np_, j)
                        if odd and j >= 1:
                            u = j - 1
                            emit_fa_silu(*units[u])
                            emit_fa_evac(tcs, *units[u])
                            if j <= 2:
                                emit_fa_mm(*units[j + 1])
                    if odd:
                        emit_fa_silu(*units[3])
                        emit_fa_evac(tcs, *units[3])
                        if nx0 is not None:
                            emit_fa_wload(nx0 + 1)

                @step("vector", r=("hps1", "yb"), w=("yb",))
                def _(e):
                    r = None
                    for j in range(NJ):
                        r = e.tensor_tensor(
                            out=yb[:, j * T:(j + 1) * T],
                            in0=yb[:, j * T:(j + 1) * T],
                            in1=hps[1][:, j * T:(j + 1) * T], op=ALU.add)
                    return r

                @step("vector", r=("xnfT", "c_pack", "yb"), w=("yb",))
                def _(e):
                    r = None
                    for j in range(NJ):
                        r = e.scalar_tensor_tensor(
                            out=yb[:, j * T:(j + 1) * T],
                            in0=xnfT[:, j * T:(j + 1) * T],
                            scalar=c_pack[:, DPAR_ + j:DPAR_ + j + 1],
                            in1=yb[:, j * T:(j + 1) * T],
                            op0=ALU.mult, op1=ALU.add)
                    return r

                for g in range(G):
                    @step("tensor", r=("yb", "c_idn"), w=("psT",))
                    def _(e, g=g):
                        r = None
                        for jj in range(NJ):
                            r = e.transpose(
                                psT[:, jj * 128:(jj + 1) * 128],
                                yb[:, jj * T + g * 128:jj * T + (g + 1) * 128],
                                c_idn[:, :])
                        return r

                    @step("vector", r=("psT",), w=("xnsT",))
                    def _(e, g=g):
                        return e.tensor_copy(
                            out=xnsT[:, g * DS:(g + 1) * DS], in_=psT[:, :])

                @step("gpsimd", r=("xnsT",), w=("s_d",), n_dma=1)
                def _(e, c=c):
                    return e.dma_start(
                        out=s_d.ap()[c * T:(c + 1) * T, :]
                        .rearrange("(g p) d -> p g d", p=128),
                        in_=xnsT[:, :].rearrange("p (g d) -> p g d", g=G))

                if c % 4 == 3 and c + 1 < NCH:
                    emit_fa_ln(c // 4 + 1)

            @step("gpsimd", r=("s_d",), w=(), cc_inc=True)
            def _(e):
                return e.collective_compute(
                    "AllGather", ALU.bypass,
                    replica_groups=[[0, 1, 2, 3], [4, 5, 6, 7]],
                    ins=[s_d.ap().opt()], outs=[s_g.ap().opt()])

            barrier()

        with ExitStack() as fb:
            fsb2 = lambda n, s, d=F32: fb.enter_context(
                nc.sbuf_tensor(n, s, d))
            fusesb = fsb2("fusesb", [128, NIT * T], BF16)
            wdb = [fsb2(f"wdb{i}", [128, NIT * 64], BF16) for i in range(2)]
            gbuf = fsb2("gbuf", [128, NDT * T])
            gTt = fsb2("gTt", [128, D])
            ssel = [fsb2(f"ssel{i}", [128, 16 * DS]) for i in range(2)]
            sacc = fsb2("sacc", [128, D])
            xt = fsb2("xt", [128, D])
            psC = [fb.enter_context(nc.psum_tensor(f"psC{i}", [128, 512],
                                                   F32)) for i in range(2)]
            psT2 = fb.enter_context(nc.psum_tensor("psT2", [128, 512], F32))

            fusesb3 = fusesb[:, :].rearrange("p (it t) -> p it t", it=NIT)
            gbuf3 = gbuf[:, :].rearrange("p (j t) -> p j t", j=NDT)
            ssel4 = [s[:, :].rearrange("p (r q d) -> p r q d", r=4, q=4)
                     for s in ssel]
            sgv = s_g.ap().rearrange("(r q c gg p) d -> p r q c gg d",
                                     r=4, q=4, c=FTC, gg=G, p=128)

            def emit_sload(tc, g):
                par = (tc * G + g) % 2

                @step("sync", r=(), w=(f"ssel{par}",), n_dma=2, cc_wait=1)
                def _(e, tc=tc, g=g, par=par):
                    return [
                        e.dma_start(out=ssel4[par][:, :, q, :],
                                    in_=sgv[:, :, q, tc, g, :])
                        for q in range(2)
                    ]

                @step("scalar", r=(), w=(f"ssel{par}",), n_dma=2, cc_wait=1)
                def _(e, tc=tc, g=g, par=par):
                    return [
                        e.dma_start(out=ssel4[par][:, :, q, :],
                                    in_=sgv[:, :, q, tc, g, :])
                        for q in range(2, 4)
                    ]

            def emit_xload(tc, g):
                @step("sync", r=(), w=("xt",), n_dma=1)
                def _(e, tc=tc, g=g):
                    return e.dma_start(
                        out=xt[:, :],
                        in_=xq[tc * T + g * 128:tc * T + (g + 1) * 128, :])

            for tc in range(FTC):
                if tc == 0:
                    @step("sync", r=("fuse_d",), w=("fusesb",), n_dma=1)
                    def _(e, tc=tc):
                        return e.dma_start(
                            out=fusesb3,
                            in_=fuse_d.ap()[:, tc * T:(tc + 1) * T]
                            .rearrange("(it p) t -> p it t", p=128))
                    emit_sload(0, 0)
                    emit_sload(0, 1)
                    emit_xload(0, 0)

                for dt in range(NDT):
                    pc = dt % 2
                    PC = f"psC{pc}"

                    for hl in range(2):
                        wdb3 = wdb[hl][:, :].rearrange(
                            "p (it d) -> p it d", it=NIT // 2)

                        @step("gpsimd", r=(), w=(f"wdb{hl}",), n_dma=1)
                        def _(e, dt=dt, hl=hl):
                            return e.dma_start(
                                out=wdb[hl][:, :],
                                in_=wdp[dt, :, hl * 4096:(hl + 1) * 4096])

                        @step("tensor", r=(f"wdb{hl}", "fusesb"), w=(PC,))
                        def _(e, dt=dt, pc=pc, hl=hl, wdb3=wdb3):
                            r = None
                            for it in range(NIT // 2):
                                itg = hl * (NIT // 2) + it
                                r = e.matmul(psC[pc][:, :], wdb3[:, it, :],
                                             fusesb3[:, itg, :],
                                             start=(itg == 0),
                                             stop=(itg == NIT - 1))
                            return r

                    @step("scalar", r=(PC, "c_pack"), w=("gbuf",))
                    def _(e, dt=dt, pc=pc):
                        col = SD_ + dt
                        return e.activation(out=gbuf3[:, dt, :],
                                            in_=psC[pc][:, :], func=AF.Silu,
                                            scale=c_pack[:, col:col + 1])

                for g in range(G):
                    par = (tc * G + g) % 2
                    if g == 0 and tc + 1 < FTC:
                        @step("sync", r=("fuse_d",), w=("fusesb",), n_dma=1)
                        def _(e, tc=tc):
                            return e.dma_start(
                                out=fusesb3,
                                in_=fuse_d.ap()[:, (tc + 1) * T:
                                                (tc + 2) * T]
                                .rearrange("(it p) t -> p it t", p=128))

                    for rr in range(4):
                        @step("tensor", r=("gbuf", "c_idn"), w=("psT2",))
                        def _(e, g=g, rr=rr):
                            r = None
                            for u in range(4):
                                dt = rr * 4 + u
                                r = e.transpose(
                                    psT2[:, u * 128:(u + 1) * 128],
                                    gbuf[:, dt * T + g * 128:
                                         dt * T + (g + 1) * 128],
                                    c_idn[:, :])
                            return r

                        @step("vector", r=("psT2",), w=("gTt",))
                        def _(e, rr=rr):
                            return e.tensor_copy(
                                out=gTt[:, rr * 512:(rr + 1) * 512],
                                in_=psT2[:, :])

                    @step("vector", r=(f"ssel{par}", "c_pack"), w=("sacc",))
                    def _(e, par=par):
                        r = None
                        for r4 in range(4):
                            e.tensor_single_scalar(
                                sacc[:, r4 * DS:(r4 + 1) * DS],
                                ssel4[par][:, r4, 0, :],
                                c_pack[:, MASK_:MASK_ + 1], ALU.mult)
                            for Q in range(1, 4):
                                r = e.scalar_tensor_tensor(
                                    out=sacc[:, r4 * DS:(r4 + 1) * DS],
                                    in0=ssel4[par][:, r4, Q, :],
                                    scalar=c_pack[:, MASK_ + Q:MASK_ + Q + 1],
                                    in1=sacc[:, r4 * DS:(r4 + 1) * DS],
                                    op0=ALU.mult, op1=ALU.add)
                        return r

                    @step("vector", r=("gTt", "xt", "sacc"), w=("sacc",))
                    def _(e):
                        e.tensor_tensor(out=sacc[:, :], in0=sacc[:, :],
                                        in1=gTt[:, :], op=ALU.mult)
                        return e.tensor_tensor(out=sacc[:, :], in0=sacc[:, :],
                                               in1=xt[:, :], op=ALU.add)

                    @step("gpsimd", r=("sacc",), w=(), n_dma=1)
                    def _(e, tc=tc, g=g):
                        return e.dma_start(
                            out=outq[tc * T + g * 128:tc * T + (g + 1) * 128,
                                     :],
                            in_=sacc[:, :])

                    if g + 2 < G:
                        emit_sload(tc, g + 2)
                    elif tc + 1 < FTC:
                        emit_sload(tc + 1, g + 2 - G)
                    if g + 1 < G:
                        emit_xload(tc, g + 1)
                    elif tc + 1 < FTC:
                        emit_xload(tc + 1, 0)

            barrier()

        RELAXED = set() if False else {
            "aq0", "aq1", "hq0", "hq1", "hq2", "uq", "yb", "hps0", "hps1",
            "hst", "dt4", "dtx4", "xnfT", "xnsT", "xnbf", "xnTbf", "xnT",
            "gbx", "fusebf", "fusesb", "gbuf", "gTt", "sacc",
            "ssel0", "ssel1", "xt",
            "brep0", "brep1", "crep0", "crep1",
            "wgb0", "wgb1", "wub0", "wub1", "wdb0", "wdb1",
        }
        engcnt = {e: 0 for e in ENGS}
        last_w = {}
        readers = {}
        stepidx = {e: 0 for e in ENGS}
        plans = []

        for s in steps:
            waits = {}

            def add(eng2, c2):
                if waits.get(eng2, 0) < c2:
                    waits[eng2] = c2

            if s.barrier:
                for e2 in ENGS:
                    if e2 != s.eng and engcnt[e2] > 0:
                        add(e2, engcnt[e2])
            else:
                for t in s.r:
                    if t in last_w:
                        e2, c2, si2, wdma = last_w[t]
                        if e2 != s.eng:
                            add(e2, c2)
                        elif wdma or (stepidx[s.eng] - si2 <= 2
                                      and t not in RELAXED):
                            add(e2, c2)
                    if t.startswith("ps"):
                        for e2, c2 in readers.get(t, {}).items():
                            if e2 != s.eng:
                                add(e2, c2)
                for t in s.w:
                    if t in last_w:
                        e2, c2, si2, wdma = last_w[t]
                        if e2 != s.eng:
                            add(e2, c2)
                        elif wdma:
                            add(e2, c2)
                    for e2, c2 in readers.get(t, {}).items():
                        if e2 != s.eng:
                            add(e2, c2)
                        elif s.dma:
                            add(e2, c2)

            plans.append(waits)
            engcnt[s.eng] += s.inc
            si = stepidx[s.eng]
            stepidx[s.eng] += 1
            if not s.barrier:
                for t in s.w:
                    last_w[t] = (s.eng, engcnt[s.eng], si, s.dma)
                    readers[t] = {}
                for t in s.r:
                    readers.setdefault(t, {})[s.eng] = engcnt[s.eng]
                    if t.startswith("ps") and t not in s.w:
                        pass

        totals = dict(engcnt)

        with nc.Block() as block:
            def make_body(engname):
                def body(eng):
                    watermark = {e2: 0 for e2 in ENGS}
                    for s, waits in zip(steps, plans):
                        if s.eng != engname:
                            continue
                        for e2 in ENGS:
                            c2 = waits.get(e2, 0)
                            if c2 > watermark[e2]:
                                eng.wait_ge(semE[e2], c2)
                                watermark[e2] = c2
                        if s.cc_wait:
                            eng.wait_ge(semC, s.cc_wait)
                        if s.barrier:
                            eng.nop().then_inc(semE[engname], 1)
                            continue
                        ret = s.fn(eng)
                        if s.cc_inc:
                            ret.then_inc(semC, 1)
                        elif s.dma:
                            instrs = ret if isinstance(ret, list) else [ret]
                            assert 16 * len(instrs) == s.inc
                            for ins_ in instrs:
                                ins_.then_inc(semE[engname], 16)
                        else:
                            ret.then_inc(semE[engname], 1)
                    for e2 in ENGS:
                        if totals[e2] > watermark[e2]:
                            eng.wait_ge(semE[e2], totals[e2])
                return body

            block.sync(make_body("sync"))
            block.gpsimd(make_body("gpsimd"))
            block.scalar(make_body("scalar"))
            block.vector(make_body("vector"))
            block.tensor(make_body("tensor"))

    return nc


def kernel(L0_attn_x, L0_attn_W_xproj, L0_attn_W_dt, L0_attn_b_dt,
           L0_attn_A_log, L0_attn_D_param, L0_ffn_w_gate, L0_ffn_w_up,
           L0_ffn_w_down, L0_ffn_s_gate, L0_ffn_s_up, L0_ffn_s_down):
    x = np.asarray(L0_attn_x, dtype=np.float32)
    W_xproj = np.asarray(L0_attn_W_xproj, dtype=np.float32)
    W_dt = np.asarray(L0_attn_W_dt, dtype=np.float32)
    b_dt = np.asarray(L0_attn_b_dt, dtype=np.float32)
    A_log = np.asarray(L0_attn_A_log, dtype=np.float32)
    D_param = np.asarray(L0_attn_D_param, dtype=np.float32)
    w_gate = np.asarray(L0_ffn_w_gate)
    w_up = np.asarray(L0_ffn_w_up)
    w_down = np.asarray(L0_ffn_w_down)
    s_gate = np.asarray(L0_ffn_s_gate, dtype=np.float32)
    s_up = np.asarray(L0_ffn_s_up, dtype=np.float32)
    s_down = np.asarray(L0_ffn_s_down, dtype=np.float32)

    nc = build_program()

    wxpT = np.ascontiguousarray(W_xproj.T)
    wgp = np.ascontiguousarray(
        w_gate.reshape(16, 128, 32, 256).transpose(2, 1, 0, 3)
    ).reshape(NSP, 128, NDT * 256)
    wup = np.ascontiguousarray(
        w_up.reshape(16, 128, 32, 256).transpose(2, 1, 0, 3)
    ).reshape(NSP, 128, NDT * 256)
    wdp = np.ascontiguousarray(
        w_down.reshape(64, 128, 16, 128).transpose(2, 1, 0, 3)
    ).reshape(16, 128, INTER)
    idn = np.eye(128, dtype=np.float32)
    sg_c = np.ascontiguousarray(s_gate.reshape(64, 128).T)
    su_c = np.ascontiguousarray(s_up.reshape(64, 128).T)
    sd_c = np.ascontiguousarray(s_down.reshape(16, 128).T)

    in_maps = []
    for k in range(8):
        b, r = k // 4, k % 4
        dlo = r * DS
        cp = np.zeros((128, CPW), dtype=np.float32)
        cp[:, EPS_] = EPS
        cp[:, ONE_] = 1.0
        cp[:, NEG1_] = -1.0
        cp[:, BDT_:BDT_ + 4] = b_dt[dlo:dlo + DS].reshape(4, 128).T
        cp[:, ALOG_:ALOG_ + 64] = (
            A_log[dlo:dlo + DS].reshape(4, 128, 16).transpose(1, 0, 2)
            .reshape(128, 64))
        cp[:, DPAR_:DPAR_ + 4] = D_param[dlo:dlo + DS].reshape(4, 128).T
        cp[:, SG_:SG_ + 64] = sg_c
        cp[:, SU_:SU_ + 64] = su_c
        cp[:, SD_:SD_ + 16] = sd_c
        cp[:, MASK_ + r] = 1.0
        in_maps.append(dict(
            xb=np.ascontiguousarray(x[b]),
            xbs=np.ascontiguousarray(x[b][:, dlo:dlo + DS]),
            xq=np.ascontiguousarray(x[b, r * TQ:(r + 1) * TQ]),
            wxpT=wxpT,
            wdtT=np.ascontiguousarray(W_dt[dlo:dlo + DS].T),
            cpack=cp,
            idn=idn,
            wgp=wgp,
            wup=wup,
            wdp=wdp,
        ))

    res = run_bass_kernel_spmd(nc, in_maps, core_ids=list(range(8)))

    out = np.empty((2, L, D), dtype=np.float32)
    for k in range(8):
        b, r = k // 4, k % 4
        out[b, r * TQ:(r + 1) * TQ] = np.asarray(res.results[k]["outq"])
    return out
